# version 04
# baseline (speedup 1.0000x reference)
import numpy as np
import ml_dtypes
import concourse.bass as bass
import concourse.mybir as mybir
from concourse.bass_utils import run_bass_kernel_spmd
from concourse.expressions_rust import RuntimeValue

F32 = mybir.dt.float32
BF16 = mybir.dt.bfloat16
I32 = mybir.dt.int32
AF = mybir.ActivationFunctionType
ALU = mybir.AluOpType
ET = mybir.EngineType
AXX = mybir.AxisListType.X

D = 2048
H = 16
S = 512
KPART = 16 * 128 * 512
SLABE = 2 * 16 * 128 * 512
T = 1024
II = 8192
DK = 16
LN2 = float(np.log(2.0))
MAGIC = float(1.5 * 2.0 ** 23)
LN_EPS = 1e-5
ATT_EPS = 1e-6
NU_A, NU_B = 8, 16
BF = ml_dtypes.bfloat16
RG = [list(range(8))]


class Sem:
    def __init__(self, nc, name):
        self.h = nc.alloc_semaphore(name)
        self.n = 0

    def inc(self, k=1):
        self.n += k
        return self.n


class Sched:
    def __init__(self):
        self.q = {e: [] for e in ("sync", "gpsimd", "tensor", "vector", "scalar")}

    def em(self, eng, fn):
        self.q[eng].append(fn)

    def wait(self, eng, sem, val):
        self.q[eng].append(lambda e, s=sem.h, v=val: e.wait_ge(s, v))

    def run_block(self, nc):
        with nc.Block() as block:
            for name, deco in (("sync", block.sync), ("gpsimd", block.gpsimd),
                               ("tensor", block.tensor), ("vector", block.vector),
                               ("scalar", block.scalar)):
                if self.q[name]:
                    def body(e, _q=tuple(self.q[name])):
                        for f in _q:
                            f(e)
                    deco(body)


class Arena:

    def __init__(self, nc, name, cols, dtype):
        self.t = nc.alloc_sbuf_tensor(name, [128, cols], dtype)
        self.cols = cols
        self.off = 0
        self.marks = []

    def push(self):
        self.marks.append(self.off)

    def pop(self):
        self.off = self.marks.pop()

    def take(self, cols):
        self.off = (self.off + 15) // 16 * 16
        st = self.off
        self.off += cols
        assert self.off <= self.cols, f"{self.t.name} arena overflow: {self.off} > {self.cols}"
        return self.t.ap()[:, st:st + cols]


def build_program(mode="ag", rounding="magic", debug=(), stop_after=None, noag=False):
    nc = bass.Bass("TRN2", target_bir_lowering=False, debug=False, dynamic_dma_scratch_size=20480)
    local = (mode == "local")

    def inp(name, shape, dt=F32):
        return nc.dram_tensor(name, shape, dt, kind="ExternalInput").ap()

    xa = inp("xa", [S, D])
    xb = inp("xb", [S, D])
    wT_in = [inp(n, [D, D]) for n in ("wqT", "wkT", "wvT", "woT")]
    cvT = inp("cvT", [128, DK])
    wmod = inp("wmod", [D, (96 if local else 12) * 128])
    NCH = 96 if local else 12
    WR = 8192 if local else 1024
    WDC = 8192 if local else 1024
    wg_sl = inp("wg_sl", [WR, D])
    wu_sl = inp("wu_sl", [WR, D])
    wd_sl = inp("wd_sl", [D, WDC])
    cosq = inp("cosq", [T, 64])
    sinq = inp("sinq", [T, 64])
    utab = inp("utab", [1, 64], I32)

    out = nc.dram_tensor("out", [T, D], F32, kind="ExternalOutput").ap()
    dbg_handles = {}

    def dbgh(name, shape, dt=F32):
        if name in debug:
            dbg_handles[name] = nc.dram_tensor("dbg_" + name, shape, dt, kind="ExternalOutput").ap()

    for nm, shp in (("mod", [128, 96]), ("q0", [128, 2048]), ("k0", [128, 2048]),
                    ("v0", [128, 2048]), ("ln0", [128, 2048]), ("attn0", [128, 2048]),
                    ("x2", [128, 2048]), ("hq0", [128, 2048]), ("amx", [128, 2048]),
                    ("wq0", [128, 2048]), ("den0", [128, 512]), ("wqa", [128, 2048]), ("lnx", [128, 2048]), ("lnf", [128, 2048]), ("st", [128, 16]),
                    ("qtm0", [128, 512]), ("rope0", [128, 512]), ("cos0", [128, 64]),
                    ("qbb", [128, 2048]), ("psq0", [128, 512]),
                    ("qe_y", [128, 2048]), ("qe_a", [128, 2048]), ("qe_g2", [128, 2048]),
                    ("qe_w1", [128, 2048]), ("qe_w2", [128, 2048]), ("qe_rnd", [128, 2048]),
                    ("qe_qout", [128, 2048]), ("iq0", [128, 512]), ("ffn0", [128, 2048]),
                    ("ffn1", [128, 2048]), ("ffn4", [128, 2048]), ("ffn5", [128, 2048]),
                    ("int4", [128, 8192]), ("iq4", [128, 512]), ("scall", [128, 2048])):
        dbgh(nm, shp)
    for nm, shp in (("wqfull", [2048, 2048]), ("lnxf", [128, 8192]), ("lnxf2", [128, 8192])):
        if nm in debug:
            dbg_handles[nm] = nc.dram_tensor("dbg_" + nm, shp, BF16, kind="ExternalOutput").ap()
    if False:
        dbgh(nm, shp)

    ident_bf = nc.inline_tensor(np.eye(128, dtype=BF), "identbf").ap()
    ident_f = nc.inline_tensor(np.eye(128, dtype=np.float32), "identf").ap()
    ones_bf = nc.inline_tensor(np.ones((128, 1), dtype=BF), "onesbf").ap()
    ones1 = nc.inline_tensor(np.ones((1, 128), dtype=np.float32), "ones1").ap()
    mk = np.zeros((128, 4 * S), dtype=BF)
    for kt in range(4):
        kid = kt * 128 + np.arange(128)[:, None]
        mk[:, kt * S:(kt + 1) * S] = (kid <= np.arange(S)[None, :]).astype(BF)
    maskc = nc.inline_tensor(mk, "maskc").ap()

    def idram(name, shape, dt):
        return nc.dram_tensor(name, shape, dt, kind="Internal").ap()

    kv_in = idram("kv_in", [2, SLABE], BF16)
    kv_all = idram("kv_all", [17 * SLABE], BF16)
    kv_seq = idram("kv_seq", [24 * SLABE], BF16)
    kvi_k = kv_in.rearrange("s (a p f) -> s a p f", p=128, f=S)
    kvi_flat = kv_in.rearrange("s e -> (s e)")
    wq_attn = idram("wq_attn", [4, D, D], BF16)
    wgq_in = idram("wgq_in", [D, WR], BF16)
    wuq_in = idram("wuq_in", [D, WR], BF16)
    wdq_in = idram("wdq_in", [WDC, D], BF16)
    if local:
        wg_all, wu_all, wd_all = wgq_in, wuq_in, wdq_in
        mod_in = idram("mod_in", [96, 128], F32)
        mod_all = mod_in
    else:
        wg_all = idram("wg_all", [8, D, 1024], BF16)
        wu_all = idram("wu_all", [8, D, 1024], BF16)
        wd_all = idram("wd_all", [8, 1024, D], BF16)
        mod_in = idram("mod_in", [12, 128], F32)
        mod_all = idram("mod_all", [8, 12, 128], F32)
    mod_flat = mod_all.rearrange("a b c -> (a b c)") if not local else mod_all.rearrange("b c -> (b c)")
    x2_dram = idram("x2_dram", [T, D], F32)
    rows_dram = idram("rows_dram", [8, D], F32)
    RIDX = {"ga": 0, "gb": 1, "cb1": 2, "qb": 3, "kb": 4, "vb": 5}
    inter_dram = idram("inter_dram", [T, II], F32)

    AF32 = Arena(nc, "af32", 26624, F32)
    ABF = Arena(nc, "abf", 41472, BF16)
    mm_row = nc.alloc_sbuf_tensor("mmrow", [1, D], F32).ap()
    ones1_sb = nc.alloc_sbuf_tensor("ones1s", [1, 128], F32).ap()
    utab_sb = nc.alloc_sbuf_tensor("utabs", [1, 64], I32).ap()

    id_sb = ABF.take(128)
    idf_sb = AF32.take(128)
    onesb_sb = ABF.take(16)[:, 0:1]
    mask_sb = ABF.take(4 * S)
    qT_sb = ABF.take(H * T)
    hqT_sb = qT_sb
    attnT_sb = ABF.take(H * T)
    mod_sb = AF32.take(96)
    ca1_sb = AF32.take(DK)
    cb1_sb = AF32.take(DK)
    cos_sb = AF32.take(8 * 64)
    sin_sb = AF32.take(8 * 64)
    cst66 = AF32.take(16)

    PS = [nc.alloc_psum_tensor(f"ps{i}", [128, 512], F32).ap() for i in range(8)]

    sprobe = Sem(nc, "sprobe") if "semprobe" in debug else None
    sl = Sem(nc, "sl")
    sg = Sem(nc, "sg")
    sm = Sem(nc, "sm")
    sv = Sem(nc, "sv")
    ss = Sem(nc, "ss")
    scc = Sem(nc, "scc")
    sout = Sem(nc, "sout")

    def mm_group(sch, fns):
        def run(e, _fns=tuple(fns)):
            ins = None
            for f in _fns:
                ins = f(e)
            ins.then_inc(sm.h)
        sch.em("tensor", run)
        return sm.inc()

    def v_end(sch):
        raise RuntimeError("unused")

    def emit_scales(sch, amax_ap, scr3, scale_ap, rscale_ap):
        ag, lt, t2 = scr3
        sch.em("vector", lambda e: e.tensor_scalar(ag, amax_ap, float(2.0 ** -60), None, op0=ALU.max).then_inc(sv.h))
        sch.wait("scalar", sv, sv.inc())
        sch.em("scalar", lambda e: e.activation(out=lt, in_=ag, func=AF.Ln).then_inc(ss.h))
        sch.wait("vector", ss, ss.inc())
        sch.em("vector", lambda e: e.tensor_scalar(ag, lt, float(1.0 / LN2), 64.0 + 1e-9, op0=ALU.mult, op1=ALU.add).then_inc(sv.h))
        sch.wait("vector", sv, sv.inc())
        if rounding == "magic":
            sch.em("vector", lambda e: e.tensor_scalar(t2, ag, MAGIC, MAGIC, op0=ALU.add, op1=ALU.subtract).then_inc(sv.h))
            sch.wait("vector", sv, sv.inc())
            sch.em("vector", lambda e: e.tensor_tensor(lt, t2, ag, op=ALU.subtract).then_inc(sv.h))
            sch.wait("vector", sv, sv.inc())
            sch.em("vector", lambda e: e.tensor_scalar(lt, lt, 1e-12, None, op0=ALU.is_ge).then_inc(sv.h))
            sch.wait("vector", sv, sv.inc())
            sch.em("vector", lambda e: e.scalar_tensor_tensor(t2, lt, -1.0, t2, op0=ALU.mult, op1=ALU.add).then_inc(sv.h))
        else:
            r = lt
            sch.em("vector", lambda e: e.tensor_copy(r, ag).then_inc(sv.h))
            sch.wait("vector", sv, sv.inc())
            for stp in (64.0, 32.0, 16.0, 8.0, 4.0, 2.0, 1.0):
                sch.em("vector", lambda e, s=stp: e.tensor_scalar(t2, r, s, None, op0=ALU.is_ge).then_inc(sv.h))
                sch.wait("vector", sv, sv.inc())
                sch.em("vector", lambda e, s=stp: e.scalar_tensor_tensor(r, t2, -s, r, op0=ALU.mult, op1=ALU.add).then_inc(sv.h))
                sch.wait("vector", sv, sv.inc())
            sch.em("vector", lambda e: e.tensor_tensor(t2, ag, r, op=ALU.subtract).then_inc(sv.h))
        sch.wait("scalar", sv, sv.inc())
        sch.em("scalar", lambda e: e.activation(out=scale_ap, in_=t2, func=AF.Exp, scale=LN2, bias=cst66[:, 0:1]))
        sch.em("scalar", lambda e: e.activation(out=rscale_ap, in_=t2, func=AF.Exp, scale=float(-LN2), bias=cst66[:, 1:2]).then_inc(ss.h))
        return ss.inc()

    def emit_round(sch, v_ap, m_ap, b_ap, r_ap):
        if rounding == "magic":
            sch.em("vector", lambda e: e.tensor_scalar(m_ap, v_ap, MAGIC, MAGIC, op0=ALU.add, op1=ALU.subtract))
        else:
            sch.em("vector", lambda e: e.tensor_scalar(r_ap, v_ap, 16.0, None, op0=ALU.add))
            for stp in (16.0, 8.0, 4.0, 2.0, 1.0):
                sch.em("vector", lambda e, s=stp: e.tensor_scalar(b_ap, r_ap, s, None, op0=ALU.is_ge))
                sch.em("vector", lambda e, s=stp: e.scalar_tensor_tensor(r_ap, b_ap, -s, r_ap, op0=ALU.mult, op1=ALU.add))
            sch.em("vector", lambda e: e.tensor_scalar(b_ap, r_ap, 0.5, None, op0=ALU.is_ge))
            sch.em("vector", lambda e: e.tensor_tensor(m_ap, v_ap, r_ap, op=ALU.subtract))
            sch.em("vector", lambda e: e.tensor_tensor(m_ap, m_ap, b_ap, op=ALU.add))

    def emit_qelem(sch, x_ap, out_ap, rscale_ap, scale_ap, FD, scr7, mfrac, mxn, dbg=False):
        y, a, g2, g4, t1, w1, w2 = scr7
        nb = FD // 32
        x3 = x_ap.rearrange("p (a b) -> p a b", b=32)
        y3 = y.rearrange("p (a b) -> p a b", b=32)
        rs3 = rscale_ap.rearrange("p (a o) -> p a o", o=1).broadcast_to([128, nb, 32])
        sc3 = scale_ap.rearrange("p (a o) -> p a o", o=1).broadcast_to([128, nb, 32])
        sch.em("vector", lambda e: e.tensor_tensor(y3, x3, rs3, op=ALU.mult).then_inc(sv.h))
        sch.wait("scalar", sv, sv.inc())
        sch.em("scalar", lambda e: e.activation(out=a, in_=y, func=AF.Abs).then_inc(ss.h))
        sch.wait("vector", ss, ss.inc())
        sch.em("vector", lambda e: e.tensor_scalar(g2, a, 2.0, None, op0=ALU.is_ge))
        sch.em("vector", lambda e: e.tensor_scalar(g4, a, 4.0, None, op0=ALU.is_ge))
        if dbg:
            dump(sch, "qe_y", y)
            dump(sch, "qe_a", a)
            dump(sch, "qe_g2", g2)
            sch.wait("vector", sg, sg.n)
        sch.em("vector", lambda e: e.scalar_tensor_tensor(t1, g2, 2.0, g4, op0=ALU.mult, op1=ALU.add))
        k1, base = (2.0, 8.0) if mfrac == 3 else (0.5, 2.0)
        sch.em("vector", lambda e: e.tensor_scalar(w1, t1, -k1, base, op0=ALU.mult, op1=ALU.add))
        sch.em("vector", lambda e: e.tensor_tensor(w2, y, w1, op=ALU.mult))
        if dbg:
            dump(sch, "qe_w1", w1)
            dump(sch, "qe_w2", w2)
            sch.wait("vector", sg, sg.n)
        emit_round(sch, w2, a, t1, w1)
        if dbg:
            dump(sch, "qe_rnd", a)
            sch.wait("vector", sg, sg.n)
        sch.em("vector", lambda e: e.scalar_tensor_tensor(t1, g4, 2.0, g2, op0=ALU.mult, op1=ALU.add))
        s1 = 0.125 if mfrac == 3 else 0.5
        sch.em("vector", lambda e: e.tensor_scalar(w1, t1, s1, s1, op0=ALU.mult, op1=ALU.add))
        sch.em("vector", lambda e: e.tensor_tensor(w2, a, w1, op=ALU.mult))
        sch.em("vector", lambda e: e.tensor_scalar(w2, w2, float(mxn), float(-mxn), op0=ALU.min, op1=ALU.max))
        o3 = out_ap.rearrange("p (a b) -> p a b", b=32)
        w23 = w2.rearrange("p (a b) -> p a b", b=32)
        sch.em("vector", lambda e: e.tensor_tensor(o3, w23, sc3, op=ALU.mult).then_inc(sv.h))
        return sv.inc()

    def emit_bcast_row(sch, row_dram_ap, dst_ap):
        sch.em("gpsimd", lambda e: e.dma_start(out=mm_row, in_=row_dram_ap).then_inc(sg.h, 16))
        sch.wait("tensor", sg, sg.inc(16))
        for n4 in range(4):
            def mmf(e, n=n4):
                return e.matmul(PS[7][:, 0:512], ones1_sb, mm_row[0:1, n * 512:(n + 1) * 512], start=True, stop=True)
            w = mm_group(sch, [mmf])
            sch.wait("vector", sm, w)
            sch.em("vector", lambda e, n=n4: e.tensor_copy(dst_ap[:, n * 512:(n + 1) * 512], PS[7][:, 0:512]).then_inc(sv.h))
            sch.wait("tensor", sv, sv.inc())
        sch.wait("gpsimd", sm, sm.n)

    def emit_ln(sch, x_ap, xsq_ap, st_ap, out_fn):
        mr, m, s2r, s2m, mm2, var, veps, sq_, rstd = (st_ap[:, i:i + 1] for i in range(9))
        sch.em("vector", lambda e: e.reduce_sum(mr, x_ap, axis=AXX).then_inc(sv.h))
        sch.wait("vector", sv, sv.inc())
        sch.em("vector", lambda e: e.tensor_scalar(m, mr, float(1.0 / D), None, op0=ALU.mult).then_inc(sv.h))
        sch.wait("scalar", sv, sv.inc())
        sch.em("scalar", lambda e: e.activation(out=xsq_ap, in_=x_ap, func=AF.Square).then_inc(ss.h))
        sch.wait("vector", ss, ss.inc())
        sch.em("vector", lambda e: e.reduce_sum(s2r, xsq_ap, axis=AXX).then_inc(sv.h))
        sch.wait("vector", sv, sv.inc())
        sch.em("vector", lambda e: e.tensor_scalar(s2m, s2r, float(1.0 / D), None, op0=ALU.mult))
        sch.em("vector", lambda e: e.tensor_tensor(mm2, m, m, op=ALU.mult).then_inc(sv.h))
        sch.wait("vector", sv, sv.inc())
        sch.em("vector", lambda e: e.tensor_tensor(var, s2m, mm2, op=ALU.subtract).then_inc(sv.h))
        sch.wait("vector", sv, sv.inc())
        sch.em("vector", lambda e: e.tensor_scalar(veps, var, float(LN_EPS), None, op0=ALU.add).then_inc(sv.h))
        sch.wait("scalar", sv, sv.inc())
        sch.em("scalar", lambda e: e.activation(out=sq_, in_=veps, func=AF.Sqrt).then_inc(ss.h))
        sch.wait("vector", ss, ss.inc())
        sch.em("vector", lambda e: e.reciprocal(rstd, sq_).then_inc(sv.h))
        sch.wait("vector", sv, sv.inc())
        out_fn(m, rstd)

    def dump(sch, name, src_ap):
        h = dbg_handles.get(name)
        if h is None:
            return
        sch.wait("gpsimd", sv, sv.n)
        sch.wait("gpsimd", ss, ss.n)
        sch.wait("gpsimd", sm, sm.n)
        sch.em("gpsimd", lambda e: e.dma_start(out=h, in_=src_ap).then_inc(sg.h, 16))
        sch.wait("gpsimd", sg, sg.inc(16))

    sch = Sched()
    AF32.push(); ABF.push()
    cs_sb = AF32.take(DK)
    scT_sb = AF32.take(DK)
    gate_t = AF32.take(D)[0:1, :]
    modv = AF32.take(96)
    modt = AF32.take(128)
    wmt = [AF32.take(12 * 128) for _ in range(2)]
    zt = ABF.take(2048)
    for dst, src in ((id_sb, ident_bf), (idf_sb, ident_f), (onesb_sb, ones_bf),
                     (mask_sb, maskc), (ones1_sb, ones1), (utab_sb, utab), (cs_sb, cvT)):
        sch.em("sync", lambda e, d=dst, s=src: e.dma_start(out=d, in_=s).then_inc(sl.h, 16))
        sl.inc(16)
    for tc in range(8):
        sch.em("sync", lambda e, t=tc: e.dma_start(out=cos_sb[:, t * 64:(t + 1) * 64], in_=cosq[t * 128:(t + 1) * 128, :]).then_inc(sl.h, 16)); sl.inc(16)
        sch.em("sync", lambda e, t=tc: e.dma_start(out=sin_sb[:, t * 64:(t + 1) * 64], in_=sinq[t * 128:(t + 1) * 128, :]).then_inc(sl.h, 16)); sl.inc(16)
    all_loaded = sl.n
    sch.em("vector", lambda e: e.memset(cst66[:, 0:1], float(-66.0 * LN2)))
    sch.em("vector", lambda e: e.memset(cst66[:, 1:2], float(66.0 * LN2)))
    sch.em("vector", lambda e: e.memset(zt, 0.0).then_inc(sv.h))
    sch.wait("gpsimd", sv, sv.inc())
    for zchunk in range(SLABE // (128 * 2048)):
        sch.em("gpsimd", lambda e, z=zchunk: e.dma_start(
            out=kv_all[16 * SLABE + z * 128 * 2048: 16 * SLABE + (z + 1) * 128 * 2048].rearrange("(p f) -> p f", p=128),
            in_=zt).then_inc(sg.h, 16)); sg.inc(16)
    sch.wait("scalar", sl, all_loaded)
    sch.em("scalar", lambda e: e.activation(out=scT_sb, in_=cs_sb, func=AF.Silu).then_inc(ss.h))
    sch.wait("tensor", ss, ss.inc())
    nchunk = NCH // 12
    for ck in range(nchunk):
        for dk in range(DK):
            buf = wmt[dk % 2]
            if ck > 0 or dk >= 2:
                sch.wait("sync", sm, sm.n)
            sch.em("sync", lambda e, b=buf, d=dk, c=ck: e.dma_start(
                out=b, in_=wmod[d * 128:(d + 1) * 128, c * 1536:(c + 1) * 1536]).then_inc(sl.h, 16))
            sch.wait("tensor", sl, sl.inc(16))
            fns = []
            for ch in range(12):
                def mmf(e, b=buf, c=ch, d=dk, k=ck):
                    return e.matmul(PS[5][:, k * 12 + c:k * 12 + c + 1] if local else PS[5][:, c:c + 1],
                                    b[:, c * 128:(c + 1) * 128], scT_sb[:, d:d + 1],
                                    start=(k == 0 and d == 0 and c == 0),
                                    stop=(k == nchunk - 1 and d == DK - 1 and c == 11))
                fns.append(mmf)
            mm_group(sch, fns)
    sch.wait("vector", sm, sm.n)
    sch.em("vector", lambda e: e.tensor_copy(modv[:, 0:NCH], PS[5][:, 0:NCH]).then_inc(sv.h))
    sch.wait("tensor", sv, sv.inc())
    wtr = mm_group(sch, [lambda e: e.matmul(PS[6][0:NCH, 0:128], modv[:, 0:NCH], idf_sb, start=True, stop=True, is_transpose=True)])
    sch.wait("vector", sm, wtr)
    sch.em("vector", lambda e: e.tensor_copy(modt[0:NCH, 0:128], PS[6][0:NCH, 0:128]).then_inc(sv.h))
    sch.wait("gpsimd", sv, sv.inc())
    sch.em("gpsimd", lambda e: e.dma_start(out=mod_in[0:NCH, :], in_=modt[0:NCH, 0:128]).then_inc(sg.h, 16))
    sg.inc(16)
    if not local:
        sch.wait("gpsimd", sg, sg.n)
        sch.em("gpsimd", lambda e: e.collective_compute(
            "AllGather", ALU.bypass, replica_groups=RG,
            ins=[mod_in.opt()], outs=[mod_all.opt()]).then_inc(scc.h))
        scc.inc()
        sch.wait("gpsimd", scc, scc.n)
    sch.em("gpsimd", lambda e: e.dma_start(out=modt[0:96, 0:128], in_=mod_all if local else mod_all.rearrange("a b c -> (a b) c")).then_inc(sg.h, 16))
    sch.wait("tensor", sg, sg.inc(16))
    wtr = mm_group(sch, [lambda e: e.matmul(PS[6][:, 0:96], modt[0:96, 0:128], idf_sb[0:96, 0:96], start=True, stop=True, is_transpose=True)])
    sch.wait("vector", sm, wtr)
    sch.em("vector", lambda e: e.tensor_copy(mod_sb, PS[6][:, 0:96]).then_inc(sv.h))
    sv.inc()
    sch.wait("vector", sg, sg.n)
    sch.em("vector", lambda e: e.tensor_scalar(ca1_sb, mod_sb[:, 16:32], 1.0, None, op0=ALU.add))
    sch.em("vector", lambda e: e.tensor_scalar(cb1_sb, mod_sb[:, 64:80], 1.0, None, op0=ALU.add).then_inc(sv.h))
    sv.inc()
    scr_row = mm_row[0:1, :]
    for gn, lo in (("ga", 4096), ("gb", 10240)):
        sch.em("gpsimd", lambda e, q=lo: e.dma_start(out=scr_row, in_=mod_flat[q:q + D].rearrange("(o d) -> o d", o=1)).then_inc(sg.h, 16))
        sch.wait("vector", sg, sg.inc(16))
        sch.em("vector", lambda e: e.tensor_scalar(gate_t, scr_row, 0.0, None, op0=ALU.is_ge))
        sch.em("vector", lambda e: e.tensor_scalar(gate_t, gate_t, 2.0, -1.0, op0=ALU.mult, op1=ALU.add).then_inc(sv.h))
        sch.wait("scalar", sv, sv.inc())
        sch.em("scalar", lambda e: e.activation(out=scr_row, in_=scr_row, func=AF.Abs).then_inc(ss.h))
        sch.wait("vector", ss, ss.inc())
        sch.em("vector", lambda e: e.tensor_scalar(scr_row, scr_row, 0.25, None, op0=ALU.max))
        sch.em("vector", lambda e: e.tensor_tensor(scr_row, scr_row, gate_t, op=ALU.mult).then_inc(sv.h))
        sch.wait("gpsimd", sv, sv.inc())
        sch.em("gpsimd", lambda e, n=gn: e.dma_start(out=rows_dram[RIDX[n], :].rearrange("(o d) -> o d", o=1), in_=scr_row).then_inc(sg.h, 16))
        sch.wait("vector", sg, sg.inc(16))
    sch.em("gpsimd", lambda e: e.dma_start(out=scr_row, in_=mod_flat[8192:8192 + D].rearrange("(o d) -> o d", o=1)).then_inc(sg.h, 16))
    sch.wait("vector", sg, sg.inc(16))
    sch.em("vector", lambda e: e.tensor_scalar(scr_row, scr_row, 1.0, None, op0=ALU.add).then_inc(sv.h))
    sch.wait("gpsimd", sv, sv.inc())
    sch.em("gpsimd", lambda e: e.dma_start(out=rows_dram[RIDX["cb1"], :].rearrange("(o d) -> o d", o=1), in_=scr_row).then_inc(sg.h, 16))
    sch.wait("vector", sg, sg.inc(16))
    dump(sch, "mod", mod_sb)
    AF32.pop(); ABF.pop()
    sch.run_block(nc)
    nc.all_engine_barrier()
    if stop_after == "B0":
        return nc, dbg_handles

    sch = Sched()
    AF32.push(); ABF.push()
    wt_f = [AF32.take(D) for _ in range(2)]
    wt_b = ABF.take(D)
    wtf_reader = [[0, 0], [0, 0]]
    for wi in range(4):
        brow = ("qb", "kb", "vb", None)[wi]
        for dk in range(DK):
            buf = wt_f[dk % 2]
            if wtf_reader[dk % 2][0] or wtf_reader[dk % 2][1]:
                sch.wait("sync", sv, wtf_reader[dk % 2][0])
                sch.wait("sync", sm, wtf_reader[dk % 2][1])
            sch.em("sync", lambda e, b=buf, d=dk, w=wi: e.dma_start(out=b, in_=wT_in[w][d * 128:(d + 1) * 128, :]).then_inc(sl.h, 16))
            lw = sl.inc(16)
            if brow is not None:
                sch.wait("tensor", sl, lw)
                if dk == 0:
                    sch.wait("tensor", sv, sv.n)
                fns = []
                for n4 in range(4):
                    def mmf(e, b=buf, n=n4, d=dk):
                        return e.matmul(PS[4 + n][0:1, 0:512], mod_sb[:, d:d + 1],
                                        b[:, n * 512:(n + 1) * 512], start=(d == 0), stop=(d == DK - 1))
                    fns.append(mmf)
                _bias_t = mm_group(sch, fns)
            else:
                _bias_t = 0
            sch.wait("vector", sl, lw)
            if wi < 3:
                sch.em("vector", lambda e, b=buf, d=dk: e.tensor_scalar(wt_b, b, ca1_sb[:, d:d + 1], None, op0=ALU.mult).then_inc(sv.h))
            else:
                sch.em("vector", lambda e, b=buf: e.tensor_copy(wt_b, b).then_inc(sv.h))
            _sc_t = sv.inc()
            wtf_reader[dk % 2] = [_sc_t, _bias_t]
            sch.wait("gpsimd", sv, _sc_t)
            sch.em("gpsimd", lambda e, w=wi, d=dk: e.dma_start(out=wq_attn[w, d * 128:(d + 1) * 128, :], in_=wt_b).then_inc(sg.h, 16))
            sg.inc(16)
            sch.wait("vector", sg, sg.n)
        if brow is not None:
            sch.wait("vector", sm, sm.n)
            for n4 in range(4):
                sch.em("vector", lambda e, n=n4: e.tensor_copy(mm_row[0:1, n * 512:(n + 1) * 512], PS[4 + n][0:1, 0:512]))
            sch.em("vector", lambda e: e.tensor_scalar(mm_row[0:1, 0:1], mm_row[0:1, 0:1], 1.0, None, op0=ALU.mult).then_inc(sv.h))
            sch.wait("gpsimd", sv, sv.inc())
            sch.em("gpsimd", lambda e, bn=brow: e.dma_start(out=rows_dram[RIDX[bn], :].rearrange("(o d) -> o d", o=1), in_=mm_row).then_inc(sg.h, 16))
            sch.wait("tensor", sg, sg.inc(16))
            sch.wait("vector", sg, sg.n)
    if "wqa" in dbg_handles:
        sch.wait("gpsimd", sg, sg.n)
        sch.em("gpsimd", lambda e: e.dma_start(out=wt_b, in_=wq_attn[0, 0:128, :]).then_inc(sg.h, 16))
        sch.wait("gpsimd", sg, sg.inc(16))
        sch.em("gpsimd", lambda e: e.dma_start(out=dbg_handles["wqa"], in_=wt_b).then_inc(sg.h, 16))
        sch.wait("gpsimd", sg, sg.inc(16))
    if "wqfull" in dbg_handles:
        sch.wait("gpsimd", sg, sg.n)
        sch.em("gpsimd", lambda e: e.dma_start(out=dbg_handles["wqfull"], in_=wq_attn[0]).then_inc(sg.h, 16))
        sch.wait("gpsimd", sg, sg.inc(16))
    AF32.pop(); ABF.pop()
    sch.run_block(nc)
    nc.all_engine_barrier()
    if stop_after == "B1":
        return nc, dbg_handles

    sch = Sched()
    AF32.push(); ABF.push()
    x_tc = [AF32.take(D) for _ in range(2)]
    xsq = AF32.take(D)
    st_t = AF32.take(16)
    qtm = AF32.take(D)
    ropetmp = AF32.take(512)
    lnf = AF32.take(D)
    ropef = AF32.take(512)
    qb_b = AF32.take(D)
    kb_b = AF32.take(D)
    vb_b = AF32.take(D)
    snap_lnf = AF32.take(D) if "lnf" in dbg_handles else None
    snap_st = AF32.take(16) if "lnf" in dbg_handles else None
    lnxT = attnT_sb[:, 0:DK * S]
    vbf = attnT_sb[:, DK * S:DK * S + 2048]
    _o = DK * S + 2048
    wstr = [attnT_sb[:, _o:_o + 512], attnT_sb[:, _o + 512:_o + 1024]]
    kstage = attnT_sb[:, _o + 1024:_o + 1536]
    emit_bcast_row(sch, rows_dram[RIDX["qb"], :].rearrange("(o d) -> o d", o=1), qb_b)
    emit_bcast_row(sch, rows_dram[RIDX["kb"], :].rearrange("(o d) -> o d", o=1), kb_b)
    emit_bcast_row(sch, rows_dram[RIDX["vb"], :].rearrange("(o d) -> o d", o=1), vb_b)

    _b2slots = 1 if stop_after in ("B2ln", "B2q", "B2s0") else 2
    wstr_reader = [0, 0]
    for slot in range(_b2slots):
        xin = xa if slot == 0 else xb
        for tc in range(4):
            buf = x_tc[tc % 2]
            if tc >= 2 or slot > 0:
                sch.wait("sync", ss, ss.n)
                sch.wait("sync", sv, sv.n)
            sch.em("sync", lambda e, b=buf, t=tc, xi=xin: e.dma_start(out=b, in_=xi[t * 128:(t + 1) * 128, :]).then_inc(sl.h, 16))
            sch.wait("vector", sl, sl.inc(16))
            lnb = lnf

            def norm_out(m, rstd, b=buf, o=lnb):
                sch.em("vector", lambda e: e.tensor_scalar(o, b, m, rstd, op0=ALU.subtract, op1=ALU.mult).then_inc(sv.h))
                sv.inc()
            emit_ln(sch, buf, xsq, st_t, norm_out)
            if slot == 0 and tc == 0 and "lnf" in dbg_handles:
                sch.em("vector", lambda e: e.tensor_copy(snap_lnf, lnf))
                sch.em("vector", lambda e: e.tensor_copy(snap_st, st_t).then_inc(sv.h))
                sv.inc()
            if slot == 0 and tc == 0:
                dump(sch, "ln0", xsq)
            wvn = sv.n
            sch.wait("tensor", sv, wvn)
            for dc in range(DK):
                pb = PS[dc % 2]
                if dc >= 2:
                    sch.wait("tensor", sv, sv.n)
                w = mm_group(sch, [lambda e, p=pb, d=dc, o=lnb: e.matmul(
                    p[:, 0:128], o[:, d * 128:(d + 1) * 128], idf_sb, start=True, stop=True, is_transpose=True)])
                sch.wait("vector", sm, w)
                sch.em("vector", lambda e, p=pb, d=dc, t=tc: e.tensor_copy(
                    lnxT[:, d * S + t * 128:d * S + (t + 1) * 128], p[:, 0:128].bitcast(BF16) if False else p[:, 0:128]).then_inc(sv.h))
                sv.inc()
        sch.wait("tensor", sv, sv.n)
        if slot == 0:
            dump(sch, "lnx", lnxT[:, 0:2048])
            dump(sch, "lnxf", lnxT[:, 0:8192])
            if "lnf" in dbg_handles:
                dump(sch, "lnf", snap_lnf)
                dump(sch, "st", snap_st[:, 0:16])
        if stop_after == "B2ln":
            break
        _projs = ((0, False),) if stop_after == "B2q" else ((0, False), (1, False), (2, True))
        ps_guard = sv.n
        for pi, (wix, is_v) in enumerate(_projs):
            bb = (qb_b, kb_b, vb_b)[pi]
            for n4 in range(4):
                for dk in range(DK):
                    buf = wstr[dk % 2]
                    if wstr_reader[dk % 2]:
                        sch.wait("sync", sm, wstr_reader[dk % 2])
                    sch.em("sync", lambda e, b=buf, w=wix, d=dk, n=n4: e.dma_start(
                        out=b, in_=wq_attn[w, d * 128:(d + 1) * 128, n * 512:(n + 1) * 512]).then_inc(sl.h, 16))
                    sch.wait("tensor", sl, sl.inc(16))
                    if dk == 0:
                        sch.wait("tensor", sv, ps_guard)
                    fns = []
                    for tc in range(4):
                        def mmf(e, b=buf, d=dk, t=tc):
                            return e.matmul(PS[t][:, 0:512], lnxT[:, d * S + t * 128:d * S + (t + 1) * 128], b,
                                            start=(d == 0), stop=(d == DK - 1))
                        fns.append(mmf)
                    wstr_reader[dk % 2] = mm_group(sch, fns)
                mw = sm.n
                sch.wait("vector", sm, mw)
                for tc in range(4):
                    if is_v:
                        sch.em("vector", lambda e, t=tc, n=n4, b=bb: e.tensor_tensor(
                            vbf[:, t * 512:(t + 1) * 512], PS[t][:, 0:512], b[:, n * 512:(n + 1) * 512], op=ALU.add).then_inc(sv.h))
                        sch.wait("gpsimd", sv, sv.inc())
                        for j in range(4):
                            _vt = ((n4 * 4 + j) * 4 + tc)
                            sch.em("gpsimd", lambda e, sl_=slot, a=_vt, t=tc, n=n4, jj=j: e.dma_start(
                                out=kv_in[sl_, KPART + a * 16384:KPART + (a + 1) * 16384].rearrange("(p f) -> p f", p=128),
                                in_=vbf[:, t * 512 + jj * 128:t * 512 + (jj + 1) * 128]).then_inc(sg.h, 16))
                            sg.inc(16)
                        sch.wait("vector", sg, sg.n)
                    else:
                        if pi == 0 and n4 == 0 and tc == 0 and slot == 0 and "psq0" in dbg_handles:
                            sch.em("vector", lambda e: e.tensor_copy(ropetmp[:, 0:512], PS[0][:, 0:512]).then_inc(sv.h))
                            sv.inc()
                            dump(sch, "psq0", ropetmp[:, 0:512])
                            dump(sch, "qbb", bb)
                            sch.wait("vector", sg, sg.n)
                        sch.em("vector", lambda e, t=tc, n=n4, b=bb: e.tensor_tensor(
                            qtm[:, 0:512], PS[t][:, 0:512], b[:, n * 512:(n + 1) * 512], op=ALU.add).then_inc(sv.h))
                        sv.inc()
                        if pi == 0 and n4 == 0 and tc == 0 and slot == 0 and "qtm0" in dbg_handles:
                            dump(sch, "qtm0", qtm[:, 0:512])
                            dump(sch, "cos0", cos_sb[:, 0:64])
                            sch.wait("vector", sg, sg.n)
                        co = cos_sb[:, (slot * 4 + tc) * 64:(slot * 4 + tc + 1) * 64]
                        si = sin_sb[:, (slot * 4 + tc) * 64:(slot * 4 + tc + 1) * 64]
                        co4 = co.rearrange("p (o j) -> p o j", o=1).broadcast_to([128, 4, 64])
                        si4 = si.rearrange("p (o j) -> p o j", o=1).broadcast_to([128, 4, 64])
                        q4 = qtm[:, 0:512].rearrange("p (h j) -> p h j", j=128)
                        qlo, qhi = q4[:, :, 0:64], q4[:, :, 64:128]
                        tlo = ropetmp[:, 0:256].rearrange("p (h j) -> p h j", j=64)
                        thi = ropetmp[:, 256:512].rearrange("p (h j) -> p h j", j=64)
                        o4 = ropef[:, 0:512].rearrange("p (h j) -> p h j", j=128)
                        olo, ohi = o4[:, :, 0:64], o4[:, :, 64:128]
                        sch.em("vector", lambda e, a=tlo, x1=qlo, c=co4: e.tensor_tensor(a, x1, c, op=ALU.mult))
                        sch.em("vector", lambda e, a=thi, x2=qhi, s_=si4: e.tensor_tensor(a, x2, s_, op=ALU.mult))
                        sch.em("vector", lambda e, o=olo, a=tlo, b2=thi: e.tensor_tensor(o, a, b2, op=ALU.subtract))
                        sch.em("vector", lambda e, a=tlo, x1=qlo, s_=si4: e.tensor_tensor(a, x1, s_, op=ALU.mult))
                        sch.em("vector", lambda e, a=thi, x2=qhi, c=co4: e.tensor_tensor(a, x2, c, op=ALU.mult))
                        sch.em("vector", lambda e, o=ohi, a=tlo, b2=thi: e.tensor_tensor(o, a, b2, op=ALU.add).then_inc(sv.h))
                        sch.wait("tensor", sv, sv.inc())
                        if pi == 0 and n4 == 0 and tc == 0 and slot == 0 and "rope0" in dbg_handles:
                            dump(sch, "rope0", ropef[:, 0:512])
                            sch.wait("vector", sg, sg.n)
                        for hh in range(4):
                            h = n4 * 4 + hh
                            pb = PS[4 + hh % 2]
                            if hh >= 2:
                                sch.wait("tensor", sv, sv.n)
                            w = mm_group(sch, [lambda e, p=pb, q=hh: e.matmul(
                                p[:, 0:128], ropef[:, q * 128:(q + 1) * 128], idf_sb, start=True, stop=True, is_transpose=True)])
                            sch.wait("vector", sm, w)
                            if pi == 0:
                                dst = qT_sb[:, h * T + slot * S + tc * 128: h * T + slot * S + (tc + 1) * 128]
                                sch.em("vector", lambda e, p=pb, d=dst: e.tensor_copy(d, p[:, 0:128]).then_inc(sv.h))
                                sv.inc()
                            else:
                                sch.em("vector", lambda e, p=pb: e.tensor_copy(kstage[:, 0:128], p[:, 0:128]).then_inc(sv.h))
                                sch.wait("gpsimd", sv, sv.inc())
                                sch.em("gpsimd", lambda e, sl_=slot, hx=h, t=tc: e.dma_start(
                                    out=kvi_k[sl_, hx, :, t * 128:(t + 1) * 128],
                                    in_=kstage[:, 0:128]).then_inc(sg.h, 16))
                                sg.inc(16)
                                sch.wait("vector", sg, sg.n)
                ps_guard = sv.n
    sch.wait("gpsimd", sg, sg.n)
    sch.wait("gpsimd", sv, sv.n)
    if stop_after == "B2q":
        dump(sch, "lnxf2", lnxT[:, 0:8192])
    if not local and not noag and stop_after not in ("B2ln", "B2q", "B2s0", "B2noag"):
        sch.em("gpsimd", lambda e: e.collective_compute(
            "AllGather", ALU.bypass, replica_groups=RG,
            ins=[kvi_flat.opt()], outs=[kv_all[0:16 * SLABE].opt()]).then_inc(scc.h))
        scc.inc()
    dump(sch, "q0", qT_sb[:, 0:2048])
    AF32.pop(); ABF.pop()
    sch.run_block(nc)
    nc.all_engine_barrier()
    if stop_after is not None and stop_after.startswith("B2"):
        return nc, dbg_handles

    sch = Sched()
    if not local and not noag:
        sch.wait("scalar", scc, scc.n)
    AF32.push(); ABF.push()
    wrow = [AF32.take(D) for _ in range(2)]
    qscr = [AF32.take(D) for _ in range(7)]
    am = AF32.take(64)
    sscr = tuple(AF32.take(64) for _ in range(3))
    sc_t = AF32.take(64)
    rs_t = AF32.take(64)
    qout = AF32.take(D)
    tstage = ABF.take(DK * 128)
    ps_reader = [0, 0]

    wrow_reader = [0, 0]
    wrow_lt = {}

    def quant_matrix(src, dst, nrows, ncols, dbg=False):
        nb_ = ncols // 32
        ndc = ncols // 128
        nrt = nrows // 128

        def issue_load(rt):
            b = wrow[rt % 2]
            if wrow_reader[rt % 2]:
                sch.wait("scalar", sv, wrow_reader[rt % 2])
            sch.em("scalar", lambda e, b=b, r=rt, s=src, c=ncols: e.dma_start(out=b[:, 0:c], in_=s[r * 128:(r + 1) * 128, :]).then_inc(sl.h, 16))
            wrow_lt[rt] = sl.inc(16)

        issue_load(0)
        for rt in range(nrt):
            buf = wrow[rt % 2]
            if rt + 1 < nrt:
                issue_load(rt + 1)
            sch.wait("vector", sl, wrow_lt[rt])
            a3 = buf[:, 0:ncols].rearrange("p (a b) -> p a b", b=32)
            sch.em("vector", lambda e, a=a3, c=nb_: e.reduce_max(am[:, 0:c], a, axis=AXX, apply_absolute_value=True).then_inc(sv.h))
            sch.wait("vector", sv, sv.inc())
            emit_scales(sch, am[:, 0:nb_], tuple(s[:, 0:nb_] for s in sscr), sc_t[:, 0:nb_], rs_t[:, 0:nb_])
            sch.wait("vector", ss, ss.n)
            emit_qelem(sch, buf[:, 0:ncols], qout[:, 0:ncols], rs_t[:, 0:nb_], sc_t[:, 0:nb_], ncols, [q[:, 0:ncols] for q in qscr], 1, 6.0,
                       dbg=(dbg and rt == 0 and "qe_y" in dbg_handles))
            wrow_reader[rt % 2] = sv.n
            if dbg and rt == 0 and "qe_qout" in dbg_handles:
                dump(sch, "qe_qout", qout)
                sch.wait("vector", sg, sg.n)
                sch.wait("tensor", sg, sg.n)
            sch.wait("tensor", sv, sv.n)
            for dc in range(ndc):
                pb = PS[dc % 2]
                if ps_reader[dc % 2]:
                    sch.wait("tensor", sv, ps_reader[dc % 2])
                w = mm_group(sch, [lambda e, p=pb, d=dc: e.matmul(
                    p[:, 0:128], qout[:, d * 128:(d + 1) * 128], idf_sb, start=True, stop=True, is_transpose=True)])
                sch.wait("vector", sm, w)
                sch.em("vector", lambda e, p=pb, d=dc: e.tensor_copy(tstage[:, d * 128:(d + 1) * 128], p[:, 0:128]).then_inc(sv.h))
                ps_reader[dc % 2] = sv.inc()
            sch.wait("gpsimd", sv, sv.n)
            if dbg and rt == 0:
                dump(sch, "wq0", tstage[:, 0:2048])
            for dc in range(ndc):
                sch.em("gpsimd", lambda e, d=dc, r=rt, ds=dst: e.dma_start(
                    out=ds[d * 128:(d + 1) * 128, r * 128:(r + 1) * 128],
                    in_=tstage[:, d * 128:(d + 1) * 128]).then_inc(sg.h, 16))
                sg.inc(16)
            sch.wait("vector", sg, sg.n)

    quant_matrix(wg_sl, wgq_in, WR, D, dbg=True)
    quant_matrix(wu_sl, wuq_in, WR, D)
    quant_matrix(wd_sl, wdq_in, D, WDC)
    sch.wait("gpsimd", sg, sg.n)
    sch.wait("gpsimd", sv, sv.n)
    if not local:
        for src_, dst_ in ((wgq_in, wg_all), (wuq_in, wu_all), (wdq_in, wd_all)):
            sch.em("gpsimd", lambda e, s=src_, d=dst_: e.collective_compute(
                "AllGather", ALU.bypass, replica_groups=RG,
                ins=[s.opt()], outs=[d.opt()]).then_inc(scc.h))
            scc.inc()
    AF32.pop(); ABF.pop()
    sch.run_block(nc)
    nc.all_engine_barrier()
    if stop_after == "B3":
        return nc, dbg_handles

    kv_ready = scc.n if not local else 0
    sch = Sched()
    AF32.push(); ABF.push()
    s2_sb = [AF32.take(S) for _ in range(2)]
    bc_sb = AF32.take(S)
    kt_sb = [ABF.take(S) for _ in range(2)]
    vt_sb = [ABF.take(S) for _ in range(2)]
    s4_sb = [ABF.take(S) for _ in range(2)]
    if not local:
        sch.wait("gpsimd", scc, kv_ready)
    gather_base = sg.n
    gather_done = {}

    def gather(e):
        for u in range(24):
            if u == 0 or u == NU_A:
                e.dma_start(out=kv_seq[u * SLABE:(u + 1) * SLABE],
                            in_=kvi_flat[(0 if u == 0 else 1) * SLABE:(1 if u == 0 else 2) * SLABE]).then_inc(sg.h, 16)
            else:
                with e.register(f"g{u}") as r:
                    e.reg_load(r, utab_sb[0:1, u:u + 1])
                    rv = RuntimeValue(r, min_val=0, max_val=16 * SLABE)
                    e.dma_start(out=kv_seq[u * SLABE:(u + 1) * SLABE],
                                in_=kv_all[bass.DynSlice(rv, SLABE)]).then_inc(sg.h, 16)
    sch.em("gpsimd", gather)
    for u in range(24):
        gather_done[u] = gather_base + 16 * (u + 1)
    sg.inc(16 * 24)

    NUM, DEN = PS[2], PS[3]
    kv_reader = [0, 0]
    qk_bank = [0, 0]
    s2_bank = [0, 0]
    s4_bank = [0, 0]
    for h in range(H):
        for slot in range(2):
            NU = NU_A if slot == 0 else NU_B
            q_rhs = qT_sb[:, h * T + slot * S: h * T + (slot + 1) * S]
            for u in range(NU):
                uid = u + (0 if slot == 0 else NU_A)
                if kv_reader[u % 2]:
                    sch.wait("gpsimd", sm, kv_reader[u % 2])
                sch.wait("gpsimd", sg, gather_done[uid])

                def kvload(e, i=uid, b=u % 2, hh=h):
                    kb_ = i * SLABE + hh * 65536
                    e.dma_start(out=kt_sb[b], in_=kv_seq[kb_:kb_ + 65536].rearrange("(p f) -> p f", p=128)).then_inc(sg.h, 16)
                    vb_ = i * SLABE + KPART + hh * 4 * 16384
                    e.dma_start(out=vt_sb[b][:, 0:512].rearrange("p (a f) -> p a f", f=128),
                                in_=kv_seq[vb_:vb_ + 65536].rearrange("(a p f) -> p a f", p=128, f=128)).then_inc(sg.h, 16)
                sch.em("gpsimd", kvload)
                kt_t = sg.inc(16)
                vt_t = sg.inc(16)
                sch.wait("tensor", sg, kt_t)
                diag = (u == 0)
                first = (u == 0)
                last = (u == NU - 1)

                def emit_v(kt):
                    if kt == 0:
                        sch.wait("tensor", sg, vt_t)
                    sch.wait("tensor", sv, s4_t[kt])
                    f_ = first and kt == 0
                    l_ = last and kt == 3
                    def vm(e, b=u % 2, k=kt, f=f_, l=l_):
                        e.matmul(NUM[:, 0:512], vt_sb[b][:, k * 128:(k + 1) * 128], s4_sb[k % 2], start=f, stop=l)
                        return e.matmul(DEN[0:1, 0:512], onesb_sb, s4_sb[k % 2], start=f, stop=l)
                    s4_bank[kt % 2] = mm_group(sch, [vm])

                s4_t = {}
                for kt in range(4):
                    sb_ = PS[kt % 2]
                    if qk_bank[kt % 2]:
                        sch.wait("tensor", ss, qk_bank[kt % 2])
                    w = mm_group(sch, [lambda e, p=sb_, b=u % 2, k=kt, q=q_rhs: e.matmul(
                        p[:, 0:512], kt_sb[b][:, k * 128:(k + 1) * 128], q, start=True, stop=True)])
                    sch.wait("scalar", sm, w)
                    if s2_bank[kt % 2]:
                        sch.wait("scalar", sv, s2_bank[kt % 2])
                    sch.em("scalar", lambda e, p=sb_, k=kt: e.activation(out=s2_sb[k % 2], in_=p[:, 0:512], func=AF.Square).then_inc(ss.h))
                    qk_bank[kt % 2] = ss.inc()
                    sch.wait("vector", ss, qk_bank[kt % 2])
                    if s4_bank[kt % 2]:
                        sch.wait("vector", sm, s4_bank[kt % 2])
                    if diag:
                        sch.em("vector", lambda e, k=kt: e.tensor_tensor(s2_sb[k % 2], s2_sb[k % 2], mask_sb[:, k * S:(k + 1) * S], op=ALU.mult))
                    sch.em("vector", lambda e, k=kt: e.tensor_tensor(s4_sb[k % 2], s2_sb[k % 2], s2_sb[k % 2], op=ALU.mult).then_inc(sv.h))
                    s2_bank[kt % 2] = s4_t[kt] = sv.inc()
                    if kt >= 1:
                        emit_v(kt - 1)
                emit_v(3)
                kv_reader[u % 2] = sm.n
            sch.wait("vector", sm, sm.n)
            sch.em("vector", lambda e: e.tensor_scalar(mm_row[0:1, 0:512], DEN[0:1, 0:512], float(ATT_EPS), None, op0=ALU.max).then_inc(sv.h))
            sch.wait("scalar", sv, sv.inc())
            sch.em("scalar", lambda e: e.activation(out=mm_row[0:1, 512:1024], in_=mm_row[0:1, 0:512], func=AF.Ln))
            sch.em("scalar", lambda e: e.activation(out=mm_row[0:1, 0:512], in_=mm_row[0:1, 512:1024], func=AF.Exp, scale=-1.0).then_inc(ss.h))
            sch.wait("tensor", ss, ss.inc())
            w = mm_group(sch, [lambda e: e.matmul(PS[4][:, 0:512], ones1_sb, mm_row[0:1, 0:512], start=True, stop=True)])
            sch.wait("vector", sm, w)
            sch.em("vector", lambda e: e.tensor_copy(bc_sb, PS[4][:, 0:512]))
            dst = attnT_sb[:, h * T + slot * S:(h * T + (slot + 1) * S)]
            sch.em("vector", lambda e, d=dst: e.tensor_tensor(d, NUM[:, 0:512], bc_sb, op=ALU.mult).then_inc(sv.h))
            sch.wait("tensor", sv, sv.inc())
            if h == 0 and slot == 0:
                dump(sch, "den0", bc_sb)
    dump(sch, "attn0", attnT_sb[:, 0:2048])
    AF32.pop(); ABF.pop()
    sch.run_block(nc)
    nc.all_engine_barrier()
    if stop_after == "B4":
        return nc, dbg_handles

    sch = Sched()
    AF32.push(); ABF.push()
    x_tc2 = [AF32.take(D) for _ in range(2)]
    x2t = AF32.take(D)
    epi = AF32.take(512)
    xsq2 = AF32.take(D)
    st2 = AF32.take(16)
    h2s = AF32.take(D)
    am2 = AF32.take(64)
    ss2 = tuple(AF32.take(64) for _ in range(3))
    sc2 = AF32.take(64)
    rs2 = AF32.take(64)
    qscr2 = [AF32.take(512) for _ in range(7)]
    ga_b = AF32.take(D)
    cb1_b = AF32.take(D)
    sb_b = AF32.take(D)
    wo_t = [ABF.take(D) for _ in range(2)]
    hq = AF32.take(D)
    emit_bcast_row(sch, rows_dram[RIDX["ga"], :].rearrange("(o d) -> o d", o=1), ga_b)
    emit_bcast_row(sch, rows_dram[RIDX["cb1"], :].rearrange("(o d) -> o d", o=1), cb1_b)
    emit_bcast_row(sch, mod_flat[6144:6144 + D].rearrange("(o d) -> o d", o=1), sb_b)

    wo_reader = [0, 0]
    xc = [AF32.take(512) for _ in range(2)]
    xc_reader = [0, 0]
    ps_free = 0
    for n4 in range(4):
        for dk in range(DK):
            wb = wo_t[dk % 2]
            if wo_reader[dk % 2]:
                sch.wait("sync", sm, wo_reader[dk % 2])
            sch.em("sync", lambda e, b=wb, d=dk, n=n4: e.dma_start(
                out=b[:, 0:512], in_=wq_attn[3, d * 128:(d + 1) * 128, n * 512:(n + 1) * 512]).then_inc(sl.h, 16))
            sch.wait("tensor", sl, sl.inc(16))
            if dk == 0:
                sch.wait("tensor", sv, ps_free)
            fns = []
            for t8 in range(8):
                def mmf(e, b=wb, t=t8, d=dk):
                    lh = attnT_sb[:, d * T + (t // 4) * S + (t % 4) * 128: d * T + (t // 4) * S + (t % 4 + 1) * 128]
                    return e.matmul(PS[t][:, 0:512], lh, b[:, 0:512], start=(d == 0), stop=(d == DK - 1))
                fns.append(mmf)
            wo_reader[dk % 2] = mm_group(sch, fns)
        sch.wait("vector", sm, sm.n)
        for t8 in range(8):
            slot, tc = t8 // 4, t8 % 4
            xin = xa if slot == 0 else xb
            xb_ = xc[t8 % 2]
            if xc_reader[t8 % 2]:
                sch.wait("sync", sv, xc_reader[t8 % 2])
            sch.em("sync", lambda e, b=xb_, xi=xin, t=tc, n=n4: e.dma_start(
                out=b, in_=xi[t * 128:(t + 1) * 128, n * 512:(n + 1) * 512]).then_inc(sl.h, 16))
            sch.wait("vector", sl, sl.inc(16))
            sch.em("vector", lambda e, t=t8, n=n4: e.tensor_tensor(epi, PS[t][:, 0:512], ga_b[:, n * 512:(n + 1) * 512], op=ALU.mult))
            sch.em("vector", lambda e, b=xb_: e.tensor_tensor(epi, epi, b, op=ALU.add).then_inc(sv.h))
            xc_reader[t8 % 2] = sv.inc()
            sch.wait("gpsimd", sv, xc_reader[t8 % 2])
            sch.em("gpsimd", lambda e, t=t8, n=n4: e.dma_start(
                out=x2_dram[t * 128:(t + 1) * 128, n * 512:(n + 1) * 512], in_=epi).then_inc(sg.h, 16))
            sg.inc(16)
            sch.wait("vector", sg, sg.n)
        ps_free = sv.n
    sch.wait("sync", sg, sg.n)
    for t8 in range(8):
        sch.wait("sync", sv, sv.n)
        sch.em("sync", lambda e, t=t8: e.dma_start(out=x2t, in_=x2_dram[t * 128:(t + 1) * 128, :]).then_inc(sl.h, 16))
        sch.wait("vector", sl, sl.inc(16))

        def norm2(m, rstd):
            sch.em("vector", lambda e: e.tensor_scalar(h2s, x2t, m, rstd, op0=ALU.subtract, op1=ALU.mult))
            sch.em("vector", lambda e: e.tensor_tensor(h2s, h2s, cb1_b, op=ALU.mult))
            sch.em("vector", lambda e: e.tensor_tensor(h2s, h2s, sb_b, op=ALU.add))
            a3 = h2s.rearrange("p (a b) -> p a b", b=32)
            sch.em("vector", lambda e, a=a3: e.reduce_max(am2, a, axis=AXX, apply_absolute_value=True).then_inc(sv.h))
            sch.wait("vector", sv, sv.inc())
        emit_ln(sch, x2t, xsq2, st2, norm2)
        emit_scales(sch, am2, ss2, sc2, rs2)
        sch.wait("vector", ss, ss.n)
        for n4 in range(4):
            emit_qelem(sch, h2s[:, n4 * 512:(n4 + 1) * 512], hq[:, n4 * 512:(n4 + 1) * 512],
                       rs2[:, n4 * 16:(n4 + 1) * 16], sc2[:, n4 * 16:(n4 + 1) * 16], 512, qscr2, 3, 7.5)
        sch.wait("tensor", sv, sv.n)
        for dc in range(DK):
            pb = PS[4 + dc % 2]
            if dc >= 2:
                sch.wait("tensor", sv, sv.n)
            w = mm_group(sch, [lambda e, p=pb, d=dc: e.matmul(
                p[:, 0:128], hq[:, d * 128:(d + 1) * 128], idf_sb, start=True, stop=True, is_transpose=True)])
            sch.wait("vector", sm, w)
            sch.em("vector", lambda e, p=pb, d=dc, t=t8: e.tensor_copy(
                hqT_sb[:, d * T + t * 128:d * T + (t + 1) * 128], p[:, 0:128]).then_inc(sv.h))
            sv.inc()
        sch.wait("tensor", sv, sv.n)
        if t8 == 0:
            dump(sch, "x2", x2t)
            dump(sch, "hq0", hq)
    AF32.pop(); ABF.pop()
    sch.run_block(nc)
    nc.all_engine_barrier()
    if stop_after == "B6":
        return nc, dbg_handles

    w_ready = scc.n
    sch = Sched()
    amx_all = AF32.take(2048)
    AF32.push(); ABF.push()
    sg_scr = AF32.take(512)
    int_scr = AF32.take(512)
    wgu = attnT_sb[:, 0:2 * DK * 512]
    if not local:
        sch.wait("sync", scc, w_ready)
    for ic in range(16):
        r, hf = ic // 2, ic % 2
        if ic > 0:
            sch.wait("sync", sm, sm.n)
        for d2 in range(DK):
            for gu in range(2):
                srcw = wg_all if gu == 0 else wu_all
                if local:
                    ap_ = srcw[d2 * 128:(d2 + 1) * 128, ic * 512:(ic + 1) * 512]
                else:
                    ap_ = srcw[r, d2 * 128:(d2 + 1) * 128, hf * 512:(hf + 1) * 512]
                sch.em("sync", lambda e, a=ap_, d=d2, g=gu: e.dma_start(
                    out=wgu[:, (g * DK + d) * 512:(g * DK + d + 1) * 512], in_=a).then_inc(sl.h, 16))
                sl.inc(16)
        sch.wait("tensor", sl, sl.n)
        for t8 in range(8):
            pg, pu = PS[t8 % 2], PS[2 + t8 % 2]
            if not (ic == 0 and t8 < 2):
                sch.wait("tensor", sv, sv.n)
            fns = []
            for d2 in range(DK):
                def mmf(e, d=d2, t=t8, a=pg, b=pu):
                    lh = hqT_sb[:, d * T + t * 128: d * T + (t + 1) * 128]
                    e.matmul(a[:, 0:512], lh, wgu[:, d * 512:(d + 1) * 512], start=(d == 0), stop=(d == DK - 1))
                    return e.matmul(b[:, 0:512], lh, wgu[:, (DK + d) * 512:(DK + d + 1) * 512], start=(d == 0), stop=(d == DK - 1))
                fns.append(mmf)
            w = mm_group(sch, fns)
            sch.wait("scalar", sm, w)
            sch.em("scalar", lambda e, a=pg: e.activation(out=sg_scr, in_=a[:, 0:512], func=AF.Silu).then_inc(ss.h))
            sch.wait("vector", ss, ss.inc())
            sch.em("vector", lambda e, b=pu: e.tensor_tensor(int_scr, sg_scr, b[:, 0:512], op=ALU.mult))
            i3 = int_scr.rearrange("p (a b) -> p a b", b=32)
            sch.em("vector", lambda e, a=i3, t=t8, i=ic: e.reduce_max(
                amx_all[:, t * 256 + i * 16: t * 256 + (i + 1) * 16], a, axis=AXX,
                apply_absolute_value=True).then_inc(sv.h))
            vw = sv.inc()
            sch.wait("gpsimd", sv, vw)
            sch.em("gpsimd", lambda e, t=t8, i=ic: e.dma_start(
                out=inter_dram[t * 128:(t + 1) * 128, i * 512:(i + 1) * 512], in_=int_scr).then_inc(sg.h, 16))
            sg.inc(16)
            sch.wait("vector", sg, sg.n)
    dump(sch, "amx", amx_all)
    AF32.pop(); ABF.pop()
    sch.run_block(nc)
    nc.all_engine_barrier()
    if stop_after == "B8":
        return nc, dbg_handles

    sch = Sched()
    AF32.push(); ABF.push()
    sc_all = AF32.take(2048)
    rs_all = AF32.take(2048)
    ssc = tuple(AF32.take(512) for _ in range(3))
    itile = [AF32.take(512) for _ in range(2)]
    qscr3 = [AF32.take(512) for _ in range(7)]
    gb_b = AF32.take(D)
    x2l = AF32.take(D)
    ot = AF32.take(D)
    iq = AF32.take(512)
    iqT = attnT_sb[:, 0:2 * II]
    wd_t = [ABF.take(D) for _ in range(2)]
    wd_reader = [0, 0]
    itile_reader = [0, 0]
    dbgffn = AF32.take(D) if "ffn0" in dbg_handles else None
    emit_bcast_row(sch, rows_dram[RIDX["gb"], :].rearrange("(o d) -> o d", o=1), gb_b)
    if "int4" in dbg_handles:
        sch.em("gpsimd", lambda e: e.dma_start(out=dbg_handles["int4"], in_=inter_dram[512:640, :]).then_inc(sg.h, 16))
        sch.wait("gpsimd", sg, sg.inc(16))
    for qc in range(4):
        emit_scales(sch, amx_all[:, qc * 512:(qc + 1) * 512], ssc,
                    sc_all[:, qc * 512:(qc + 1) * 512], rs_all[:, qc * 512:(qc + 1) * 512])
    sch.wait("vector", ss, ss.n)
    if "scall" in dbg_handles:
        dump(sch, "scall", sc_all)
        sch.wait("vector", sg, sg.n)
        sch.wait("scalar", sg, sg.n)
    for pair in range(4):
        for half in range(2):
            t8 = pair * 2 + half
            for ic in range(16):
                buf = itile[ic % 2]
                if itile_reader[ic % 2]:
                    sch.wait("sync", sv, itile_reader[ic % 2])
                sch.em("sync", lambda e, b=buf, t=t8, i=ic: e.dma_start(
                    out=b, in_=inter_dram[t * 128:(t + 1) * 128, i * 512:(i + 1) * 512]).then_inc(sl.h, 16))
                sch.wait("vector", sl, sl.inc(16))
                rsl = rs_all[:, t8 * 256 + ic * 16: t8 * 256 + (ic + 1) * 16]
                scl = sc_all[:, t8 * 256 + ic * 16: t8 * 256 + (ic + 1) * 16]
                emit_qelem(sch, buf, iq, rsl, scl, 512, qscr3, 3, 7.5)
                itile_reader[ic % 2] = sv.n
                if pair == 0 and half == 0 and ic == 0 and "iq0" in dbg_handles:
                    dump(sch, "iq0", iq)
                    sch.wait("vector", sg, sg.n)
                if pair == 2 and half == 0 and ic == 0 and "iq4" in dbg_handles:
                    dump(sch, "iq4", iq)
                    sch.wait("vector", sg, sg.n)
                sch.wait("tensor", sv, sv.n)
                for j in range(4):
                    pb = PS[6 + j % 2]
                    if j >= 2:
                        sch.wait("tensor", sv, sv.n)
                    w = mm_group(sch, [lambda e, p=pb, q=j: e.matmul(
                        p[:, 0:128], iq[:, q * 128:(q + 1) * 128], idf_sb, start=True, stop=True, is_transpose=True)])
                    sch.wait("vector", sm, w)
                    sch.em("vector", lambda e, p=pb, hh=half, i=ic, q=j: e.tensor_copy(
                        iqT[:, hh * II + (i * 4 + q) * 128: hh * II + (i * 4 + q + 1) * 128], p[:, 0:128]).then_inc(sv.h))
                    sv.inc()
            sch.wait("tensor", sv, sv.n)
        for ik in range(II // 128):
            buf = wd_t[ik % 2]
            if wd_reader[ik % 2]:
                sch.wait("sync", sm, wd_reader[ik % 2])
            if local:
                ap_ = wd_all[ik * 128:(ik + 1) * 128, :]
            else:
                ap_ = wd_all[ik // 8, (ik % 8) * 128:(ik % 8 + 1) * 128, :]
            sch.em("sync", lambda e, b=buf, a=ap_: e.dma_start(out=b, in_=a).then_inc(sl.h, 16))
            sch.wait("tensor", sl, sl.inc(16))
            fns = []
            for half in range(2):
                for n4 in range(4):
                    def mmf(e, b=buf, hh=half, n=n4, k=ik):
                        return e.matmul(PS[hh * 4 + n][:, 0:512], iqT[:, hh * II + k * 128:hh * II + (k + 1) * 128],
                                        b[:, n * 512:(n + 1) * 512], start=(k == 0), stop=(k == II // 128 - 1))
                    fns.append(mmf)
            wd_reader[ik % 2] = mm_group(sch, fns)
        sch.wait("vector", sm, sm.n)
        if pair in (0, 2) and "ffn0" in dbg_handles:
            for hh, nm in ((0, "ffn0" if pair == 0 else "ffn4"), (1, "ffn1" if pair == 0 else "ffn5")):
                for n4 in range(4):
                    sch.em("vector", lambda e, n=n4, h_=hh: e.tensor_copy(dbgffn[:, n * 512:(n + 1) * 512], PS[h_ * 4 + n][:, 0:512]).then_inc(sv.h))
                    sv.inc()
                dump(sch, nm, dbgffn)
                sch.wait("vector", sg, sg.n)
        for half in range(2):
            t8 = pair * 2 + half
            sch.wait("sync", sv, sv.n)
            sch.em("sync", lambda e, t=t8: e.dma_start(out=x2l, in_=x2_dram[t * 128:(t + 1) * 128, :]).then_inc(sl.h, 16))
            sch.wait("vector", sl, sl.inc(16))
            sch.wait("vector", sout, sout.n)
            for n4 in range(4):
                sch.em("vector", lambda e, hh=half, n=n4: e.tensor_tensor(
                    ot[:, n * 512:(n + 1) * 512], PS[hh * 4 + n][:, 0:512], gb_b[:, n * 512:(n + 1) * 512], op=ALU.mult))
                sch.em("vector", lambda e, n=n4: e.tensor_tensor(
                    ot[:, n * 512:(n + 1) * 512], ot[:, n * 512:(n + 1) * 512], x2l[:, n * 512:(n + 1) * 512], op=ALU.add))
            sch.em("vector", lambda e: e.tensor_scalar(st2[:, 8:9] if False else ot[:, 0:1], ot[:, 0:1], 1.0, None, op0=ALU.mult).then_inc(sv.h))
            vw = sv.inc()
            sch.wait("gpsimd", sv, vw)
            sch.em("gpsimd", lambda e, t=t8: e.dma_start(out=out[t * 128:(t + 1) * 128, :], in_=ot).then_inc(sout.h, 16))
            sout.inc(16)
            sch.wait("vector", sout, sout.n)
            sch.wait("tensor", sv, vw)
    sch.wait("gpsimd", sout, sout.n)
    AF32.pop(); ABF.pop()
    sch.run_block(nc)

    return nc, dbg_handles


def _host_tables(c):
    bA, bB = c, 15 - c
    pos = np.concatenate([np.arange(bA * S, (bA + 1) * S), np.arange(bB * S, (bB + 1) * S)])
    j = np.arange(64)
    theta = 10000.0 ** (-(j.astype(np.float64)) * 2.0 / 128.0)
    ang = pos[:, None].astype(np.float64) * theta[None, :]
    cosq = np.cos(ang).astype(np.float32)
    sinq = np.sin(ang).astype(np.float32)

    def slab(g):
        return 2 * g if g < 8 else 2 * (15 - g) + 1
    ua = [slab(g) for g in range(bA, -1, -1)] + [16] * (NU_A - (bA + 1))
    ub = [slab(g) for g in range(bB, -1, -1)] + [16] * (NU_B - (bB + 1))
    sl_ = np.array(ua + ub + [16] * (64 - NU_A - NU_B), np.int64)
    utab = (sl_ * SLABE).astype(np.int32).reshape(1, 64)
    return cosq, sinq, utab


def _prep_inputs(inputs, mode="ag"):
    x = np.ascontiguousarray(np.asarray(inputs["L0_attn_x"]).reshape(8192, D))
    wqT = np.ascontiguousarray(np.asarray(inputs["L0_attn_Wq"]).T)
    wkT = np.ascontiguousarray(np.asarray(inputs["L0_attn_Wk"]).T)
    wvT = np.ascontiguousarray(np.asarray(inputs["L0_attn_Wv"]).T)
    woT = np.ascontiguousarray(np.asarray(inputs["L0_attn_Wo"]).T)
    wg = np.asarray(inputs["L0_ffn_w_gate"])
    wu = np.asarray(inputs["L0_ffn_w_up"])
    wd = np.asarray(inputs["L0_ffn_w_down"])
    cv = np.asarray(inputs["L0_c"]).reshape(D)
    cvT = np.ascontiguousarray(cv.reshape(DK, 128).T)
    wmod = np.asarray(inputs["L0_wmod"])
    in_maps = []
    for c in range(8):
        bA, bB = c, 15 - c
        cosq, sinq, utab = _host_tables(c)
        m = {
            "xa": np.ascontiguousarray(x[bA * S:(bA + 1) * S]),
            "xb": np.ascontiguousarray(x[bB * S:(bB + 1) * S]),
            "wqT": wqT, "wkT": wkT, "wvT": wvT, "woT": woT,
            "cvT": cvT, "cosq": cosq, "sinq": sinq, "utab": utab,
        }
        if mode == "local":
            m["wmod"] = wmod
            m["wg_sl"] = wg
            m["wu_sl"] = wu
            m["wd_sl"] = wd
        else:
            m["wmod"] = np.ascontiguousarray(wmod[:, c * 1536:(c + 1) * 1536])
            m["wg_sl"] = np.ascontiguousarray(wg[c * 1024:(c + 1) * 1024])
            m["wu_sl"] = np.ascontiguousarray(wu[c * 1024:(c + 1) * 1024])
            m["wd_sl"] = np.ascontiguousarray(wd[:, c * 1024:(c + 1) * 1024])
        in_maps.append(m)
    return in_maps


_MODE = "ag"
_ROUND = "magic"


def kernel(**inputs):
    in_maps = _prep_inputs(inputs, _MODE)
    nc, _ = build_program(_MODE, _ROUND, debug=())
    res = run_bass_kernel_spmd(nc, in_maps, core_ids=list(range(8)))
    outf = np.zeros((8192, D), np.float32)
    for c in range(8):
        o = np.asarray(res.results[c]["out"])
        outf[c * S:(c + 1) * S] = o[0:S]
        outf[(15 - c) * S:(16 - c) * S] = o[S:2 * S]
    return outf.reshape(1, 8192, D)


# version 05
# speedup vs baseline: 1.0682x; 1.0682x over previous
import numpy as np
import ml_dtypes
import concourse.bass as bass
import concourse.mybir as mybir
from concourse.bass_utils import run_bass_kernel_spmd
from concourse.expressions_rust import RuntimeValue

F32 = mybir.dt.float32
BF16 = mybir.dt.bfloat16
I32 = mybir.dt.int32
AF = mybir.ActivationFunctionType
ALU = mybir.AluOpType
ET = mybir.EngineType
AXX = mybir.AxisListType.X

D = 2048
H = 16
S = 512
KPART = 16 * 128 * 512
SLABE = 2 * 16 * 128 * 512
T = 1024
II = 8192
DK = 16
LN2 = float(np.log(2.0))
MAGIC = float(1.5 * 2.0 ** 23)
LN_EPS = 1e-5
ATT_EPS = 1e-6
NU_A, NU_B = 8, 16
BF = ml_dtypes.bfloat16
RG = [list(range(8))]


class Sem:
    def __init__(self, nc, name):
        self.h = nc.alloc_semaphore(name)
        self.n = 0

    def inc(self, k=1):
        self.n += k
        return self.n


class Sched:
    def __init__(self):
        self.q = {e: [] for e in ("sync", "gpsimd", "tensor", "vector", "scalar")}

    def em(self, eng, fn):
        self.q[eng].append(fn)

    def wait(self, eng, sem, val):
        self.q[eng].append(lambda e, s=sem.h, v=val: e.wait_ge(s, v))

    def run_block(self, nc):
        with nc.Block() as block:
            for name, deco in (("sync", block.sync), ("gpsimd", block.gpsimd),
                               ("tensor", block.tensor), ("vector", block.vector),
                               ("scalar", block.scalar)):
                if self.q[name]:
                    def body(e, _q=tuple(self.q[name])):
                        for f in _q:
                            f(e)
                    deco(body)


class Arena:

    def __init__(self, nc, name, cols, dtype):
        self.t = nc.alloc_sbuf_tensor(name, [128, cols], dtype)
        self.cols = cols
        self.off = 0
        self.marks = []

    def push(self):
        self.marks.append(self.off)

    def pop(self):
        self.off = self.marks.pop()

    def take(self, cols):
        self.off = (self.off + 15) // 16 * 16
        st = self.off
        self.off += cols
        assert self.off <= self.cols, f"{self.t.name} arena overflow: {self.off} > {self.cols}"
        return self.t.ap()[:, st:st + cols]


def build_program(mode="ag", rounding="magic", debug=(), stop_after=None, noag=False):
    nc = bass.Bass("TRN2", target_bir_lowering=False, debug=False, dynamic_dma_scratch_size=20480)
    local = (mode == "local")

    def inp(name, shape, dt=F32):
        return nc.dram_tensor(name, shape, dt, kind="ExternalInput").ap()

    xa = inp("xa", [S, D])
    xb = inp("xb", [S, D])
    wT_in = [inp(n, [D, D]) for n in ("wqT", "wkT", "wvT", "woT")]
    cvT = inp("cvT", [128, DK])
    wmod = inp("wmod", [D, (96 if local else 12) * 128])
    NCH = 96 if local else 12
    WR = 8192 if local else 1024
    WDC = 8192 if local else 1024
    wg_sl = inp("wg_sl", [WR, D])
    wu_sl = inp("wu_sl", [WR, D])
    wd_sl = inp("wd_sl", [D, WDC])
    cosq = inp("cosq", [T, 64])
    sinq = inp("sinq", [T, 64])
    utab = inp("utab", [1, 64], I32)

    out = nc.dram_tensor("out", [T, D], F32, kind="ExternalOutput").ap()
    dbg_handles = {}

    def dbgh(name, shape, dt=F32):
        if name in debug:
            dbg_handles[name] = nc.dram_tensor("dbg_" + name, shape, dt, kind="ExternalOutput").ap()

    for nm, shp in (("mod", [128, 96]), ("q0", [128, 2048]), ("k0", [128, 2048]),
                    ("v0", [128, 2048]), ("ln0", [128, 2048]), ("attn0", [128, 2048]),
                    ("x2", [128, 2048]), ("hq0", [128, 2048]), ("amx", [128, 2048]),
                    ("wq0", [128, 2048]), ("den0", [128, 512]), ("wqa", [128, 2048]), ("lnx", [128, 2048]), ("lnf", [128, 2048]), ("st", [128, 16]),
                    ("qtm0", [128, 512]), ("rope0", [128, 512]), ("cos0", [128, 64]),
                    ("qbb", [128, 2048]), ("psq0", [128, 512]),
                    ("qe_y", [128, 2048]), ("qe_a", [128, 2048]), ("qe_g2", [128, 2048]),
                    ("qe_w1", [128, 2048]), ("qe_w2", [128, 2048]), ("qe_rnd", [128, 2048]),
                    ("qe_qout", [128, 2048]), ("iq0", [128, 512]), ("ffn0", [128, 2048]),
                    ("ffn1", [128, 2048]), ("ffn4", [128, 2048]), ("ffn5", [128, 2048]),
                    ("int4", [128, 8192]), ("iq4", [128, 512]), ("scall", [128, 2048])):
        dbgh(nm, shp)
    for nm, shp in (("wqfull", [2048, 2048]), ("lnxf", [128, 8192]), ("lnxf2", [128, 8192])):
        if nm in debug:
            dbg_handles[nm] = nc.dram_tensor("dbg_" + nm, shp, BF16, kind="ExternalOutput").ap()
    if False:
        dbgh(nm, shp)

    ident_bf = nc.inline_tensor(np.eye(128, dtype=BF), "identbf").ap()
    ident_f = nc.inline_tensor(np.eye(128, dtype=np.float32), "identf").ap()
    ones_bf = nc.inline_tensor(np.ones((128, 1), dtype=BF), "onesbf").ap()
    ones1 = nc.inline_tensor(np.ones((1, 128), dtype=np.float32), "ones1").ap()
    mk = np.zeros((128, 4 * S), dtype=BF)
    for kt in range(4):
        kid = kt * 128 + np.arange(128)[:, None]
        mk[:, kt * S:(kt + 1) * S] = (kid <= np.arange(S)[None, :]).astype(BF)
    maskc = nc.inline_tensor(mk, "maskc").ap()

    def idram(name, shape, dt):
        return nc.dram_tensor(name, shape, dt, kind="Internal").ap()

    kv_in = idram("kv_in", [2, SLABE], BF16)
    kv_all = idram("kv_all", [17 * SLABE], BF16)
    kv_seq = idram("kv_seq", [24 * SLABE], BF16)
    kvi_k = kv_in.rearrange("s (a p f) -> s a p f", p=128, f=S)
    kvi_flat = kv_in.rearrange("s e -> (s e)")
    wq_attn = idram("wq_attn", [4, D, D], BF16)
    wgq_in = idram("wgq_in", [D, WR], BF16)
    wuq_in = idram("wuq_in", [D, WR], BF16)
    wdq_in = idram("wdq_in", [WDC, D], BF16)
    if local:
        wg_all, wu_all, wd_all = wgq_in, wuq_in, wdq_in
        mod_in = idram("mod_in", [96, 128], F32)
        mod_all = mod_in
    else:
        wg_all = idram("wg_all", [8, D, 1024], BF16)
        wu_all = idram("wu_all", [8, D, 1024], BF16)
        wd_all = idram("wd_all", [8, 1024, D], BF16)
        mod_in = idram("mod_in", [12, 128], F32)
        mod_all = idram("mod_all", [8, 12, 128], F32)
    mod_flat = mod_all.rearrange("a b c -> (a b c)") if not local else mod_all.rearrange("b c -> (b c)")
    x2_dram = idram("x2_dram", [T, D], F32)
    rows_dram = idram("rows_dram", [8, D], F32)
    RIDX = {"ga": 0, "gb": 1, "cb1": 2, "qb": 3, "kb": 4, "vb": 5}
    inter_dram = idram("inter_dram", [T, II], F32)

    AF32 = Arena(nc, "af32", 26624, F32)
    ABF = Arena(nc, "abf", 41472, BF16)
    mm_row = nc.alloc_sbuf_tensor("mmrow", [1, D], F32).ap()
    ones1_sb = nc.alloc_sbuf_tensor("ones1s", [1, 128], F32).ap()
    utab_sb = nc.alloc_sbuf_tensor("utabs", [1, 64], I32).ap()

    id_sb = ABF.take(128)
    idf_sb = AF32.take(128)
    onesb_sb = ABF.take(16)[:, 0:1]
    mask_sb = ABF.take(4 * S)
    qT_sb = ABF.take(H * T)
    hqT_sb = qT_sb
    attnT_sb = ABF.take(H * T)
    mod_sb = AF32.take(96)
    ca1_sb = AF32.take(DK)
    cb1_sb = AF32.take(DK)
    cos_sb = AF32.take(8 * 64)
    sin_sb = AF32.take(8 * 64)
    cst66 = AF32.take(16)

    PS = [nc.alloc_psum_tensor(f"ps{i}", [128, 512], F32).ap() for i in range(8)]

    sprobe = Sem(nc, "sprobe") if "semprobe" in debug else None
    sl = Sem(nc, "sl")
    sg = Sem(nc, "sg")
    sm = Sem(nc, "sm")
    sv = Sem(nc, "sv")
    ss = Sem(nc, "ss")
    scc = Sem(nc, "scc")
    sout = Sem(nc, "sout")

    def mm_group(sch, fns):
        def run(e, _fns=tuple(fns)):
            ins = None
            for f in _fns:
                ins = f(e)
            ins.then_inc(sm.h)
        sch.em("tensor", run)
        return sm.inc()

    def v_end(sch):
        raise RuntimeError("unused")

    def emit_scales(sch, amax_ap, scr3, scale_ap, rscale_ap):
        ag, lt, t2 = scr3
        sch.em("vector", lambda e: e.tensor_scalar(ag, amax_ap, float(2.0 ** -60), None, op0=ALU.max).then_inc(sv.h))
        sch.wait("scalar", sv, sv.inc())
        sch.em("scalar", lambda e: e.activation(out=lt, in_=ag, func=AF.Ln).then_inc(ss.h))
        sch.wait("vector", ss, ss.inc())
        sch.em("vector", lambda e: e.tensor_scalar(ag, lt, float(1.0 / LN2), 64.0 + 1e-9, op0=ALU.mult, op1=ALU.add).then_inc(sv.h))
        sch.wait("vector", sv, sv.inc())
        if rounding == "magic":
            sch.em("vector", lambda e: e.tensor_scalar(t2, ag, MAGIC, MAGIC, op0=ALU.add, op1=ALU.subtract).then_inc(sv.h))
            sch.wait("vector", sv, sv.inc())
            sch.em("vector", lambda e: e.tensor_tensor(lt, t2, ag, op=ALU.subtract).then_inc(sv.h))
            sch.wait("vector", sv, sv.inc())
            sch.em("vector", lambda e: e.tensor_scalar(lt, lt, 1e-12, None, op0=ALU.is_ge).then_inc(sv.h))
            sch.wait("vector", sv, sv.inc())
            sch.em("vector", lambda e: e.scalar_tensor_tensor(t2, lt, -1.0, t2, op0=ALU.mult, op1=ALU.add).then_inc(sv.h))
        else:
            r = lt
            sch.em("vector", lambda e: e.tensor_copy(r, ag).then_inc(sv.h))
            sch.wait("vector", sv, sv.inc())
            for stp in (64.0, 32.0, 16.0, 8.0, 4.0, 2.0, 1.0):
                sch.em("vector", lambda e, s=stp: e.tensor_scalar(t2, r, s, None, op0=ALU.is_ge).then_inc(sv.h))
                sch.wait("vector", sv, sv.inc())
                sch.em("vector", lambda e, s=stp: e.scalar_tensor_tensor(r, t2, -s, r, op0=ALU.mult, op1=ALU.add).then_inc(sv.h))
                sch.wait("vector", sv, sv.inc())
            sch.em("vector", lambda e: e.tensor_tensor(t2, ag, r, op=ALU.subtract).then_inc(sv.h))
        sch.wait("scalar", sv, sv.inc())
        sch.em("scalar", lambda e: e.activation(out=scale_ap, in_=t2, func=AF.Exp, scale=LN2, bias=cst66[:, 0:1]))
        sch.em("scalar", lambda e: e.activation(out=rscale_ap, in_=t2, func=AF.Exp, scale=float(-LN2), bias=cst66[:, 1:2]).then_inc(ss.h))
        return ss.inc()

    def emit_round(sch, v_ap, m_ap, b_ap, r_ap):
        if rounding == "magic":
            sch.em("vector", lambda e: e.tensor_scalar(m_ap, v_ap, MAGIC, MAGIC, op0=ALU.add, op1=ALU.subtract))
        else:
            sch.em("vector", lambda e: e.tensor_scalar(r_ap, v_ap, 16.0, None, op0=ALU.add))
            for stp in (16.0, 8.0, 4.0, 2.0, 1.0):
                sch.em("vector", lambda e, s=stp: e.tensor_scalar(b_ap, r_ap, s, None, op0=ALU.is_ge))
                sch.em("vector", lambda e, s=stp: e.scalar_tensor_tensor(r_ap, b_ap, -s, r_ap, op0=ALU.mult, op1=ALU.add))
            sch.em("vector", lambda e: e.tensor_scalar(b_ap, r_ap, 0.5, None, op0=ALU.is_ge))
            sch.em("vector", lambda e: e.tensor_tensor(m_ap, v_ap, r_ap, op=ALU.subtract))
            sch.em("vector", lambda e: e.tensor_tensor(m_ap, m_ap, b_ap, op=ALU.add))

    def emit_qelem(sch, x_ap, out_ap, rscale_ap, scale_ap, FD, scr7, mfrac, mxn, dbg=False):
        y, a, g2, g4, t1, w1, w2 = scr7
        nb = FD // 32
        x3 = x_ap.rearrange("p (a b) -> p a b", b=32)
        y3 = y.rearrange("p (a b) -> p a b", b=32)
        rs3 = rscale_ap.rearrange("p (a o) -> p a o", o=1).broadcast_to([128, nb, 32])
        sc3 = scale_ap.rearrange("p (a o) -> p a o", o=1).broadcast_to([128, nb, 32])
        sch.em("vector", lambda e: e.tensor_tensor(y3, x3, rs3, op=ALU.mult).then_inc(sv.h))
        sch.wait("scalar", sv, sv.inc())
        sch.em("scalar", lambda e: e.activation(out=a, in_=y, func=AF.Abs).then_inc(ss.h))
        sch.wait("vector", ss, ss.inc())
        sch.em("vector", lambda e: e.tensor_scalar(g2, a, 2.0, None, op0=ALU.is_ge))
        sch.em("vector", lambda e: e.tensor_scalar(g4, a, 4.0, None, op0=ALU.is_ge))
        if dbg:
            dump(sch, "qe_y", y)
            dump(sch, "qe_a", a)
            dump(sch, "qe_g2", g2)
            sch.wait("vector", sg, sg.n)
        sch.em("vector", lambda e: e.scalar_tensor_tensor(t1, g2, 2.0, g4, op0=ALU.mult, op1=ALU.add))
        k1, base = (2.0, 8.0) if mfrac == 3 else (0.5, 2.0)
        sch.em("vector", lambda e: e.tensor_scalar(w1, t1, -k1, base, op0=ALU.mult, op1=ALU.add))
        sch.em("vector", lambda e: e.tensor_tensor(w2, y, w1, op=ALU.mult))
        if dbg:
            dump(sch, "qe_w1", w1)
            dump(sch, "qe_w2", w2)
            sch.wait("vector", sg, sg.n)
        emit_round(sch, w2, a, t1, w1)
        if dbg:
            dump(sch, "qe_rnd", a)
            sch.wait("vector", sg, sg.n)
        sch.em("vector", lambda e: e.scalar_tensor_tensor(t1, g4, 2.0, g2, op0=ALU.mult, op1=ALU.add))
        s1 = 0.125 if mfrac == 3 else 0.5
        sch.em("vector", lambda e: e.tensor_scalar(w1, t1, s1, s1, op0=ALU.mult, op1=ALU.add))
        sch.em("vector", lambda e: e.tensor_tensor(w2, a, w1, op=ALU.mult))
        sch.em("vector", lambda e: e.tensor_scalar(w2, w2, float(mxn), float(-mxn), op0=ALU.min, op1=ALU.max))
        o3 = out_ap.rearrange("p (a b) -> p a b", b=32)
        w23 = w2.rearrange("p (a b) -> p a b", b=32)
        sch.em("vector", lambda e: e.tensor_tensor(o3, w23, sc3, op=ALU.mult).then_inc(sv.h))
        return sv.inc()

    def emit_bcast_row(sch, row_dram_ap, dst_ap):
        sch.em("gpsimd", lambda e: e.dma_start(out=mm_row, in_=row_dram_ap).then_inc(sg.h, 16))
        sch.wait("tensor", sg, sg.inc(16))
        for n4 in range(4):
            def mmf(e, n=n4):
                return e.matmul(PS[7][:, 0:512], ones1_sb, mm_row[0:1, n * 512:(n + 1) * 512], start=True, stop=True)
            w = mm_group(sch, [mmf])
            sch.wait("vector", sm, w)
            sch.em("vector", lambda e, n=n4: e.tensor_copy(dst_ap[:, n * 512:(n + 1) * 512], PS[7][:, 0:512]).then_inc(sv.h))
            sch.wait("tensor", sv, sv.inc())
        sch.wait("gpsimd", sm, sm.n)

    def emit_ln(sch, x_ap, xsq_ap, st_ap, out_fn):
        mr, m, s2r, s2m, mm2, var, veps, sq_, rstd = (st_ap[:, i:i + 1] for i in range(9))
        sch.em("vector", lambda e: e.reduce_sum(mr, x_ap, axis=AXX).then_inc(sv.h))
        sch.wait("vector", sv, sv.inc())
        sch.em("vector", lambda e: e.tensor_scalar(m, mr, float(1.0 / D), None, op0=ALU.mult).then_inc(sv.h))
        sch.wait("scalar", sv, sv.inc())
        sch.em("scalar", lambda e: e.activation(out=xsq_ap, in_=x_ap, func=AF.Square).then_inc(ss.h))
        sch.wait("vector", ss, ss.inc())
        sch.em("vector", lambda e: e.reduce_sum(s2r, xsq_ap, axis=AXX).then_inc(sv.h))
        sch.wait("vector", sv, sv.inc())
        sch.em("vector", lambda e: e.tensor_scalar(s2m, s2r, float(1.0 / D), None, op0=ALU.mult))
        sch.em("vector", lambda e: e.tensor_tensor(mm2, m, m, op=ALU.mult).then_inc(sv.h))
        sch.wait("vector", sv, sv.inc())
        sch.em("vector", lambda e: e.tensor_tensor(var, s2m, mm2, op=ALU.subtract).then_inc(sv.h))
        sch.wait("vector", sv, sv.inc())
        sch.em("vector", lambda e: e.tensor_scalar(veps, var, float(LN_EPS), None, op0=ALU.add).then_inc(sv.h))
        sch.wait("scalar", sv, sv.inc())
        sch.em("scalar", lambda e: e.activation(out=sq_, in_=veps, func=AF.Sqrt).then_inc(ss.h))
        sch.wait("vector", ss, ss.inc())
        sch.em("vector", lambda e: e.reciprocal(rstd, sq_).then_inc(sv.h))
        sch.wait("vector", sv, sv.inc())
        out_fn(m, rstd)

    def dump(sch, name, src_ap):
        h = dbg_handles.get(name)
        if h is None:
            return
        sch.wait("gpsimd", sv, sv.n)
        sch.wait("gpsimd", ss, ss.n)
        sch.wait("gpsimd", sm, sm.n)
        sch.em("gpsimd", lambda e: e.dma_start(out=h, in_=src_ap).then_inc(sg.h, 16))
        sch.wait("gpsimd", sg, sg.inc(16))

    sch = Sched()
    AF32.push(); ABF.push()
    cs_sb = AF32.take(DK)
    scT_sb = AF32.take(DK)
    gate_t = AF32.take(D)[0:1, :]
    modv = AF32.take(96)
    modt = AF32.take(128)
    wmt = [AF32.take(12 * 128) for _ in range(2)]
    zt = ABF.take(2048)
    for dst, src in ((id_sb, ident_bf), (idf_sb, ident_f), (onesb_sb, ones_bf),
                     (mask_sb, maskc), (ones1_sb, ones1), (utab_sb, utab), (cs_sb, cvT)):
        sch.em("sync", lambda e, d=dst, s=src: e.dma_start(out=d, in_=s).then_inc(sl.h, 16))
        sl.inc(16)
    for tc in range(8):
        sch.em("sync", lambda e, t=tc: e.dma_start(out=cos_sb[:, t * 64:(t + 1) * 64], in_=cosq[t * 128:(t + 1) * 128, :]).then_inc(sl.h, 16)); sl.inc(16)
        sch.em("sync", lambda e, t=tc: e.dma_start(out=sin_sb[:, t * 64:(t + 1) * 64], in_=sinq[t * 128:(t + 1) * 128, :]).then_inc(sl.h, 16)); sl.inc(16)
    all_loaded = sl.n
    sch.em("vector", lambda e: e.memset(cst66[:, 0:1], float(-66.0 * LN2)))
    sch.em("vector", lambda e: e.memset(cst66[:, 1:2], float(66.0 * LN2)))
    sch.em("vector", lambda e: e.memset(zt, 0.0).then_inc(sv.h))
    sch.wait("gpsimd", sv, sv.inc())
    for zchunk in range(SLABE // (128 * 2048)):
        sch.em("gpsimd", lambda e, z=zchunk: e.dma_start(
            out=kv_all[16 * SLABE + z * 128 * 2048: 16 * SLABE + (z + 1) * 128 * 2048].rearrange("(p f) -> p f", p=128),
            in_=zt).then_inc(sg.h, 16)); sg.inc(16)
    sch.wait("scalar", sl, all_loaded)
    sch.em("scalar", lambda e: e.activation(out=scT_sb, in_=cs_sb, func=AF.Silu).then_inc(ss.h))
    sch.wait("tensor", ss, ss.inc())
    nchunk = NCH // 12
    for ck in range(nchunk):
        for dk in range(DK):
            buf = wmt[dk % 2]
            if ck > 0 or dk >= 2:
                sch.wait("sync", sm, sm.n)
            sch.em("sync", lambda e, b=buf, d=dk, c=ck: e.dma_start(
                out=b, in_=wmod[d * 128:(d + 1) * 128, c * 1536:(c + 1) * 1536]).then_inc(sl.h, 16))
            sch.wait("tensor", sl, sl.inc(16))
            fns = []
            for ch in range(12):
                def mmf(e, b=buf, c=ch, d=dk, k=ck):
                    return e.matmul(PS[5][:, k * 12 + c:k * 12 + c + 1] if local else PS[5][:, c:c + 1],
                                    b[:, c * 128:(c + 1) * 128], scT_sb[:, d:d + 1],
                                    start=(k == 0 and d == 0 and c == 0),
                                    stop=(k == nchunk - 1 and d == DK - 1 and c == 11))
                fns.append(mmf)
            mm_group(sch, fns)
    sch.wait("vector", sm, sm.n)
    sch.em("vector", lambda e: e.tensor_copy(modv[:, 0:NCH], PS[5][:, 0:NCH]).then_inc(sv.h))
    sch.wait("tensor", sv, sv.inc())
    wtr = mm_group(sch, [lambda e: e.matmul(PS[6][0:NCH, 0:128], modv[:, 0:NCH], idf_sb, start=True, stop=True, is_transpose=True)])
    sch.wait("vector", sm, wtr)
    sch.em("vector", lambda e: e.tensor_copy(modt[0:NCH, 0:128], PS[6][0:NCH, 0:128]).then_inc(sv.h))
    sch.wait("gpsimd", sv, sv.inc())
    sch.em("gpsimd", lambda e: e.dma_start(out=mod_in[0:NCH, :], in_=modt[0:NCH, 0:128]).then_inc(sg.h, 16))
    sg.inc(16)
    if not local:
        sch.wait("gpsimd", sg, sg.n)
        sch.em("gpsimd", lambda e: e.collective_compute(
            "AllGather", ALU.bypass, replica_groups=RG,
            ins=[mod_in.opt()], outs=[mod_all.opt()]).then_inc(scc.h))
        scc.inc()
        sch.wait("gpsimd", scc, scc.n)
    sch.em("gpsimd", lambda e: e.dma_start(out=modt[0:96, 0:128], in_=mod_all if local else mod_all.rearrange("a b c -> (a b) c")).then_inc(sg.h, 16))
    sch.wait("tensor", sg, sg.inc(16))
    wtr = mm_group(sch, [lambda e: e.matmul(PS[6][:, 0:96], modt[0:96, 0:128], idf_sb[0:96, 0:96], start=True, stop=True, is_transpose=True)])
    sch.wait("vector", sm, wtr)
    sch.em("vector", lambda e: e.tensor_copy(mod_sb, PS[6][:, 0:96]).then_inc(sv.h))
    sv.inc()
    sch.wait("vector", sg, sg.n)
    sch.em("vector", lambda e: e.tensor_scalar(ca1_sb, mod_sb[:, 16:32], 1.0, None, op0=ALU.add))
    sch.em("vector", lambda e: e.tensor_scalar(cb1_sb, mod_sb[:, 64:80], 1.0, None, op0=ALU.add).then_inc(sv.h))
    sv.inc()
    scr_row = mm_row[0:1, :]
    for gn, lo in (("ga", 4096), ("gb", 10240)):
        sch.em("gpsimd", lambda e, q=lo: e.dma_start(out=scr_row, in_=mod_flat[q:q + D].rearrange("(o d) -> o d", o=1)).then_inc(sg.h, 16))
        sch.wait("vector", sg, sg.inc(16))
        sch.em("vector", lambda e: e.tensor_scalar(gate_t, scr_row, 0.0, None, op0=ALU.is_ge))
        sch.em("vector", lambda e: e.tensor_scalar(gate_t, gate_t, 2.0, -1.0, op0=ALU.mult, op1=ALU.add).then_inc(sv.h))
        sch.wait("scalar", sv, sv.inc())
        sch.em("scalar", lambda e: e.activation(out=scr_row, in_=scr_row, func=AF.Abs).then_inc(ss.h))
        sch.wait("vector", ss, ss.inc())
        sch.em("vector", lambda e: e.tensor_scalar(scr_row, scr_row, 0.25, None, op0=ALU.max))
        sch.em("vector", lambda e: e.tensor_tensor(scr_row, scr_row, gate_t, op=ALU.mult).then_inc(sv.h))
        sch.wait("gpsimd", sv, sv.inc())
        sch.em("gpsimd", lambda e, n=gn: e.dma_start(out=rows_dram[RIDX[n], :].rearrange("(o d) -> o d", o=1), in_=scr_row).then_inc(sg.h, 16))
        sch.wait("vector", sg, sg.inc(16))
    sch.em("gpsimd", lambda e: e.dma_start(out=scr_row, in_=mod_flat[8192:8192 + D].rearrange("(o d) -> o d", o=1)).then_inc(sg.h, 16))
    sch.wait("vector", sg, sg.inc(16))
    sch.em("vector", lambda e: e.tensor_scalar(scr_row, scr_row, 1.0, None, op0=ALU.add).then_inc(sv.h))
    sch.wait("gpsimd", sv, sv.inc())
    sch.em("gpsimd", lambda e: e.dma_start(out=rows_dram[RIDX["cb1"], :].rearrange("(o d) -> o d", o=1), in_=scr_row).then_inc(sg.h, 16))
    sch.wait("vector", sg, sg.inc(16))
    dump(sch, "mod", mod_sb)
    AF32.pop(); ABF.pop()
    sch.run_block(nc)
    nc.all_engine_barrier()
    if stop_after == "B0":
        return nc, dbg_handles

    sch = Sched()
    AF32.push(); ABF.push()
    wt_f = [AF32.take(D) for _ in range(2)]
    wt_b = ABF.take(D)
    wtf_reader = [[0, 0], [0, 0]]
    for wi in range(4):
        brow = ("qb", "kb", "vb", None)[wi]
        for dk in range(DK):
            buf = wt_f[dk % 2]
            if wtf_reader[dk % 2][0] or wtf_reader[dk % 2][1]:
                sch.wait("sync", sv, wtf_reader[dk % 2][0])
                sch.wait("sync", sm, wtf_reader[dk % 2][1])
            sch.em("sync", lambda e, b=buf, d=dk, w=wi: e.dma_start(out=b, in_=wT_in[w][d * 128:(d + 1) * 128, :]).then_inc(sl.h, 16))
            lw = sl.inc(16)
            if brow is not None:
                sch.wait("tensor", sl, lw)
                if dk == 0:
                    sch.wait("tensor", sv, sv.n)
                fns = []
                for n4 in range(4):
                    def mmf(e, b=buf, n=n4, d=dk):
                        return e.matmul(PS[4 + n][0:1, 0:512], mod_sb[:, d:d + 1],
                                        b[:, n * 512:(n + 1) * 512], start=(d == 0), stop=(d == DK - 1))
                    fns.append(mmf)
                _bias_t = mm_group(sch, fns)
            else:
                _bias_t = 0
            sch.wait("vector", sl, lw)
            if wi < 3:
                sch.em("vector", lambda e, b=buf, d=dk: e.tensor_scalar(wt_b, b, ca1_sb[:, d:d + 1], None, op0=ALU.mult).then_inc(sv.h))
            else:
                sch.em("vector", lambda e, b=buf: e.tensor_copy(wt_b, b).then_inc(sv.h))
            _sc_t = sv.inc()
            wtf_reader[dk % 2] = [_sc_t, _bias_t]
            sch.wait("gpsimd", sv, _sc_t)
            sch.em("gpsimd", lambda e, w=wi, d=dk: e.dma_start(out=wq_attn[w, d * 128:(d + 1) * 128, :], in_=wt_b).then_inc(sg.h, 16))
            sg.inc(16)
            sch.wait("vector", sg, sg.n)
        if brow is not None:
            sch.wait("vector", sm, sm.n)
            for n4 in range(4):
                sch.em("vector", lambda e, n=n4: e.tensor_copy(mm_row[0:1, n * 512:(n + 1) * 512], PS[4 + n][0:1, 0:512]))
            sch.em("vector", lambda e: e.tensor_scalar(mm_row[0:1, 0:1], mm_row[0:1, 0:1], 1.0, None, op0=ALU.mult).then_inc(sv.h))
            sch.wait("gpsimd", sv, sv.inc())
            sch.em("gpsimd", lambda e, bn=brow: e.dma_start(out=rows_dram[RIDX[bn], :].rearrange("(o d) -> o d", o=1), in_=mm_row).then_inc(sg.h, 16))
            sch.wait("tensor", sg, sg.inc(16))
            sch.wait("vector", sg, sg.n)
    if "wqa" in dbg_handles:
        sch.wait("gpsimd", sg, sg.n)
        sch.em("gpsimd", lambda e: e.dma_start(out=wt_b, in_=wq_attn[0, 0:128, :]).then_inc(sg.h, 16))
        sch.wait("gpsimd", sg, sg.inc(16))
        sch.em("gpsimd", lambda e: e.dma_start(out=dbg_handles["wqa"], in_=wt_b).then_inc(sg.h, 16))
        sch.wait("gpsimd", sg, sg.inc(16))
    if "wqfull" in dbg_handles:
        sch.wait("gpsimd", sg, sg.n)
        sch.em("gpsimd", lambda e: e.dma_start(out=dbg_handles["wqfull"], in_=wq_attn[0]).then_inc(sg.h, 16))
        sch.wait("gpsimd", sg, sg.inc(16))
    AF32.pop(); ABF.pop()
    sch.run_block(nc)
    nc.all_engine_barrier()
    if stop_after == "B1":
        return nc, dbg_handles

    sch = Sched()
    AF32.push(); ABF.push()
    x_tc = [AF32.take(D) for _ in range(2)]
    xsq = AF32.take(D)
    st_t = AF32.take(16)
    qtm = AF32.take(D)
    ropetmp = AF32.take(512)
    lnf = AF32.take(D)
    ropef = AF32.take(512)
    qb_b = AF32.take(D)
    kb_b = AF32.take(D)
    vb_b = AF32.take(D)
    snap_lnf = AF32.take(D) if "lnf" in dbg_handles else None
    snap_st = AF32.take(16) if "lnf" in dbg_handles else None
    lnxT = attnT_sb[:, 0:DK * S]
    vbf = attnT_sb[:, DK * S:DK * S + 2048]
    _o = DK * S + 2048
    wstr = [attnT_sb[:, _o:_o + 512], attnT_sb[:, _o + 512:_o + 1024]]
    kstage = attnT_sb[:, _o + 1024:_o + 1536]
    emit_bcast_row(sch, rows_dram[RIDX["qb"], :].rearrange("(o d) -> o d", o=1), qb_b)
    emit_bcast_row(sch, rows_dram[RIDX["kb"], :].rearrange("(o d) -> o d", o=1), kb_b)
    emit_bcast_row(sch, rows_dram[RIDX["vb"], :].rearrange("(o d) -> o d", o=1), vb_b)

    _b2slots = 1 if stop_after in ("B2ln", "B2q", "B2s0") else 2
    wstr_reader = [0, 0]
    for slot in range(_b2slots):
        xin = xa if slot == 0 else xb
        for tc in range(4):
            buf = x_tc[tc % 2]
            if tc >= 2 or slot > 0:
                sch.wait("sync", ss, ss.n)
                sch.wait("sync", sv, sv.n)
            sch.em("sync", lambda e, b=buf, t=tc, xi=xin: e.dma_start(out=b, in_=xi[t * 128:(t + 1) * 128, :]).then_inc(sl.h, 16))
            sch.wait("vector", sl, sl.inc(16))
            lnb = lnf

            def norm_out(m, rstd, b=buf, o=lnb):
                sch.em("vector", lambda e: e.tensor_scalar(o, b, m, rstd, op0=ALU.subtract, op1=ALU.mult).then_inc(sv.h))
                sv.inc()
            emit_ln(sch, buf, xsq, st_t, norm_out)
            if slot == 0 and tc == 0 and "lnf" in dbg_handles:
                sch.em("vector", lambda e: e.tensor_copy(snap_lnf, lnf))
                sch.em("vector", lambda e: e.tensor_copy(snap_st, st_t).then_inc(sv.h))
                sv.inc()
            if slot == 0 and tc == 0:
                dump(sch, "ln0", xsq)
            wvn = sv.n
            sch.wait("tensor", sv, wvn)
            for dc in range(DK):
                pb = PS[dc % 2]
                if dc >= 2:
                    sch.wait("tensor", sv, sv.n)
                w = mm_group(sch, [lambda e, p=pb, d=dc, o=lnb: e.matmul(
                    p[:, 0:128], o[:, d * 128:(d + 1) * 128], idf_sb, start=True, stop=True, is_transpose=True)])
                sch.wait("vector", sm, w)
                sch.em("vector", lambda e, p=pb, d=dc, t=tc: e.tensor_copy(
                    lnxT[:, d * S + t * 128:d * S + (t + 1) * 128], p[:, 0:128].bitcast(BF16) if False else p[:, 0:128]).then_inc(sv.h))
                sv.inc()
        sch.wait("tensor", sv, sv.n)
        if slot == 0:
            dump(sch, "lnx", lnxT[:, 0:2048])
            dump(sch, "lnxf", lnxT[:, 0:8192])
            if "lnf" in dbg_handles:
                dump(sch, "lnf", snap_lnf)
                dump(sch, "st", snap_st[:, 0:16])
        if stop_after == "B2ln":
            break
        _projs = ((0, False),) if stop_after == "B2q" else ((0, False), (1, False), (2, True))
        ps_guard = sv.n
        for pi, (wix, is_v) in enumerate(_projs):
            bb = (qb_b, kb_b, vb_b)[pi]
            for n4 in range(4):
                for dk in range(DK):
                    buf = wstr[dk % 2]
                    if wstr_reader[dk % 2]:
                        sch.wait("sync", sm, wstr_reader[dk % 2])
                    sch.em("sync", lambda e, b=buf, w=wix, d=dk, n=n4: e.dma_start(
                        out=b, in_=wq_attn[w, d * 128:(d + 1) * 128, n * 512:(n + 1) * 512]).then_inc(sl.h, 16))
                    sch.wait("tensor", sl, sl.inc(16))
                    if dk == 0:
                        sch.wait("tensor", sv, ps_guard)
                    fns = []
                    for tc in range(4):
                        def mmf(e, b=buf, d=dk, t=tc):
                            return e.matmul(PS[t][:, 0:512], lnxT[:, d * S + t * 128:d * S + (t + 1) * 128], b,
                                            start=(d == 0), stop=(d == DK - 1))
                        fns.append(mmf)
                    wstr_reader[dk % 2] = mm_group(sch, fns)
                mw = sm.n
                sch.wait("vector", sm, mw)
                for tc in range(4):
                    if is_v:
                        sch.em("vector", lambda e, t=tc, n=n4, b=bb: e.tensor_tensor(
                            vbf[:, t * 512:(t + 1) * 512], PS[t][:, 0:512], b[:, n * 512:(n + 1) * 512], op=ALU.add).then_inc(sv.h))
                        sch.wait("gpsimd", sv, sv.inc())
                        for j in range(4):
                            _vt = ((n4 * 4 + j) * 4 + tc)
                            sch.em("gpsimd", lambda e, sl_=slot, a=_vt, t=tc, n=n4, jj=j: e.dma_start(
                                out=kv_in[sl_, KPART + a * 16384:KPART + (a + 1) * 16384].rearrange("(p f) -> p f", p=128),
                                in_=vbf[:, t * 512 + jj * 128:t * 512 + (jj + 1) * 128]).then_inc(sg.h, 16))
                            sg.inc(16)
                        sch.wait("vector", sg, sg.n)
                    else:
                        if pi == 0 and n4 == 0 and tc == 0 and slot == 0 and "psq0" in dbg_handles:
                            sch.em("vector", lambda e: e.tensor_copy(ropetmp[:, 0:512], PS[0][:, 0:512]).then_inc(sv.h))
                            sv.inc()
                            dump(sch, "psq0", ropetmp[:, 0:512])
                            dump(sch, "qbb", bb)
                            sch.wait("vector", sg, sg.n)
                        sch.em("vector", lambda e, t=tc, n=n4, b=bb: e.tensor_tensor(
                            qtm[:, 0:512], PS[t][:, 0:512], b[:, n * 512:(n + 1) * 512], op=ALU.add).then_inc(sv.h))
                        sv.inc()
                        if pi == 0 and n4 == 0 and tc == 0 and slot == 0 and "qtm0" in dbg_handles:
                            dump(sch, "qtm0", qtm[:, 0:512])
                            dump(sch, "cos0", cos_sb[:, 0:64])
                            sch.wait("vector", sg, sg.n)
                        co = cos_sb[:, (slot * 4 + tc) * 64:(slot * 4 + tc + 1) * 64]
                        si = sin_sb[:, (slot * 4 + tc) * 64:(slot * 4 + tc + 1) * 64]
                        co4 = co.rearrange("p (o j) -> p o j", o=1).broadcast_to([128, 4, 64])
                        si4 = si.rearrange("p (o j) -> p o j", o=1).broadcast_to([128, 4, 64])
                        q4 = qtm[:, 0:512].rearrange("p (h j) -> p h j", j=128)
                        qlo, qhi = q4[:, :, 0:64], q4[:, :, 64:128]
                        tlo = ropetmp[:, 0:256].rearrange("p (h j) -> p h j", j=64)
                        thi = ropetmp[:, 256:512].rearrange("p (h j) -> p h j", j=64)
                        o4 = ropef[:, 0:512].rearrange("p (h j) -> p h j", j=128)
                        olo, ohi = o4[:, :, 0:64], o4[:, :, 64:128]
                        sch.em("vector", lambda e, a=tlo, x1=qlo, c=co4: e.tensor_tensor(a, x1, c, op=ALU.mult))
                        sch.em("vector", lambda e, a=thi, x2=qhi, s_=si4: e.tensor_tensor(a, x2, s_, op=ALU.mult))
                        sch.em("vector", lambda e, o=olo, a=tlo, b2=thi: e.tensor_tensor(o, a, b2, op=ALU.subtract))
                        sch.em("vector", lambda e, a=tlo, x1=qlo, s_=si4: e.tensor_tensor(a, x1, s_, op=ALU.mult))
                        sch.em("vector", lambda e, a=thi, x2=qhi, c=co4: e.tensor_tensor(a, x2, c, op=ALU.mult))
                        sch.em("vector", lambda e, o=ohi, a=tlo, b2=thi: e.tensor_tensor(o, a, b2, op=ALU.add).then_inc(sv.h))
                        sch.wait("tensor", sv, sv.inc())
                        if pi == 0 and n4 == 0 and tc == 0 and slot == 0 and "rope0" in dbg_handles:
                            dump(sch, "rope0", ropef[:, 0:512])
                            sch.wait("vector", sg, sg.n)
                        for hh in range(4):
                            h = n4 * 4 + hh
                            pb = PS[4 + hh % 2]
                            if hh >= 2:
                                sch.wait("tensor", sv, sv.n)
                            w = mm_group(sch, [lambda e, p=pb, q=hh: e.matmul(
                                p[:, 0:128], ropef[:, q * 128:(q + 1) * 128], idf_sb, start=True, stop=True, is_transpose=True)])
                            sch.wait("vector", sm, w)
                            if pi == 0:
                                dst = qT_sb[:, h * T + slot * S + tc * 128: h * T + slot * S + (tc + 1) * 128]
                                sch.em("vector", lambda e, p=pb, d=dst: e.tensor_copy(d, p[:, 0:128]).then_inc(sv.h))
                                sv.inc()
                            else:
                                sch.em("vector", lambda e, p=pb: e.tensor_copy(kstage[:, 0:128], p[:, 0:128]).then_inc(sv.h))
                                sch.wait("gpsimd", sv, sv.inc())
                                sch.em("gpsimd", lambda e, sl_=slot, hx=h, t=tc: e.dma_start(
                                    out=kvi_k[sl_, hx, :, t * 128:(t + 1) * 128],
                                    in_=kstage[:, 0:128]).then_inc(sg.h, 16))
                                sg.inc(16)
                                sch.wait("vector", sg, sg.n)
                ps_guard = sv.n
    sch.wait("gpsimd", sg, sg.n)
    sch.wait("gpsimd", sv, sv.n)
    if stop_after == "B2q":
        dump(sch, "lnxf2", lnxT[:, 0:8192])
    if not local and not noag and stop_after not in ("B2ln", "B2q", "B2s0", "B2noag"):
        sch.em("gpsimd", lambda e: e.collective_compute(
            "AllGather", ALU.bypass, replica_groups=RG,
            ins=[kvi_flat.opt()], outs=[kv_all[0:16 * SLABE].opt()]).then_inc(scc.h))
        scc.inc()
    dump(sch, "q0", qT_sb[:, 0:2048])
    AF32.pop(); ABF.pop()
    sch.run_block(nc)
    nc.all_engine_barrier()
    if stop_after is not None and stop_after.startswith("B2"):
        return nc, dbg_handles

    sch = Sched()
    if not local and not noag:
        sch.wait("scalar", scc, scc.n)
    AF32.push(); ABF.push()
    wrow = [AF32.take(D) for _ in range(2)]
    qscr = [AF32.take(D) for _ in range(7)]
    am = AF32.take(64)
    sscr = tuple(AF32.take(64) for _ in range(3))
    sc_t = AF32.take(64)
    rs_t = AF32.take(64)
    qout = AF32.take(D)
    tstage = ABF.take(DK * 128)
    ps_reader = [0, 0]

    wrow_reader = [0, 0]
    wrow_lt = {}

    def quant_matrix(src, dst, nrows, ncols, dbg=False):
        nb_ = ncols // 32
        ndc = ncols // 128
        nrt = nrows // 128

        def issue_load(rt):
            b = wrow[rt % 2]
            if wrow_reader[rt % 2]:
                sch.wait("scalar", sv, wrow_reader[rt % 2])
            sch.em("scalar", lambda e, b=b, r=rt, s=src, c=ncols: e.dma_start(out=b[:, 0:c], in_=s[r * 128:(r + 1) * 128, :]).then_inc(sl.h, 16))
            wrow_lt[rt] = sl.inc(16)

        issue_load(0)
        for rt in range(nrt):
            buf = wrow[rt % 2]
            if rt + 1 < nrt:
                issue_load(rt + 1)
            sch.wait("vector", sl, wrow_lt[rt])
            a3 = buf[:, 0:ncols].rearrange("p (a b) -> p a b", b=32)
            sch.em("vector", lambda e, a=a3, c=nb_: e.reduce_max(am[:, 0:c], a, axis=AXX, apply_absolute_value=True).then_inc(sv.h))
            sch.wait("vector", sv, sv.inc())
            emit_scales(sch, am[:, 0:nb_], tuple(s[:, 0:nb_] for s in sscr), sc_t[:, 0:nb_], rs_t[:, 0:nb_])
            sch.wait("vector", ss, ss.n)
            emit_qelem(sch, buf[:, 0:ncols], qout[:, 0:ncols], rs_t[:, 0:nb_], sc_t[:, 0:nb_], ncols, [q[:, 0:ncols] for q in qscr], 1, 6.0,
                       dbg=(dbg and rt == 0 and "qe_y" in dbg_handles))
            wrow_reader[rt % 2] = sv.n
            if dbg and rt == 0 and "qe_qout" in dbg_handles:
                dump(sch, "qe_qout", qout)
                sch.wait("vector", sg, sg.n)
                sch.wait("tensor", sg, sg.n)
            sch.wait("tensor", sv, sv.n)
            for dc in range(ndc):
                pb = PS[dc % 2]
                if ps_reader[dc % 2]:
                    sch.wait("tensor", sv, ps_reader[dc % 2])
                w = mm_group(sch, [lambda e, p=pb, d=dc: e.matmul(
                    p[:, 0:128], qout[:, d * 128:(d + 1) * 128], idf_sb, start=True, stop=True, is_transpose=True)])
                sch.wait("vector", sm, w)
                sch.em("vector", lambda e, p=pb, d=dc: e.tensor_copy(tstage[:, d * 128:(d + 1) * 128], p[:, 0:128]).then_inc(sv.h))
                ps_reader[dc % 2] = sv.inc()
            sch.wait("gpsimd", sv, sv.n)
            if dbg and rt == 0:
                dump(sch, "wq0", tstage[:, 0:2048])
            for dc in range(ndc):
                sch.em("gpsimd", lambda e, d=dc, r=rt, ds=dst: e.dma_start(
                    out=ds[d * 128:(d + 1) * 128, r * 128:(r + 1) * 128],
                    in_=tstage[:, d * 128:(d + 1) * 128]).then_inc(sg.h, 16))
                sg.inc(16)
            sch.wait("vector", sg, sg.n)

    quant_matrix(wg_sl, wgq_in, WR, D, dbg=True)
    quant_matrix(wu_sl, wuq_in, WR, D)
    quant_matrix(wd_sl, wdq_in, D, WDC)
    sch.wait("gpsimd", sg, sg.n)
    sch.wait("gpsimd", sv, sv.n)
    if not local:
        for src_, dst_ in ((wgq_in, wg_all), (wuq_in, wu_all), (wdq_in, wd_all)):
            sch.em("gpsimd", lambda e, s=src_, d=dst_: e.collective_compute(
                "AllGather", ALU.bypass, replica_groups=RG,
                ins=[s.opt()], outs=[d.opt()]).then_inc(scc.h))
            scc.inc()
    AF32.pop(); ABF.pop()
    sch.run_block(nc)
    nc.all_engine_barrier()
    if stop_after == "B3":
        return nc, dbg_handles

    kv_ready = scc.n if not local else 0
    sch = Sched()
    AF32.push(); ABF.push()
    s2_sb = [AF32.take(S) for _ in range(2)]
    bc_sb = AF32.take(S)
    kt_sb = [ABF.take(S) for _ in range(2)]
    vt_sb = [ABF.take(S) for _ in range(2)]
    s4_sb = [ABF.take(S) for _ in range(2)]
    if not local:
        sch.wait("gpsimd", scc, kv_ready)
    gather_base = sg.n
    gather_done = {}

    def gather(e):
        for u in range(24):
            if u == 0 or u == NU_A:
                e.dma_start(out=kv_seq[u * SLABE:(u + 1) * SLABE],
                            in_=kvi_flat[(0 if u == 0 else 1) * SLABE:(1 if u == 0 else 2) * SLABE]).then_inc(sg.h, 16)
            else:
                with e.register(f"g{u}") as r:
                    e.reg_load(r, utab_sb[0:1, u:u + 1])
                    rv = RuntimeValue(r, min_val=0, max_val=16 * SLABE)
                    e.dma_start(out=kv_seq[u * SLABE:(u + 1) * SLABE],
                                in_=kv_all[bass.DynSlice(rv, SLABE)]).then_inc(sg.h, 16)
    sch.em("gpsimd", gather)
    for u in range(24):
        gather_done[u] = gather_base + 16 * (u + 1)
    sg.inc(16 * 24)

    NUM, DEN = PS[2], PS[3]
    kv_reader = [0, 0]
    qk_bank = [0, 0]
    s2_bank = [0, 0]
    s4_bank = [0, 0]
    pending_v = []

    def flush_v(n):
        while len(pending_v) > n:
            s4t, vbuf, vtt, f_, l_, kt = pending_v.pop(0)
            if kt == 0:
                sch.wait("tensor", sg, vtt)
            sch.wait("tensor", sv, s4t)
            def vm(e, b=vbuf, k=kt, f=f_, l=l_):
                e.matmul(NUM[:, 0:512], vt_sb[b][:, k * 128:(k + 1) * 128], s4_sb[k % 2], start=f, stop=l)
                return e.matmul(DEN[0:1, 0:512], onesb_sb, s4_sb[k % 2], start=f, stop=l)
            s4_bank[kt % 2] = mm_group(sch, [vm])
            if kt == 3:
                kv_reader[vbuf] = s4_bank[1]

    for h in range(H):
        for slot in range(2):
            NU = NU_A if slot == 0 else NU_B
            q_rhs = qT_sb[:, h * T + slot * S: h * T + (slot + 1) * S]
            for u in range(NU):
                uid = u + (0 if slot == 0 else NU_A)
                if kv_reader[u % 2]:
                    sch.wait("gpsimd", sm, kv_reader[u % 2])
                sch.wait("gpsimd", sg, gather_done[uid])

                def kvload(e, i=uid, b=u % 2, hh=h):
                    kb_ = i * SLABE + hh * 65536
                    e.dma_start(out=kt_sb[b], in_=kv_seq[kb_:kb_ + 65536].rearrange("(p f) -> p f", p=128)).then_inc(sg.h, 16)
                    vb_ = i * SLABE + KPART + hh * 4 * 16384
                    e.dma_start(out=vt_sb[b][:, 0:512].rearrange("p (a f) -> p a f", f=128),
                                in_=kv_seq[vb_:vb_ + 65536].rearrange("(a p f) -> p a f", p=128, f=128)).then_inc(sg.h, 16)
                sch.em("gpsimd", kvload)
                kt_t = sg.inc(16)
                vt_t = sg.inc(16)
                sch.wait("tensor", sg, kt_t)
                diag = (u == 0)
                first = (u == 0)
                last = (u == NU - 1)

                for kt in range(4):
                    sb_ = PS[kt % 2]
                    if qk_bank[kt % 2]:
                        sch.wait("tensor", ss, qk_bank[kt % 2])
                    w = mm_group(sch, [lambda e, p=sb_, b=u % 2, k=kt, q=q_rhs: e.matmul(
                        p[:, 0:512], kt_sb[b][:, k * 128:(k + 1) * 128], q, start=True, stop=True)])
                    flush_v(1)
                    sch.wait("scalar", sm, w)
                    if s2_bank[kt % 2]:
                        sch.wait("scalar", sv, s2_bank[kt % 2])
                    sch.em("scalar", lambda e, p=sb_, k=kt: e.activation(out=s2_sb[k % 2], in_=p[:, 0:512], func=AF.Square).then_inc(ss.h))
                    qk_bank[kt % 2] = ss.inc()
                    sch.wait("vector", ss, qk_bank[kt % 2])
                    if s4_bank[kt % 2]:
                        sch.wait("vector", sm, s4_bank[kt % 2])
                    if diag:
                        sch.em("vector", lambda e, k=kt: e.tensor_tensor(s2_sb[k % 2], s2_sb[k % 2], mask_sb[:, k * S:(k + 1) * S], op=ALU.mult))
                    sch.em("vector", lambda e, k=kt: e.tensor_tensor(s4_sb[k % 2], s2_sb[k % 2], s2_sb[k % 2], op=ALU.mult).then_inc(sv.h))
                    s2_bank[kt % 2] = sv.inc()
                    pending_v.append((s2_bank[kt % 2], u % 2, vt_t, first and kt == 0, last and kt == 3, kt))
            flush_v(0)
            sch.wait("vector", sm, sm.n)
            sch.em("vector", lambda e: e.tensor_scalar(mm_row[0:1, 0:512], DEN[0:1, 0:512], float(ATT_EPS), None, op0=ALU.max).then_inc(sv.h))
            sch.wait("scalar", sv, sv.inc())
            sch.em("scalar", lambda e: e.activation(out=mm_row[0:1, 512:1024], in_=mm_row[0:1, 0:512], func=AF.Ln))
            sch.em("scalar", lambda e: e.activation(out=mm_row[0:1, 0:512], in_=mm_row[0:1, 512:1024], func=AF.Exp, scale=-1.0).then_inc(ss.h))
            sch.wait("tensor", ss, ss.inc())
            w = mm_group(sch, [lambda e: e.matmul(PS[4][:, 0:512], ones1_sb, mm_row[0:1, 0:512], start=True, stop=True)])
            sch.wait("vector", sm, w)
            sch.em("vector", lambda e: e.tensor_copy(bc_sb, PS[4][:, 0:512]))
            dst = attnT_sb[:, h * T + slot * S:(h * T + (slot + 1) * S)]
            sch.em("vector", lambda e, d=dst: e.tensor_tensor(d, NUM[:, 0:512], bc_sb, op=ALU.mult).then_inc(sv.h))
            sch.wait("tensor", sv, sv.inc())
            if h == 0 and slot == 0:
                dump(sch, "den0", bc_sb)
    dump(sch, "attn0", attnT_sb[:, 0:2048])
    AF32.pop(); ABF.pop()
    sch.run_block(nc)
    nc.all_engine_barrier()
    if stop_after == "B4":
        return nc, dbg_handles

    sch = Sched()
    AF32.push(); ABF.push()
    x_tc2 = [AF32.take(D) for _ in range(2)]
    x2t = AF32.take(D)
    epi = AF32.take(512)
    xsq2 = AF32.take(D)
    st2 = AF32.take(16)
    h2s = AF32.take(D)
    am2 = AF32.take(64)
    ss2 = tuple(AF32.take(64) for _ in range(3))
    sc2 = AF32.take(64)
    rs2 = AF32.take(64)
    qscr2 = [AF32.take(512) for _ in range(7)]
    ga_b = AF32.take(D)
    cb1_b = AF32.take(D)
    sb_b = AF32.take(D)
    wo_t = [ABF.take(D) for _ in range(2)]
    hq = AF32.take(D)
    emit_bcast_row(sch, rows_dram[RIDX["ga"], :].rearrange("(o d) -> o d", o=1), ga_b)
    emit_bcast_row(sch, rows_dram[RIDX["cb1"], :].rearrange("(o d) -> o d", o=1), cb1_b)
    emit_bcast_row(sch, mod_flat[6144:6144 + D].rearrange("(o d) -> o d", o=1), sb_b)

    wo_reader = [0, 0]
    xc = [AF32.take(512) for _ in range(2)]
    xc_reader = [0, 0]
    ps_free = 0
    for n4 in range(4):
        for dk in range(DK):
            wb = wo_t[dk % 2]
            if wo_reader[dk % 2]:
                sch.wait("sync", sm, wo_reader[dk % 2])
            sch.em("sync", lambda e, b=wb, d=dk, n=n4: e.dma_start(
                out=b[:, 0:512], in_=wq_attn[3, d * 128:(d + 1) * 128, n * 512:(n + 1) * 512]).then_inc(sl.h, 16))
            sch.wait("tensor", sl, sl.inc(16))
            if dk == 0:
                sch.wait("tensor", sv, ps_free)
            fns = []
            for t8 in range(8):
                def mmf(e, b=wb, t=t8, d=dk):
                    lh = attnT_sb[:, d * T + (t // 4) * S + (t % 4) * 128: d * T + (t // 4) * S + (t % 4 + 1) * 128]
                    return e.matmul(PS[t][:, 0:512], lh, b[:, 0:512], start=(d == 0), stop=(d == DK - 1))
                fns.append(mmf)
            wo_reader[dk % 2] = mm_group(sch, fns)
        sch.wait("vector", sm, sm.n)
        for t8 in range(8):
            slot, tc = t8 // 4, t8 % 4
            xin = xa if slot == 0 else xb
            xb_ = xc[t8 % 2]
            if xc_reader[t8 % 2]:
                sch.wait("sync", sv, xc_reader[t8 % 2])
            sch.em("sync", lambda e, b=xb_, xi=xin, t=tc, n=n4: e.dma_start(
                out=b, in_=xi[t * 128:(t + 1) * 128, n * 512:(n + 1) * 512]).then_inc(sl.h, 16))
            sch.wait("vector", sl, sl.inc(16))
            sch.em("vector", lambda e, t=t8, n=n4: e.tensor_tensor(epi, PS[t][:, 0:512], ga_b[:, n * 512:(n + 1) * 512], op=ALU.mult))
            sch.em("vector", lambda e, b=xb_: e.tensor_tensor(epi, epi, b, op=ALU.add).then_inc(sv.h))
            xc_reader[t8 % 2] = sv.inc()
            sch.wait("gpsimd", sv, xc_reader[t8 % 2])
            sch.em("gpsimd", lambda e, t=t8, n=n4: e.dma_start(
                out=x2_dram[t * 128:(t + 1) * 128, n * 512:(n + 1) * 512], in_=epi).then_inc(sg.h, 16))
            sg.inc(16)
            sch.wait("vector", sg, sg.n)
        ps_free = sv.n
    sch.wait("sync", sg, sg.n)
    for t8 in range(8):
        sch.wait("sync", sv, sv.n)
        sch.em("sync", lambda e, t=t8: e.dma_start(out=x2t, in_=x2_dram[t * 128:(t + 1) * 128, :]).then_inc(sl.h, 16))
        sch.wait("vector", sl, sl.inc(16))

        def norm2(m, rstd):
            sch.em("vector", lambda e: e.tensor_scalar(h2s, x2t, m, rstd, op0=ALU.subtract, op1=ALU.mult))
            sch.em("vector", lambda e: e.tensor_tensor(h2s, h2s, cb1_b, op=ALU.mult))
            sch.em("vector", lambda e: e.tensor_tensor(h2s, h2s, sb_b, op=ALU.add))
            a3 = h2s.rearrange("p (a b) -> p a b", b=32)
            sch.em("vector", lambda e, a=a3: e.reduce_max(am2, a, axis=AXX, apply_absolute_value=True).then_inc(sv.h))
            sch.wait("vector", sv, sv.inc())
        emit_ln(sch, x2t, xsq2, st2, norm2)
        emit_scales(sch, am2, ss2, sc2, rs2)
        sch.wait("vector", ss, ss.n)
        for n4 in range(4):
            emit_qelem(sch, h2s[:, n4 * 512:(n4 + 1) * 512], hq[:, n4 * 512:(n4 + 1) * 512],
                       rs2[:, n4 * 16:(n4 + 1) * 16], sc2[:, n4 * 16:(n4 + 1) * 16], 512, qscr2, 3, 7.5)
        sch.wait("tensor", sv, sv.n)
        for dc in range(DK):
            pb = PS[4 + dc % 2]
            if dc >= 2:
                sch.wait("tensor", sv, sv.n)
            w = mm_group(sch, [lambda e, p=pb, d=dc: e.matmul(
                p[:, 0:128], hq[:, d * 128:(d + 1) * 128], idf_sb, start=True, stop=True, is_transpose=True)])
            sch.wait("vector", sm, w)
            sch.em("vector", lambda e, p=pb, d=dc, t=t8: e.tensor_copy(
                hqT_sb[:, d * T + t * 128:d * T + (t + 1) * 128], p[:, 0:128]).then_inc(sv.h))
            sv.inc()
        sch.wait("tensor", sv, sv.n)
        if t8 == 0:
            dump(sch, "x2", x2t)
            dump(sch, "hq0", hq)
    AF32.pop(); ABF.pop()
    sch.run_block(nc)
    nc.all_engine_barrier()
    if stop_after == "B6":
        return nc, dbg_handles

    w_ready = scc.n
    sch = Sched()
    amx_all = AF32.take(2048)
    AF32.push(); ABF.push()
    sg_scr = AF32.take(512)
    int_scr = AF32.take(512)
    wgu = attnT_sb[:, 0:2 * DK * 512]
    if not local:
        sch.wait("sync", scc, w_ready)
    for ic in range(16):
        r, hf = ic // 2, ic % 2
        if ic > 0:
            sch.wait("sync", sm, sm.n)
        for d2 in range(DK):
            for gu in range(2):
                srcw = wg_all if gu == 0 else wu_all
                if local:
                    ap_ = srcw[d2 * 128:(d2 + 1) * 128, ic * 512:(ic + 1) * 512]
                else:
                    ap_ = srcw[r, d2 * 128:(d2 + 1) * 128, hf * 512:(hf + 1) * 512]
                sch.em("sync", lambda e, a=ap_, d=d2, g=gu: e.dma_start(
                    out=wgu[:, (g * DK + d) * 512:(g * DK + d + 1) * 512], in_=a).then_inc(sl.h, 16))
                sl.inc(16)
        sch.wait("tensor", sl, sl.n)
        for t8 in range(8):
            pg, pu = PS[t8 % 2], PS[2 + t8 % 2]
            if not (ic == 0 and t8 < 2):
                sch.wait("tensor", sv, sv.n)
            fns = []
            for d2 in range(DK):
                def mmf(e, d=d2, t=t8, a=pg, b=pu):
                    lh = hqT_sb[:, d * T + t * 128: d * T + (t + 1) * 128]
                    e.matmul(a[:, 0:512], lh, wgu[:, d * 512:(d + 1) * 512], start=(d == 0), stop=(d == DK - 1))
                    return e.matmul(b[:, 0:512], lh, wgu[:, (DK + d) * 512:(DK + d + 1) * 512], start=(d == 0), stop=(d == DK - 1))
                fns.append(mmf)
            w = mm_group(sch, fns)
            sch.wait("scalar", sm, w)
            sch.em("scalar", lambda e, a=pg: e.activation(out=sg_scr, in_=a[:, 0:512], func=AF.Silu).then_inc(ss.h))
            sch.wait("vector", ss, ss.inc())
            sch.em("vector", lambda e, b=pu: e.tensor_tensor(int_scr, sg_scr, b[:, 0:512], op=ALU.mult))
            i3 = int_scr.rearrange("p (a b) -> p a b", b=32)
            sch.em("vector", lambda e, a=i3, t=t8, i=ic: e.reduce_max(
                amx_all[:, t * 256 + i * 16: t * 256 + (i + 1) * 16], a, axis=AXX,
                apply_absolute_value=True).then_inc(sv.h))
            vw = sv.inc()
            sch.wait("gpsimd", sv, vw)
            sch.em("gpsimd", lambda e, t=t8, i=ic: e.dma_start(
                out=inter_dram[t * 128:(t + 1) * 128, i * 512:(i + 1) * 512], in_=int_scr).then_inc(sg.h, 16))
            sg.inc(16)
            sch.wait("vector", sg, sg.n)
    dump(sch, "amx", amx_all)
    AF32.pop(); ABF.pop()
    sch.run_block(nc)
    nc.all_engine_barrier()
    if stop_after == "B8":
        return nc, dbg_handles

    sch = Sched()
    AF32.push(); ABF.push()
    sc_all = AF32.take(2048)
    rs_all = AF32.take(2048)
    ssc = tuple(AF32.take(512) for _ in range(3))
    itile = [AF32.take(512) for _ in range(2)]
    qscr3 = [AF32.take(512) for _ in range(7)]
    gb_b = AF32.take(D)
    x2l = AF32.take(D)
    ot = AF32.take(D)
    iq = AF32.take(512)
    iqT = attnT_sb[:, 0:2 * II]
    wd_t = [ABF.take(D) for _ in range(2)]
    wd_reader = [0, 0]
    itile_reader = [0, 0]
    dbgffn = AF32.take(D) if "ffn0" in dbg_handles else None
    emit_bcast_row(sch, rows_dram[RIDX["gb"], :].rearrange("(o d) -> o d", o=1), gb_b)
    if "int4" in dbg_handles:
        sch.em("gpsimd", lambda e: e.dma_start(out=dbg_handles["int4"], in_=inter_dram[512:640, :]).then_inc(sg.h, 16))
        sch.wait("gpsimd", sg, sg.inc(16))
    for qc in range(4):
        emit_scales(sch, amx_all[:, qc * 512:(qc + 1) * 512], ssc,
                    sc_all[:, qc * 512:(qc + 1) * 512], rs_all[:, qc * 512:(qc + 1) * 512])
    sch.wait("vector", ss, ss.n)
    if "scall" in dbg_handles:
        dump(sch, "scall", sc_all)
        sch.wait("vector", sg, sg.n)
        sch.wait("scalar", sg, sg.n)
    for pair in range(4):
        for half in range(2):
            t8 = pair * 2 + half
            for ic in range(16):
                buf = itile[ic % 2]
                if itile_reader[ic % 2]:
                    sch.wait("sync", sv, itile_reader[ic % 2])
                sch.em("sync", lambda e, b=buf, t=t8, i=ic: e.dma_start(
                    out=b, in_=inter_dram[t * 128:(t + 1) * 128, i * 512:(i + 1) * 512]).then_inc(sl.h, 16))
                sch.wait("vector", sl, sl.inc(16))
                rsl = rs_all[:, t8 * 256 + ic * 16: t8 * 256 + (ic + 1) * 16]
                scl = sc_all[:, t8 * 256 + ic * 16: t8 * 256 + (ic + 1) * 16]
                emit_qelem(sch, buf, iq, rsl, scl, 512, qscr3, 3, 7.5)
                itile_reader[ic % 2] = sv.n
                if pair == 0 and half == 0 and ic == 0 and "iq0" in dbg_handles:
                    dump(sch, "iq0", iq)
                    sch.wait("vector", sg, sg.n)
                if pair == 2 and half == 0 and ic == 0 and "iq4" in dbg_handles:
                    dump(sch, "iq4", iq)
                    sch.wait("vector", sg, sg.n)
                sch.wait("tensor", sv, sv.n)
                for j in range(4):
                    pb = PS[6 + j % 2]
                    if j >= 2:
                        sch.wait("tensor", sv, sv.n)
                    w = mm_group(sch, [lambda e, p=pb, q=j: e.matmul(
                        p[:, 0:128], iq[:, q * 128:(q + 1) * 128], idf_sb, start=True, stop=True, is_transpose=True)])
                    sch.wait("vector", sm, w)
                    sch.em("vector", lambda e, p=pb, hh=half, i=ic, q=j: e.tensor_copy(
                        iqT[:, hh * II + (i * 4 + q) * 128: hh * II + (i * 4 + q + 1) * 128], p[:, 0:128]).then_inc(sv.h))
                    sv.inc()
            sch.wait("tensor", sv, sv.n)
        for ik in range(II // 128):
            buf = wd_t[ik % 2]
            if wd_reader[ik % 2]:
                sch.wait("sync", sm, wd_reader[ik % 2])
            if local:
                ap_ = wd_all[ik * 128:(ik + 1) * 128, :]
            else:
                ap_ = wd_all[ik // 8, (ik % 8) * 128:(ik % 8 + 1) * 128, :]
            sch.em("sync", lambda e, b=buf, a=ap_: e.dma_start(out=b, in_=a).then_inc(sl.h, 16))
            sch.wait("tensor", sl, sl.inc(16))
            fns = []
            for half in range(2):
                for n4 in range(4):
                    def mmf(e, b=buf, hh=half, n=n4, k=ik):
                        return e.matmul(PS[hh * 4 + n][:, 0:512], iqT[:, hh * II + k * 128:hh * II + (k + 1) * 128],
                                        b[:, n * 512:(n + 1) * 512], start=(k == 0), stop=(k == II // 128 - 1))
                    fns.append(mmf)
            wd_reader[ik % 2] = mm_group(sch, fns)
        sch.wait("vector", sm, sm.n)
        if pair in (0, 2) and "ffn0" in dbg_handles:
            for hh, nm in ((0, "ffn0" if pair == 0 else "ffn4"), (1, "ffn1" if pair == 0 else "ffn5")):
                for n4 in range(4):
                    sch.em("vector", lambda e, n=n4, h_=hh: e.tensor_copy(dbgffn[:, n * 512:(n + 1) * 512], PS[h_ * 4 + n][:, 0:512]).then_inc(sv.h))
                    sv.inc()
                dump(sch, nm, dbgffn)
                sch.wait("vector", sg, sg.n)
        for half in range(2):
            t8 = pair * 2 + half
            sch.wait("sync", sv, sv.n)
            sch.em("sync", lambda e, t=t8: e.dma_start(out=x2l, in_=x2_dram[t * 128:(t + 1) * 128, :]).then_inc(sl.h, 16))
            sch.wait("vector", sl, sl.inc(16))
            sch.wait("vector", sout, sout.n)
            for n4 in range(4):
                sch.em("vector", lambda e, hh=half, n=n4: e.tensor_tensor(
                    ot[:, n * 512:(n + 1) * 512], PS[hh * 4 + n][:, 0:512], gb_b[:, n * 512:(n + 1) * 512], op=ALU.mult))
                sch.em("vector", lambda e, n=n4: e.tensor_tensor(
                    ot[:, n * 512:(n + 1) * 512], ot[:, n * 512:(n + 1) * 512], x2l[:, n * 512:(n + 1) * 512], op=ALU.add))
            sch.em("vector", lambda e: e.tensor_scalar(st2[:, 8:9] if False else ot[:, 0:1], ot[:, 0:1], 1.0, None, op0=ALU.mult).then_inc(sv.h))
            vw = sv.inc()
            sch.wait("gpsimd", sv, vw)
            sch.em("gpsimd", lambda e, t=t8: e.dma_start(out=out[t * 128:(t + 1) * 128, :], in_=ot).then_inc(sout.h, 16))
            sout.inc(16)
            sch.wait("vector", sout, sout.n)
            sch.wait("tensor", sv, vw)
    sch.wait("gpsimd", sout, sout.n)
    AF32.pop(); ABF.pop()
    sch.run_block(nc)

    return nc, dbg_handles


def _host_tables(c):
    bA, bB = c, 15 - c
    pos = np.concatenate([np.arange(bA * S, (bA + 1) * S), np.arange(bB * S, (bB + 1) * S)])
    j = np.arange(64)
    theta = 10000.0 ** (-(j.astype(np.float64)) * 2.0 / 128.0)
    ang = pos[:, None].astype(np.float64) * theta[None, :]
    cosq = np.cos(ang).astype(np.float32)
    sinq = np.sin(ang).astype(np.float32)

    def slab(g):
        return 2 * g if g < 8 else 2 * (15 - g) + 1
    ua = [slab(g) for g in range(bA, -1, -1)] + [16] * (NU_A - (bA + 1))
    ub = [slab(g) for g in range(bB, -1, -1)] + [16] * (NU_B - (bB + 1))
    sl_ = np.array(ua + ub + [16] * (64 - NU_A - NU_B), np.int64)
    utab = (sl_ * SLABE).astype(np.int32).reshape(1, 64)
    return cosq, sinq, utab


def _prep_inputs(inputs, mode="ag"):
    x = np.ascontiguousarray(np.asarray(inputs["L0_attn_x"]).reshape(8192, D))
    wqT = np.ascontiguousarray(np.asarray(inputs["L0_attn_Wq"]).T)
    wkT = np.ascontiguousarray(np.asarray(inputs["L0_attn_Wk"]).T)
    wvT = np.ascontiguousarray(np.asarray(inputs["L0_attn_Wv"]).T)
    woT = np.ascontiguousarray(np.asarray(inputs["L0_attn_Wo"]).T)
    wg = np.asarray(inputs["L0_ffn_w_gate"])
    wu = np.asarray(inputs["L0_ffn_w_up"])
    wd = np.asarray(inputs["L0_ffn_w_down"])
    cv = np.asarray(inputs["L0_c"]).reshape(D)
    cvT = np.ascontiguousarray(cv.reshape(DK, 128).T)
    wmod = np.asarray(inputs["L0_wmod"])
    in_maps = []
    for c in range(8):
        bA, bB = c, 15 - c
        cosq, sinq, utab = _host_tables(c)
        m = {
            "xa": np.ascontiguousarray(x[bA * S:(bA + 1) * S]),
            "xb": np.ascontiguousarray(x[bB * S:(bB + 1) * S]),
            "wqT": wqT, "wkT": wkT, "wvT": wvT, "woT": woT,
            "cvT": cvT, "cosq": cosq, "sinq": sinq, "utab": utab,
        }
        if mode == "local":
            m["wmod"] = wmod
            m["wg_sl"] = wg
            m["wu_sl"] = wu
            m["wd_sl"] = wd
        else:
            m["wmod"] = np.ascontiguousarray(wmod[:, c * 1536:(c + 1) * 1536])
            m["wg_sl"] = np.ascontiguousarray(wg[c * 1024:(c + 1) * 1024])
            m["wu_sl"] = np.ascontiguousarray(wu[c * 1024:(c + 1) * 1024])
            m["wd_sl"] = np.ascontiguousarray(wd[:, c * 1024:(c + 1) * 1024])
        in_maps.append(m)
    return in_maps


_MODE = "ag"
_ROUND = "magic"


def kernel(**inputs):
    in_maps = _prep_inputs(inputs, _MODE)
    nc, _ = build_program(_MODE, _ROUND, debug=())
    res = run_bass_kernel_spmd(nc, in_maps, core_ids=list(range(8)))
    outf = np.zeros((8192, D), np.float32)
    for c in range(8):
        o = np.asarray(res.results[c]["out"])
        outf[c * S:(c + 1) * S] = o[0:S]
        outf[(15 - c) * S:(16 - c) * S] = o[S:2 * S]
    return outf.reshape(1, 8192, D)


# version 06
# speedup vs baseline: 1.0864x; 1.0170x over previous
import numpy as np
import ml_dtypes
import concourse.bass as bass
import concourse.mybir as mybir
from concourse.bass_utils import run_bass_kernel_spmd
from concourse.expressions_rust import RuntimeValue

F32 = mybir.dt.float32
BF16 = mybir.dt.bfloat16
I32 = mybir.dt.int32
AF = mybir.ActivationFunctionType
ALU = mybir.AluOpType
ET = mybir.EngineType
AXX = mybir.AxisListType.X

D = 2048
H = 16
S = 512
KPART = 16 * 128 * 512
SLABE = 2 * 16 * 128 * 512
T = 1024
II = 8192
DK = 16
LN2 = float(np.log(2.0))
MAGIC = float(1.5 * 2.0 ** 23)
LN_EPS = 1e-5
ATT_EPS = 1e-6
NU_A, NU_B = 8, 16
BF = ml_dtypes.bfloat16
RG = [list(range(8))]


class Sem:
    def __init__(self, nc, name):
        self.h = nc.alloc_semaphore(name)
        self.n = 0

    def inc(self, k=1):
        self.n += k
        return self.n


class Sched:
    def __init__(self):
        self.q = {e: [] for e in ("sync", "gpsimd", "tensor", "vector", "scalar")}

    def em(self, eng, fn):
        self.q[eng].append(fn)

    def wait(self, eng, sem, val):
        self.q[eng].append(lambda e, s=sem.h, v=val: e.wait_ge(s, v))

    def run_block(self, nc):
        with nc.Block() as block:
            for name, deco in (("sync", block.sync), ("gpsimd", block.gpsimd),
                               ("tensor", block.tensor), ("vector", block.vector),
                               ("scalar", block.scalar)):
                if self.q[name]:
                    def body(e, _q=tuple(self.q[name])):
                        for f in _q:
                            f(e)
                    deco(body)


class Arena:

    def __init__(self, nc, name, cols, dtype):
        self.t = nc.alloc_sbuf_tensor(name, [128, cols], dtype)
        self.cols = cols
        self.off = 0
        self.marks = []

    def push(self):
        self.marks.append(self.off)

    def pop(self):
        self.off = self.marks.pop()

    def take(self, cols):
        self.off = (self.off + 15) // 16 * 16
        st = self.off
        self.off += cols
        assert self.off <= self.cols, f"{self.t.name} arena overflow: {self.off} > {self.cols}"
        return self.t.ap()[:, st:st + cols]


def build_program(mode="ag", rounding="magic", debug=(), stop_after=None, noag=False):
    nc = bass.Bass("TRN2", target_bir_lowering=False, debug=False, dynamic_dma_scratch_size=20480)
    local = (mode == "local")

    def inp(name, shape, dt=F32):
        return nc.dram_tensor(name, shape, dt, kind="ExternalInput").ap()

    xa = inp("xa", [S, D])
    xb = inp("xb", [S, D])
    wT_in = [inp(n, [D, D]) for n in ("wqT", "wkT", "wvT", "woT")]
    cvT = inp("cvT", [128, DK])
    wmod = inp("wmod", [D, (96 if local else 12) * 128])
    NCH = 96 if local else 12
    WR = 8192 if local else 1024
    WDC = 8192 if local else 1024
    wg_sl = inp("wg_sl", [WR, D])
    wu_sl = inp("wu_sl", [WR, D])
    wd_sl = inp("wd_sl", [D, WDC])
    cosq = inp("cosq", [T, 64])
    sinq = inp("sinq", [T, 64])
    utab = inp("utab", [1, 64], I32)

    out = nc.dram_tensor("out", [T, D], F32, kind="ExternalOutput").ap()
    dbg_handles = {}

    def dbgh(name, shape, dt=F32):
        if name in debug:
            dbg_handles[name] = nc.dram_tensor("dbg_" + name, shape, dt, kind="ExternalOutput").ap()

    for nm, shp in (("mod", [128, 96]), ("q0", [128, 2048]), ("k0", [128, 2048]),
                    ("v0", [128, 2048]), ("ln0", [128, 2048]), ("attn0", [128, 2048]),
                    ("x2", [128, 2048]), ("hq0", [128, 2048]), ("amx", [128, 2048]),
                    ("wq0", [128, 2048]), ("den0", [128, 512]), ("wqa", [128, 2048]), ("lnx", [128, 2048]), ("lnf", [128, 2048]), ("st", [128, 16]),
                    ("qtm0", [128, 512]), ("rope0", [128, 512]), ("cos0", [128, 64]),
                    ("qbb", [128, 2048]), ("psq0", [128, 512]),
                    ("qe_y", [128, 2048]), ("qe_a", [128, 2048]), ("qe_g2", [128, 2048]),
                    ("qe_w1", [128, 2048]), ("qe_w2", [128, 2048]), ("qe_rnd", [128, 2048]),
                    ("qe_qout", [128, 2048]), ("iq0", [128, 512]), ("ffn0", [128, 2048]),
                    ("ffn1", [128, 2048]), ("ffn4", [128, 2048]), ("ffn5", [128, 2048]),
                    ("int4", [128, 8192]), ("iq4", [128, 512]), ("scall", [128, 2048])):
        dbgh(nm, shp)
    for nm, shp in (("wqfull", [2048, 2048]), ("lnxf", [128, 8192]), ("lnxf2", [128, 8192])):
        if nm in debug:
            dbg_handles[nm] = nc.dram_tensor("dbg_" + nm, shp, BF16, kind="ExternalOutput").ap()
    if False:
        dbgh(nm, shp)

    ident_bf = nc.inline_tensor(np.eye(128, dtype=BF), "identbf").ap()
    ident_f = nc.inline_tensor(np.eye(128, dtype=np.float32), "identf").ap()
    ones_bf = nc.inline_tensor(np.ones((128, 1), dtype=BF), "onesbf").ap()
    ones1 = nc.inline_tensor(np.ones((1, 128), dtype=np.float32), "ones1").ap()
    mk = np.zeros((128, 4 * S), dtype=BF)
    for kt in range(4):
        kid = kt * 128 + np.arange(128)[:, None]
        mk[:, kt * S:(kt + 1) * S] = (kid <= np.arange(S)[None, :]).astype(BF)
    maskc = nc.inline_tensor(mk, "maskc").ap()

    def idram(name, shape, dt):
        return nc.dram_tensor(name, shape, dt, kind="Internal").ap()

    kv_in = idram("kv_in", [2, SLABE], BF16)
    kv_all = idram("kv_all", [17 * SLABE], BF16)
    kv_seq = idram("kv_seq", [24 * SLABE], BF16)
    kvi_k = kv_in.rearrange("s (a p f) -> s a p f", p=128, f=S)
    kvi_flat = kv_in.rearrange("s e -> (s e)")
    wq_attn = idram("wq_attn", [4, D, D], BF16)
    wgq_in = idram("wgq_in", [D, WR], BF16)
    wuq_in = idram("wuq_in", [D, WR], BF16)
    wdq_in = idram("wdq_in", [WDC, D], BF16)
    if local:
        wg_all, wu_all, wd_all = wgq_in, wuq_in, wdq_in
        mod_in = idram("mod_in", [96, 128], F32)
        mod_all = mod_in
    else:
        wg_all = idram("wg_all", [8, D, 1024], BF16)
        wu_all = idram("wu_all", [8, D, 1024], BF16)
        wd_all = idram("wd_all", [8, 1024, D], BF16)
        mod_in = idram("mod_in", [12, 128], F32)
        mod_all = idram("mod_all", [8, 12, 128], F32)
    mod_flat = mod_all.rearrange("a b c -> (a b c)") if not local else mod_all.rearrange("b c -> (b c)")
    x2_dram = idram("x2_dram", [T, D], F32)
    rows_dram = idram("rows_dram", [8, D], F32)
    RIDX = {"ga": 0, "gb": 1, "cb1": 2, "qb": 3, "kb": 4, "vb": 5}
    inter_dram = idram("inter_dram", [T, II], F32)

    AF32 = Arena(nc, "af32", 26624, F32)
    ABF = Arena(nc, "abf", 41472, BF16)
    mm_row = nc.alloc_sbuf_tensor("mmrow", [1, D], F32).ap()
    ones1_sb = nc.alloc_sbuf_tensor("ones1s", [1, 128], F32).ap()
    utab_sb = nc.alloc_sbuf_tensor("utabs", [1, 64], I32).ap()

    id_sb = ABF.take(128)
    idf_sb = AF32.take(128)
    onesb_sb = ABF.take(16)[:, 0:1]
    mask_sb = ABF.take(4 * S)
    qT_sb = ABF.take(H * T)
    hqT_sb = qT_sb
    attnT_sb = ABF.take(H * T)
    mod_sb = AF32.take(96)
    ca1_sb = AF32.take(DK)
    cb1_sb = AF32.take(DK)
    cos_sb = AF32.take(8 * 64)
    sin_sb = AF32.take(8 * 64)
    cst66 = AF32.take(16)

    PS = [nc.alloc_psum_tensor(f"ps{i}", [128, 512], F32).ap() for i in range(8)]

    sprobe = Sem(nc, "sprobe") if "semprobe" in debug else None
    sl = Sem(nc, "sl")
    sl2 = Sem(nc, "sl2")
    sg = Sem(nc, "sg")
    sm = Sem(nc, "sm")
    sv = Sem(nc, "sv")
    ss = Sem(nc, "ss")
    scc = Sem(nc, "scc")
    sout = Sem(nc, "sout")

    def mm_group(sch, fns):
        def run(e, _fns=tuple(fns)):
            ins = None
            for f in _fns:
                ins = f(e)
            ins.then_inc(sm.h)
        sch.em("tensor", run)
        return sm.inc()

    def v_end(sch):
        raise RuntimeError("unused")

    def emit_scales(sch, amax_ap, scr3, scale_ap, rscale_ap):
        ag, lt, t2 = scr3
        sch.em("vector", lambda e: e.tensor_scalar(ag, amax_ap, float(2.0 ** -60), None, op0=ALU.max).then_inc(sv.h))
        sch.wait("scalar", sv, sv.inc())
        sch.em("scalar", lambda e: e.activation(out=lt, in_=ag, func=AF.Ln).then_inc(ss.h))
        sch.wait("vector", ss, ss.inc())
        sch.em("vector", lambda e: e.tensor_scalar(ag, lt, float(1.0 / LN2), 64.0 + 1e-9, op0=ALU.mult, op1=ALU.add).then_inc(sv.h))
        sch.wait("vector", sv, sv.inc())
        if rounding == "magic":
            sch.em("vector", lambda e: e.tensor_scalar(t2, ag, MAGIC, MAGIC, op0=ALU.add, op1=ALU.subtract).then_inc(sv.h))
            sch.wait("vector", sv, sv.inc())
            sch.em("vector", lambda e: e.tensor_tensor(lt, t2, ag, op=ALU.subtract).then_inc(sv.h))
            sch.wait("vector", sv, sv.inc())
            sch.em("vector", lambda e: e.tensor_scalar(lt, lt, 1e-12, None, op0=ALU.is_ge).then_inc(sv.h))
            sch.wait("vector", sv, sv.inc())
            sch.em("vector", lambda e: e.scalar_tensor_tensor(t2, lt, -1.0, t2, op0=ALU.mult, op1=ALU.add).then_inc(sv.h))
        else:
            r = lt
            sch.em("vector", lambda e: e.tensor_copy(r, ag).then_inc(sv.h))
            sch.wait("vector", sv, sv.inc())
            for stp in (64.0, 32.0, 16.0, 8.0, 4.0, 2.0, 1.0):
                sch.em("vector", lambda e, s=stp: e.tensor_scalar(t2, r, s, None, op0=ALU.is_ge).then_inc(sv.h))
                sch.wait("vector", sv, sv.inc())
                sch.em("vector", lambda e, s=stp: e.scalar_tensor_tensor(r, t2, -s, r, op0=ALU.mult, op1=ALU.add).then_inc(sv.h))
                sch.wait("vector", sv, sv.inc())
            sch.em("vector", lambda e: e.tensor_tensor(t2, ag, r, op=ALU.subtract).then_inc(sv.h))
        sch.wait("scalar", sv, sv.inc())
        sch.em("scalar", lambda e: e.activation(out=scale_ap, in_=t2, func=AF.Exp, scale=LN2, bias=cst66[:, 0:1]))
        sch.em("scalar", lambda e: e.activation(out=rscale_ap, in_=t2, func=AF.Exp, scale=float(-LN2), bias=cst66[:, 1:2]).then_inc(ss.h))
        return ss.inc()

    def emit_round(sch, v_ap, m_ap, b_ap, r_ap):
        if rounding == "magic":
            sch.em("vector", lambda e: e.tensor_scalar(m_ap, v_ap, MAGIC, MAGIC, op0=ALU.add, op1=ALU.subtract))
        else:
            sch.em("vector", lambda e: e.tensor_scalar(r_ap, v_ap, 16.0, None, op0=ALU.add))
            for stp in (16.0, 8.0, 4.0, 2.0, 1.0):
                sch.em("vector", lambda e, s=stp: e.tensor_scalar(b_ap, r_ap, s, None, op0=ALU.is_ge))
                sch.em("vector", lambda e, s=stp: e.scalar_tensor_tensor(r_ap, b_ap, -s, r_ap, op0=ALU.mult, op1=ALU.add))
            sch.em("vector", lambda e: e.tensor_scalar(b_ap, r_ap, 0.5, None, op0=ALU.is_ge))
            sch.em("vector", lambda e: e.tensor_tensor(m_ap, v_ap, r_ap, op=ALU.subtract))
            sch.em("vector", lambda e: e.tensor_tensor(m_ap, m_ap, b_ap, op=ALU.add))

    def emit_qelem(sch, x_ap, out_ap, rscale_ap, scale_ap, FD, scr7, mfrac, mxn, dbg=False):
        y, a, g2, g4, t1, w1, w2 = scr7
        nb = FD // 32
        x3 = x_ap.rearrange("p (a b) -> p a b", b=32)
        y3 = y.rearrange("p (a b) -> p a b", b=32)
        rs3 = rscale_ap.rearrange("p (a o) -> p a o", o=1).broadcast_to([128, nb, 32])
        sc3 = scale_ap.rearrange("p (a o) -> p a o", o=1).broadcast_to([128, nb, 32])
        sch.em("vector", lambda e: e.tensor_tensor(y3, x3, rs3, op=ALU.mult).then_inc(sv.h))
        sch.wait("scalar", sv, sv.inc())
        sch.em("scalar", lambda e: e.activation(out=a, in_=y, func=AF.Abs).then_inc(ss.h))
        sch.wait("vector", ss, ss.inc())
        sch.em("vector", lambda e: e.tensor_scalar(g2, a, 2.0, None, op0=ALU.is_ge))
        sch.em("vector", lambda e: e.tensor_scalar(g4, a, 4.0, None, op0=ALU.is_ge))
        if dbg:
            dump(sch, "qe_y", y)
            dump(sch, "qe_a", a)
            dump(sch, "qe_g2", g2)
            sch.wait("vector", sg, sg.n)
        sch.em("vector", lambda e: e.scalar_tensor_tensor(t1, g2, 2.0, g4, op0=ALU.mult, op1=ALU.add))
        k1, base = (2.0, 8.0) if mfrac == 3 else (0.5, 2.0)
        sch.em("vector", lambda e: e.tensor_scalar(w1, t1, -k1, base, op0=ALU.mult, op1=ALU.add))
        sch.em("vector", lambda e: e.tensor_tensor(w2, y, w1, op=ALU.mult))
        if dbg:
            dump(sch, "qe_w1", w1)
            dump(sch, "qe_w2", w2)
            sch.wait("vector", sg, sg.n)
        emit_round(sch, w2, a, t1, w1)
        if dbg:
            dump(sch, "qe_rnd", a)
            sch.wait("vector", sg, sg.n)
        sch.em("vector", lambda e: e.scalar_tensor_tensor(t1, g4, 2.0, g2, op0=ALU.mult, op1=ALU.add))
        s1 = 0.125 if mfrac == 3 else 0.5
        sch.em("vector", lambda e: e.tensor_scalar(w1, t1, s1, s1, op0=ALU.mult, op1=ALU.add))
        sch.em("vector", lambda e: e.tensor_tensor(w2, a, w1, op=ALU.mult))
        sch.em("vector", lambda e: e.tensor_scalar(w2, w2, float(mxn), float(-mxn), op0=ALU.min, op1=ALU.max))
        o3 = out_ap.rearrange("p (a b) -> p a b", b=32)
        w23 = w2.rearrange("p (a b) -> p a b", b=32)
        sch.em("vector", lambda e: e.tensor_tensor(o3, w23, sc3, op=ALU.mult).then_inc(sv.h))
        return sv.inc()

    def emit_bcast_row(sch, row_dram_ap, dst_ap):
        sch.em("gpsimd", lambda e: e.dma_start(out=mm_row, in_=row_dram_ap).then_inc(sg.h, 16))
        sch.wait("tensor", sg, sg.inc(16))
        for n4 in range(4):
            def mmf(e, n=n4):
                return e.matmul(PS[7][:, 0:512], ones1_sb, mm_row[0:1, n * 512:(n + 1) * 512], start=True, stop=True)
            w = mm_group(sch, [mmf])
            sch.wait("vector", sm, w)
            sch.em("vector", lambda e, n=n4: e.tensor_copy(dst_ap[:, n * 512:(n + 1) * 512], PS[7][:, 0:512]).then_inc(sv.h))
            sch.wait("tensor", sv, sv.inc())
        sch.wait("gpsimd", sm, sm.n)

    def emit_ln(sch, x_ap, xsq_ap, st_ap, out_fn):
        mr, m, s2r, s2m, mm2, var, veps, sq_, rstd = (st_ap[:, i:i + 1] for i in range(9))
        sch.em("vector", lambda e: e.reduce_sum(mr, x_ap, axis=AXX).then_inc(sv.h))
        sch.wait("vector", sv, sv.inc())
        sch.em("vector", lambda e: e.tensor_scalar(m, mr, float(1.0 / D), None, op0=ALU.mult).then_inc(sv.h))
        sch.wait("scalar", sv, sv.inc())
        sch.em("scalar", lambda e: e.activation(out=xsq_ap, in_=x_ap, func=AF.Square).then_inc(ss.h))
        sch.wait("vector", ss, ss.inc())
        sch.em("vector", lambda e: e.reduce_sum(s2r, xsq_ap, axis=AXX).then_inc(sv.h))
        sch.wait("vector", sv, sv.inc())
        sch.em("vector", lambda e: e.tensor_scalar(s2m, s2r, float(1.0 / D), None, op0=ALU.mult))
        sch.em("vector", lambda e: e.tensor_tensor(mm2, m, m, op=ALU.mult).then_inc(sv.h))
        sch.wait("vector", sv, sv.inc())
        sch.em("vector", lambda e: e.tensor_tensor(var, s2m, mm2, op=ALU.subtract).then_inc(sv.h))
        sch.wait("vector", sv, sv.inc())
        sch.em("vector", lambda e: e.tensor_scalar(veps, var, float(LN_EPS), None, op0=ALU.add).then_inc(sv.h))
        sch.wait("scalar", sv, sv.inc())
        sch.em("scalar", lambda e: e.activation(out=sq_, in_=veps, func=AF.Sqrt).then_inc(ss.h))
        sch.wait("vector", ss, ss.inc())
        sch.em("vector", lambda e: e.reciprocal(rstd, sq_).then_inc(sv.h))
        sch.wait("vector", sv, sv.inc())
        out_fn(m, rstd)

    def dump(sch, name, src_ap):
        h = dbg_handles.get(name)
        if h is None:
            return
        sch.wait("gpsimd", sv, sv.n)
        sch.wait("gpsimd", ss, ss.n)
        sch.wait("gpsimd", sm, sm.n)
        sch.em("gpsimd", lambda e: e.dma_start(out=h, in_=src_ap).then_inc(sg.h, 16))
        sch.wait("gpsimd", sg, sg.inc(16))

    sch = Sched()
    AF32.push(); ABF.push()
    cs_sb = AF32.take(DK)
    scT_sb = AF32.take(DK)
    gate_t = AF32.take(D)[0:1, :]
    modv = AF32.take(96)
    modt = AF32.take(128)
    wmt = [AF32.take(12 * 128) for _ in range(2)]
    zt = ABF.take(2048)
    for dst, src in ((id_sb, ident_bf), (idf_sb, ident_f), (onesb_sb, ones_bf),
                     (mask_sb, maskc), (ones1_sb, ones1), (utab_sb, utab), (cs_sb, cvT)):
        sch.em("sync", lambda e, d=dst, s=src: e.dma_start(out=d, in_=s).then_inc(sl.h, 16))
        sl.inc(16)
    for tc in range(8):
        sch.em("sync", lambda e, t=tc: e.dma_start(out=cos_sb[:, t * 64:(t + 1) * 64], in_=cosq[t * 128:(t + 1) * 128, :]).then_inc(sl.h, 16)); sl.inc(16)
        sch.em("sync", lambda e, t=tc: e.dma_start(out=sin_sb[:, t * 64:(t + 1) * 64], in_=sinq[t * 128:(t + 1) * 128, :]).then_inc(sl.h, 16)); sl.inc(16)
    all_loaded = sl.n
    sch.em("vector", lambda e: e.memset(cst66[:, 0:1], float(-66.0 * LN2)))
    sch.em("vector", lambda e: e.memset(cst66[:, 1:2], float(66.0 * LN2)))
    sch.em("vector", lambda e: e.memset(zt, 0.0).then_inc(sv.h))
    sch.wait("gpsimd", sv, sv.inc())
    for zchunk in range(SLABE // (128 * 2048)):
        sch.em("gpsimd", lambda e, z=zchunk: e.dma_start(
            out=kv_all[16 * SLABE + z * 128 * 2048: 16 * SLABE + (z + 1) * 128 * 2048].rearrange("(p f) -> p f", p=128),
            in_=zt).then_inc(sg.h, 16)); sg.inc(16)
    sch.wait("scalar", sl, all_loaded)
    sch.em("scalar", lambda e: e.activation(out=scT_sb, in_=cs_sb, func=AF.Silu).then_inc(ss.h))
    sch.wait("tensor", ss, ss.inc())
    nchunk = NCH // 12
    for ck in range(nchunk):
        for dk in range(DK):
            buf = wmt[dk % 2]
            if ck > 0 or dk >= 2:
                sch.wait("sync", sm, sm.n)
            sch.em("sync", lambda e, b=buf, d=dk, c=ck: e.dma_start(
                out=b, in_=wmod[d * 128:(d + 1) * 128, c * 1536:(c + 1) * 1536]).then_inc(sl.h, 16))
            sch.wait("tensor", sl, sl.inc(16))
            fns = []
            for ch in range(12):
                def mmf(e, b=buf, c=ch, d=dk, k=ck):
                    return e.matmul(PS[5][:, k * 12 + c:k * 12 + c + 1] if local else PS[5][:, c:c + 1],
                                    b[:, c * 128:(c + 1) * 128], scT_sb[:, d:d + 1],
                                    start=(k == 0 and d == 0 and c == 0),
                                    stop=(k == nchunk - 1 and d == DK - 1 and c == 11))
                fns.append(mmf)
            mm_group(sch, fns)
    sch.wait("vector", sm, sm.n)
    sch.em("vector", lambda e: e.tensor_copy(modv[:, 0:NCH], PS[5][:, 0:NCH]).then_inc(sv.h))
    sch.wait("tensor", sv, sv.inc())
    wtr = mm_group(sch, [lambda e: e.matmul(PS[6][0:NCH, 0:128], modv[:, 0:NCH], idf_sb, start=True, stop=True, is_transpose=True)])
    sch.wait("vector", sm, wtr)
    sch.em("vector", lambda e: e.tensor_copy(modt[0:NCH, 0:128], PS[6][0:NCH, 0:128]).then_inc(sv.h))
    sch.wait("gpsimd", sv, sv.inc())
    sch.em("gpsimd", lambda e: e.dma_start(out=mod_in[0:NCH, :], in_=modt[0:NCH, 0:128]).then_inc(sg.h, 16))
    sg.inc(16)
    if not local:
        sch.wait("gpsimd", sg, sg.n)
        sch.em("gpsimd", lambda e: e.collective_compute(
            "AllGather", ALU.bypass, replica_groups=RG,
            ins=[mod_in.opt()], outs=[mod_all.opt()]).then_inc(scc.h))
        scc.inc()
        sch.wait("gpsimd", scc, scc.n)
    sch.em("gpsimd", lambda e: e.dma_start(out=modt[0:96, 0:128], in_=mod_all if local else mod_all.rearrange("a b c -> (a b) c")).then_inc(sg.h, 16))
    sch.wait("tensor", sg, sg.inc(16))
    wtr = mm_group(sch, [lambda e: e.matmul(PS[6][:, 0:96], modt[0:96, 0:128], idf_sb[0:96, 0:96], start=True, stop=True, is_transpose=True)])
    sch.wait("vector", sm, wtr)
    sch.em("vector", lambda e: e.tensor_copy(mod_sb, PS[6][:, 0:96]).then_inc(sv.h))
    sv.inc()
    sch.wait("vector", sg, sg.n)
    sch.em("vector", lambda e: e.tensor_scalar(ca1_sb, mod_sb[:, 16:32], 1.0, None, op0=ALU.add))
    sch.em("vector", lambda e: e.tensor_scalar(cb1_sb, mod_sb[:, 64:80], 1.0, None, op0=ALU.add).then_inc(sv.h))
    sv.inc()
    scr_row = mm_row[0:1, :]
    for gn, lo in (("ga", 4096), ("gb", 10240)):
        sch.em("gpsimd", lambda e, q=lo: e.dma_start(out=scr_row, in_=mod_flat[q:q + D].rearrange("(o d) -> o d", o=1)).then_inc(sg.h, 16))
        sch.wait("vector", sg, sg.inc(16))
        sch.em("vector", lambda e: e.tensor_scalar(gate_t, scr_row, 0.0, None, op0=ALU.is_ge))
        sch.em("vector", lambda e: e.tensor_scalar(gate_t, gate_t, 2.0, -1.0, op0=ALU.mult, op1=ALU.add).then_inc(sv.h))
        sch.wait("scalar", sv, sv.inc())
        sch.em("scalar", lambda e: e.activation(out=scr_row, in_=scr_row, func=AF.Abs).then_inc(ss.h))
        sch.wait("vector", ss, ss.inc())
        sch.em("vector", lambda e: e.tensor_scalar(scr_row, scr_row, 0.25, None, op0=ALU.max))
        sch.em("vector", lambda e: e.tensor_tensor(scr_row, scr_row, gate_t, op=ALU.mult).then_inc(sv.h))
        sch.wait("gpsimd", sv, sv.inc())
        sch.em("gpsimd", lambda e, n=gn: e.dma_start(out=rows_dram[RIDX[n], :].rearrange("(o d) -> o d", o=1), in_=scr_row).then_inc(sg.h, 16))
        sch.wait("vector", sg, sg.inc(16))
    sch.em("gpsimd", lambda e: e.dma_start(out=scr_row, in_=mod_flat[8192:8192 + D].rearrange("(o d) -> o d", o=1)).then_inc(sg.h, 16))
    sch.wait("vector", sg, sg.inc(16))
    sch.em("vector", lambda e: e.tensor_scalar(scr_row, scr_row, 1.0, None, op0=ALU.add).then_inc(sv.h))
    sch.wait("gpsimd", sv, sv.inc())
    sch.em("gpsimd", lambda e: e.dma_start(out=rows_dram[RIDX["cb1"], :].rearrange("(o d) -> o d", o=1), in_=scr_row).then_inc(sg.h, 16))
    sch.wait("vector", sg, sg.inc(16))
    dump(sch, "mod", mod_sb)
    AF32.pop(); ABF.pop()
    sch.run_block(nc)
    nc.all_engine_barrier()
    if stop_after == "B0":
        return nc, dbg_handles

    sch = Sched()
    AF32.push(); ABF.push()
    wt_f = [AF32.take(D) for _ in range(2)]
    wt_b = ABF.take(D)
    wtf_reader = [[0, 0], [0, 0]]
    for wi in range(4):
        brow = ("qb", "kb", "vb", None)[wi]
        for dk in range(DK):
            buf = wt_f[dk % 2]
            if wtf_reader[dk % 2][0] or wtf_reader[dk % 2][1]:
                sch.wait("sync", sv, wtf_reader[dk % 2][0])
                sch.wait("sync", sm, wtf_reader[dk % 2][1])
            sch.em("sync", lambda e, b=buf, d=dk, w=wi: e.dma_start(out=b, in_=wT_in[w][d * 128:(d + 1) * 128, :]).then_inc(sl.h, 16))
            lw = sl.inc(16)
            if brow is not None:
                sch.wait("tensor", sl, lw)
                if dk == 0:
                    sch.wait("tensor", sv, sv.n)
                fns = []
                for n4 in range(4):
                    def mmf(e, b=buf, n=n4, d=dk):
                        return e.matmul(PS[4 + n][0:1, 0:512], mod_sb[:, d:d + 1],
                                        b[:, n * 512:(n + 1) * 512], start=(d == 0), stop=(d == DK - 1))
                    fns.append(mmf)
                _bias_t = mm_group(sch, fns)
            else:
                _bias_t = 0
            sch.wait("vector", sl, lw)
            if wi < 3:
                sch.em("vector", lambda e, b=buf, d=dk: e.tensor_scalar(wt_b, b, ca1_sb[:, d:d + 1], None, op0=ALU.mult).then_inc(sv.h))
            else:
                sch.em("vector", lambda e, b=buf: e.tensor_copy(wt_b, b).then_inc(sv.h))
            _sc_t = sv.inc()
            wtf_reader[dk % 2] = [_sc_t, _bias_t]
            sch.wait("gpsimd", sv, _sc_t)
            sch.em("gpsimd", lambda e, w=wi, d=dk: e.dma_start(out=wq_attn[w, d * 128:(d + 1) * 128, :], in_=wt_b).then_inc(sg.h, 16))
            sg.inc(16)
            sch.wait("vector", sg, sg.n)
        if brow is not None:
            sch.wait("vector", sm, sm.n)
            for n4 in range(4):
                sch.em("vector", lambda e, n=n4: e.tensor_copy(mm_row[0:1, n * 512:(n + 1) * 512], PS[4 + n][0:1, 0:512]))
            sch.em("vector", lambda e: e.tensor_scalar(mm_row[0:1, 0:1], mm_row[0:1, 0:1], 1.0, None, op0=ALU.mult).then_inc(sv.h))
            sch.wait("gpsimd", sv, sv.inc())
            sch.em("gpsimd", lambda e, bn=brow: e.dma_start(out=rows_dram[RIDX[bn], :].rearrange("(o d) -> o d", o=1), in_=mm_row).then_inc(sg.h, 16))
            sch.wait("tensor", sg, sg.inc(16))
            sch.wait("vector", sg, sg.n)
    if "wqa" in dbg_handles:
        sch.wait("gpsimd", sg, sg.n)
        sch.em("gpsimd", lambda e: e.dma_start(out=wt_b, in_=wq_attn[0, 0:128, :]).then_inc(sg.h, 16))
        sch.wait("gpsimd", sg, sg.inc(16))
        sch.em("gpsimd", lambda e: e.dma_start(out=dbg_handles["wqa"], in_=wt_b).then_inc(sg.h, 16))
        sch.wait("gpsimd", sg, sg.inc(16))
    if "wqfull" in dbg_handles:
        sch.wait("gpsimd", sg, sg.n)
        sch.em("gpsimd", lambda e: e.dma_start(out=dbg_handles["wqfull"], in_=wq_attn[0]).then_inc(sg.h, 16))
        sch.wait("gpsimd", sg, sg.inc(16))
    AF32.pop(); ABF.pop()
    sch.run_block(nc)
    nc.all_engine_barrier()
    if stop_after == "B1":
        return nc, dbg_handles

    sch = Sched()
    AF32.push(); ABF.push()
    x_tc = [AF32.take(D) for _ in range(2)]
    xsq = AF32.take(D)
    st_t = AF32.take(16)
    qtm = AF32.take(D)
    ropetmp = AF32.take(512)
    lnf = AF32.take(D)
    ropef = AF32.take(512)
    qb_b = AF32.take(D)
    kb_b = AF32.take(D)
    vb_b = AF32.take(D)
    snap_lnf = AF32.take(D) if "lnf" in dbg_handles else None
    snap_st = AF32.take(16) if "lnf" in dbg_handles else None
    lnxT = attnT_sb[:, 0:DK * S]
    vbf = attnT_sb[:, DK * S:DK * S + 2048]
    _o = DK * S + 2048
    wstr = [attnT_sb[:, _o:_o + 512], attnT_sb[:, _o + 512:_o + 1024]]
    kstage = attnT_sb[:, _o + 1024:_o + 1536]
    emit_bcast_row(sch, rows_dram[RIDX["qb"], :].rearrange("(o d) -> o d", o=1), qb_b)
    emit_bcast_row(sch, rows_dram[RIDX["kb"], :].rearrange("(o d) -> o d", o=1), kb_b)
    emit_bcast_row(sch, rows_dram[RIDX["vb"], :].rearrange("(o d) -> o d", o=1), vb_b)

    _b2slots = 1 if stop_after in ("B2ln", "B2q", "B2s0") else 2
    wstr_reader = [0, 0]
    for slot in range(_b2slots):
        xin = xa if slot == 0 else xb
        for tc in range(4):
            buf = x_tc[tc % 2]
            if tc >= 2 or slot > 0:
                sch.wait("sync", ss, ss.n)
                sch.wait("sync", sv, sv.n)
            sch.em("sync", lambda e, b=buf, t=tc, xi=xin: e.dma_start(out=b, in_=xi[t * 128:(t + 1) * 128, :]).then_inc(sl.h, 16))
            sch.wait("vector", sl, sl.inc(16))
            lnb = lnf

            def norm_out(m, rstd, b=buf, o=lnb):
                sch.em("vector", lambda e: e.tensor_scalar(o, b, m, rstd, op0=ALU.subtract, op1=ALU.mult).then_inc(sv.h))
                sv.inc()
            emit_ln(sch, buf, xsq, st_t, norm_out)
            if slot == 0 and tc == 0 and "lnf" in dbg_handles:
                sch.em("vector", lambda e: e.tensor_copy(snap_lnf, lnf))
                sch.em("vector", lambda e: e.tensor_copy(snap_st, st_t).then_inc(sv.h))
                sv.inc()
            if slot == 0 and tc == 0:
                dump(sch, "ln0", xsq)
            wvn = sv.n
            sch.wait("tensor", sv, wvn)
            for dc in range(DK):
                pb = PS[dc % 2]
                if dc >= 2:
                    sch.wait("tensor", sv, sv.n)
                w = mm_group(sch, [lambda e, p=pb, d=dc, o=lnb: e.matmul(
                    p[:, 0:128], o[:, d * 128:(d + 1) * 128], idf_sb, start=True, stop=True, is_transpose=True)])
                sch.wait("vector", sm, w)
                sch.em("vector", lambda e, p=pb, d=dc, t=tc: e.tensor_copy(
                    lnxT[:, d * S + t * 128:d * S + (t + 1) * 128], p[:, 0:128].bitcast(BF16) if False else p[:, 0:128]).then_inc(sv.h))
                sv.inc()
        sch.wait("tensor", sv, sv.n)
        if slot == 0:
            dump(sch, "lnx", lnxT[:, 0:2048])
            dump(sch, "lnxf", lnxT[:, 0:8192])
            if "lnf" in dbg_handles:
                dump(sch, "lnf", snap_lnf)
                dump(sch, "st", snap_st[:, 0:16])
        if stop_after == "B2ln":
            break
        _projs = ((0, False),) if stop_after == "B2q" else ((0, False), (1, False), (2, True))
        ps_guard = sv.n
        for pi, (wix, is_v) in enumerate(_projs):
            bb = (qb_b, kb_b, vb_b)[pi]
            for n4 in range(4):
                for dk in range(DK):
                    buf = wstr[dk % 2]
                    if wstr_reader[dk % 2]:
                        sch.wait("sync", sm, wstr_reader[dk % 2])
                    sch.em("sync", lambda e, b=buf, w=wix, d=dk, n=n4: e.dma_start(
                        out=b, in_=wq_attn[w, d * 128:(d + 1) * 128, n * 512:(n + 1) * 512]).then_inc(sl.h, 16))
                    sch.wait("tensor", sl, sl.inc(16))
                    if dk == 0:
                        sch.wait("tensor", sv, ps_guard)
                    fns = []
                    for tc in range(4):
                        def mmf(e, b=buf, d=dk, t=tc):
                            return e.matmul(PS[t][:, 0:512], lnxT[:, d * S + t * 128:d * S + (t + 1) * 128], b,
                                            start=(d == 0), stop=(d == DK - 1))
                        fns.append(mmf)
                    wstr_reader[dk % 2] = mm_group(sch, fns)
                mw = sm.n
                sch.wait("vector", sm, mw)
                for tc in range(4):
                    if is_v:
                        sch.em("vector", lambda e, t=tc, n=n4, b=bb: e.tensor_tensor(
                            vbf[:, t * 512:(t + 1) * 512], PS[t][:, 0:512], b[:, n * 512:(n + 1) * 512], op=ALU.add).then_inc(sv.h))
                        sch.wait("gpsimd", sv, sv.inc())
                        for j in range(4):
                            _vt = ((n4 * 4 + j) * 4 + tc)
                            sch.em("gpsimd", lambda e, sl_=slot, a=_vt, t=tc, n=n4, jj=j: e.dma_start(
                                out=kv_in[sl_, KPART + a * 16384:KPART + (a + 1) * 16384].rearrange("(p f) -> p f", p=128),
                                in_=vbf[:, t * 512 + jj * 128:t * 512 + (jj + 1) * 128]).then_inc(sg.h, 16))
                            sg.inc(16)
                        sch.wait("vector", sg, sg.n)
                    else:
                        if pi == 0 and n4 == 0 and tc == 0 and slot == 0 and "psq0" in dbg_handles:
                            sch.em("vector", lambda e: e.tensor_copy(ropetmp[:, 0:512], PS[0][:, 0:512]).then_inc(sv.h))
                            sv.inc()
                            dump(sch, "psq0", ropetmp[:, 0:512])
                            dump(sch, "qbb", bb)
                            sch.wait("vector", sg, sg.n)
                        sch.em("vector", lambda e, t=tc, n=n4, b=bb: e.tensor_tensor(
                            qtm[:, 0:512], PS[t][:, 0:512], b[:, n * 512:(n + 1) * 512], op=ALU.add).then_inc(sv.h))
                        sv.inc()
                        if pi == 0 and n4 == 0 and tc == 0 and slot == 0 and "qtm0" in dbg_handles:
                            dump(sch, "qtm0", qtm[:, 0:512])
                            dump(sch, "cos0", cos_sb[:, 0:64])
                            sch.wait("vector", sg, sg.n)
                        co = cos_sb[:, (slot * 4 + tc) * 64:(slot * 4 + tc + 1) * 64]
                        si = sin_sb[:, (slot * 4 + tc) * 64:(slot * 4 + tc + 1) * 64]
                        co4 = co.rearrange("p (o j) -> p o j", o=1).broadcast_to([128, 4, 64])
                        si4 = si.rearrange("p (o j) -> p o j", o=1).broadcast_to([128, 4, 64])
                        q4 = qtm[:, 0:512].rearrange("p (h j) -> p h j", j=128)
                        qlo, qhi = q4[:, :, 0:64], q4[:, :, 64:128]
                        tlo = ropetmp[:, 0:256].rearrange("p (h j) -> p h j", j=64)
                        thi = ropetmp[:, 256:512].rearrange("p (h j) -> p h j", j=64)
                        o4 = ropef[:, 0:512].rearrange("p (h j) -> p h j", j=128)
                        olo, ohi = o4[:, :, 0:64], o4[:, :, 64:128]
                        sch.em("vector", lambda e, a=tlo, x1=qlo, c=co4: e.tensor_tensor(a, x1, c, op=ALU.mult))
                        sch.em("vector", lambda e, a=thi, x2=qhi, s_=si4: e.tensor_tensor(a, x2, s_, op=ALU.mult))
                        sch.em("vector", lambda e, o=olo, a=tlo, b2=thi: e.tensor_tensor(o, a, b2, op=ALU.subtract))
                        sch.em("vector", lambda e, a=tlo, x1=qlo, s_=si4: e.tensor_tensor(a, x1, s_, op=ALU.mult))
                        sch.em("vector", lambda e, a=thi, x2=qhi, c=co4: e.tensor_tensor(a, x2, c, op=ALU.mult))
                        sch.em("vector", lambda e, o=ohi, a=tlo, b2=thi: e.tensor_tensor(o, a, b2, op=ALU.add).then_inc(sv.h))
                        sch.wait("tensor", sv, sv.inc())
                        if pi == 0 and n4 == 0 and tc == 0 and slot == 0 and "rope0" in dbg_handles:
                            dump(sch, "rope0", ropef[:, 0:512])
                            sch.wait("vector", sg, sg.n)
                        for hh in range(4):
                            h = n4 * 4 + hh
                            pb = PS[4 + hh % 2]
                            if hh >= 2:
                                sch.wait("tensor", sv, sv.n)
                            w = mm_group(sch, [lambda e, p=pb, q=hh: e.matmul(
                                p[:, 0:128], ropef[:, q * 128:(q + 1) * 128], idf_sb, start=True, stop=True, is_transpose=True)])
                            sch.wait("vector", sm, w)
                            if pi == 0:
                                dst = qT_sb[:, h * T + slot * S + tc * 128: h * T + slot * S + (tc + 1) * 128]
                                sch.em("vector", lambda e, p=pb, d=dst: e.tensor_copy(d, p[:, 0:128]).then_inc(sv.h))
                                sv.inc()
                            else:
                                sch.em("vector", lambda e, p=pb: e.tensor_copy(kstage[:, 0:128], p[:, 0:128]).then_inc(sv.h))
                                sch.wait("gpsimd", sv, sv.inc())
                                sch.em("gpsimd", lambda e, sl_=slot, hx=h, t=tc: e.dma_start(
                                    out=kvi_k[sl_, hx, :, t * 128:(t + 1) * 128],
                                    in_=kstage[:, 0:128]).then_inc(sg.h, 16))
                                sg.inc(16)
                                sch.wait("vector", sg, sg.n)
                ps_guard = sv.n
    sch.wait("gpsimd", sg, sg.n)
    sch.wait("gpsimd", sv, sv.n)
    if stop_after == "B2q":
        dump(sch, "lnxf2", lnxT[:, 0:8192])
    if not local and not noag and stop_after not in ("B2ln", "B2q", "B2s0", "B2noag"):
        sch.em("gpsimd", lambda e: e.collective_compute(
            "AllGather", ALU.bypass, replica_groups=RG,
            ins=[kvi_flat.opt()], outs=[kv_all[0:16 * SLABE].opt()]).then_inc(scc.h))
        scc.inc()
    dump(sch, "q0", qT_sb[:, 0:2048])
    AF32.pop(); ABF.pop()
    sch.run_block(nc)
    nc.all_engine_barrier()
    if stop_after is not None and stop_after.startswith("B2"):
        return nc, dbg_handles

    sch = Sched()
    if not local and not noag:
        sch.wait("scalar", scc, scc.n)
    AF32.push(); ABF.push()
    wrow = [AF32.take(D) for _ in range(2)]
    qscr = [AF32.take(D) for _ in range(7)]
    am = AF32.take(64)
    sscr = tuple(AF32.take(64) for _ in range(3))
    sc_t = AF32.take(64)
    rs_t = AF32.take(64)
    qout = AF32.take(D)
    tstage = ABF.take(DK * 128)
    ps_reader = [0, 0]

    wrow_reader = [0, 0]
    wrow_lt = {}

    def quant_matrix(src, dst, nrows, ncols, dbg=False):
        nb_ = ncols // 32
        ndc = ncols // 128
        nrt = nrows // 128

        def issue_load(rt):
            b = wrow[rt % 2]
            if wrow_reader[rt % 2]:
                sch.wait("scalar", sv, wrow_reader[rt % 2])
            sch.em("scalar", lambda e, b=b, r=rt, s=src, c=ncols: e.dma_start(out=b[:, 0:c], in_=s[r * 128:(r + 1) * 128, :]).then_inc(sl.h, 16))
            wrow_lt[rt] = sl.inc(16)

        issue_load(0)
        for rt in range(nrt):
            buf = wrow[rt % 2]
            if rt + 1 < nrt:
                issue_load(rt + 1)
            sch.wait("vector", sl, wrow_lt[rt])
            a3 = buf[:, 0:ncols].rearrange("p (a b) -> p a b", b=32)
            sch.em("vector", lambda e, a=a3, c=nb_: e.reduce_max(am[:, 0:c], a, axis=AXX, apply_absolute_value=True).then_inc(sv.h))
            sch.wait("vector", sv, sv.inc())
            emit_scales(sch, am[:, 0:nb_], tuple(s[:, 0:nb_] for s in sscr), sc_t[:, 0:nb_], rs_t[:, 0:nb_])
            sch.wait("vector", ss, ss.n)
            emit_qelem(sch, buf[:, 0:ncols], qout[:, 0:ncols], rs_t[:, 0:nb_], sc_t[:, 0:nb_], ncols, [q[:, 0:ncols] for q in qscr], 1, 6.0,
                       dbg=(dbg and rt == 0 and "qe_y" in dbg_handles))
            wrow_reader[rt % 2] = sv.n
            if dbg and rt == 0 and "qe_qout" in dbg_handles:
                dump(sch, "qe_qout", qout)
                sch.wait("vector", sg, sg.n)
                sch.wait("tensor", sg, sg.n)
            sch.wait("tensor", sv, sv.n)
            for dc in range(ndc):
                pb = PS[dc % 2]
                if ps_reader[dc % 2]:
                    sch.wait("tensor", sv, ps_reader[dc % 2])
                w = mm_group(sch, [lambda e, p=pb, d=dc: e.matmul(
                    p[:, 0:128], qout[:, d * 128:(d + 1) * 128], idf_sb, start=True, stop=True, is_transpose=True)])
                sch.wait("vector", sm, w)
                sch.em("vector", lambda e, p=pb, d=dc: e.tensor_copy(tstage[:, d * 128:(d + 1) * 128], p[:, 0:128]).then_inc(sv.h))
                ps_reader[dc % 2] = sv.inc()
            sch.wait("gpsimd", sv, sv.n)
            if dbg and rt == 0:
                dump(sch, "wq0", tstage[:, 0:2048])
            for dc in range(ndc):
                sch.em("gpsimd", lambda e, d=dc, r=rt, ds=dst: e.dma_start(
                    out=ds[d * 128:(d + 1) * 128, r * 128:(r + 1) * 128],
                    in_=tstage[:, d * 128:(d + 1) * 128]).then_inc(sg.h, 16))
                sg.inc(16)
            sch.wait("vector", sg, sg.n)

    quant_matrix(wg_sl, wgq_in, WR, D, dbg=True)
    quant_matrix(wu_sl, wuq_in, WR, D)
    quant_matrix(wd_sl, wdq_in, D, WDC)
    sch.wait("gpsimd", sg, sg.n)
    sch.wait("gpsimd", sv, sv.n)
    if not local:
        for src_, dst_ in ((wgq_in, wg_all), (wuq_in, wu_all), (wdq_in, wd_all)):
            sch.em("gpsimd", lambda e, s=src_, d=dst_: e.collective_compute(
                "AllGather", ALU.bypass, replica_groups=RG,
                ins=[s.opt()], outs=[d.opt()]).then_inc(scc.h))
            scc.inc()
    AF32.pop(); ABF.pop()
    sch.run_block(nc)
    nc.all_engine_barrier()
    if stop_after == "B3":
        return nc, dbg_handles

    kv_ready = scc.n if not local else 0
    sch = Sched()
    AF32.push(); ABF.push()
    s2_sb = [AF32.take(S) for _ in range(2)]
    bc_sb = AF32.take(S)
    kt_sb = [ABF.take(S) for _ in range(2)]
    vt_sb = [ABF.take(S) for _ in range(2)]
    s4_sb = [ABF.take(S) for _ in range(2)]
    if not local:
        sch.wait("gpsimd", scc, kv_ready)
    gather_base = sg.n
    gather_done = {}

    def gather(e):
        for u in range(24):
            if u == 0 or u == NU_A:
                e.dma_start(out=kv_seq[u * SLABE:(u + 1) * SLABE],
                            in_=kvi_flat[(0 if u == 0 else 1) * SLABE:(1 if u == 0 else 2) * SLABE]).then_inc(sg.h, 16)
            else:
                with e.register(f"g{u}") as r:
                    e.reg_load(r, utab_sb[0:1, u:u + 1])
                    rv = RuntimeValue(r, min_val=0, max_val=16 * SLABE)
                    e.dma_start(out=kv_seq[u * SLABE:(u + 1) * SLABE],
                                in_=kv_all[bass.DynSlice(rv, SLABE)]).then_inc(sg.h, 16)
    sch.em("gpsimd", gather)
    for u in range(24):
        gather_done[u] = gather_base + 16 * (u + 1)
    sg.inc(16 * 24)

    NUM, DEN = PS[2], PS[3]
    kv_reader = [0, 0]
    qk_bank = [0, 0]
    s2_bank = [0, 0]
    s4_bank = [0, 0]
    pending_v = []

    def flush_v(n):
        while len(pending_v) > n:
            s4t, vbuf, vtt, f_, l_, kt = pending_v.pop(0)
            if kt == 0:
                sch.wait("tensor", sg, vtt)
            sch.wait("tensor", sv, s4t)
            def vm(e, b=vbuf, k=kt, f=f_, l=l_):
                e.matmul(NUM[:, 0:512], vt_sb[b][:, k * 128:(k + 1) * 128], s4_sb[k % 2], start=f, stop=l)
                return e.matmul(DEN[0:1, 0:512], onesb_sb, s4_sb[k % 2], start=f, stop=l)
            s4_bank[kt % 2] = mm_group(sch, [vm])
            if kt == 3:
                kv_reader[vbuf] = s4_bank[1]

    for h in range(H):
        for slot in range(2):
            NU = NU_A if slot == 0 else NU_B
            q_rhs = qT_sb[:, h * T + slot * S: h * T + (slot + 1) * S]
            for u in range(NU):
                uid = u + (0 if slot == 0 else NU_A)
                if kv_reader[u % 2]:
                    sch.wait("gpsimd", sm, kv_reader[u % 2])
                sch.wait("gpsimd", sg, gather_done[uid])

                def kvload(e, i=uid, b=u % 2, hh=h):
                    kb_ = i * SLABE + hh * 65536
                    e.dma_start(out=kt_sb[b], in_=kv_seq[kb_:kb_ + 65536].rearrange("(p f) -> p f", p=128)).then_inc(sg.h, 16)
                    vb_ = i * SLABE + KPART + hh * 4 * 16384
                    e.dma_start(out=vt_sb[b][:, 0:512].rearrange("p (a f) -> p a f", f=128),
                                in_=kv_seq[vb_:vb_ + 65536].rearrange("(a p f) -> p a f", p=128, f=128)).then_inc(sg.h, 16)
                sch.em("gpsimd", kvload)
                kt_t = sg.inc(16)
                vt_t = sg.inc(16)
                sch.wait("tensor", sg, kt_t)
                diag = (u == 0)
                first = (u == 0)
                last = (u == NU - 1)

                for kt in range(4):
                    sb_ = PS[kt % 2]
                    if qk_bank[kt % 2]:
                        sch.wait("tensor", ss, qk_bank[kt % 2])
                    w = mm_group(sch, [lambda e, p=sb_, b=u % 2, k=kt, q=q_rhs: e.matmul(
                        p[:, 0:512], kt_sb[b][:, k * 128:(k + 1) * 128], q, start=True, stop=True)])
                    flush_v(1)
                    sch.wait("scalar", sm, w)
                    if s2_bank[kt % 2]:
                        sch.wait("scalar", sv, s2_bank[kt % 2])
                    sch.em("scalar", lambda e, p=sb_, k=kt: e.activation(out=s2_sb[k % 2], in_=p[:, 0:512], func=AF.Square).then_inc(ss.h))
                    qk_bank[kt % 2] = ss.inc()
                    sch.wait("vector", ss, qk_bank[kt % 2])
                    if s4_bank[kt % 2]:
                        sch.wait("vector", sm, s4_bank[kt % 2])
                    if diag:
                        sch.em("vector", lambda e, k=kt: e.tensor_tensor(s2_sb[k % 2], s2_sb[k % 2], mask_sb[:, k * S:(k + 1) * S], op=ALU.mult))
                    sch.em("vector", lambda e, k=kt: e.tensor_tensor(s4_sb[k % 2], s2_sb[k % 2], s2_sb[k % 2], op=ALU.mult).then_inc(sv.h))
                    s2_bank[kt % 2] = sv.inc()
                    pending_v.append((s2_bank[kt % 2], u % 2, vt_t, first and kt == 0, last and kt == 3, kt))
            flush_v(0)
            sch.wait("vector", sm, sm.n)
            sch.em("vector", lambda e: e.tensor_scalar(mm_row[0:1, 0:512], DEN[0:1, 0:512], float(ATT_EPS), None, op0=ALU.max).then_inc(sv.h))
            sch.wait("scalar", sv, sv.inc())
            sch.em("scalar", lambda e: e.activation(out=mm_row[0:1, 512:1024], in_=mm_row[0:1, 0:512], func=AF.Ln))
            sch.em("scalar", lambda e: e.activation(out=mm_row[0:1, 0:512], in_=mm_row[0:1, 512:1024], func=AF.Exp, scale=-1.0).then_inc(ss.h))
            sch.wait("tensor", ss, ss.inc())
            w = mm_group(sch, [lambda e: e.matmul(PS[4][:, 0:512], ones1_sb, mm_row[0:1, 0:512], start=True, stop=True)])
            sch.wait("vector", sm, w)
            sch.em("vector", lambda e: e.tensor_copy(bc_sb, PS[4][:, 0:512]))
            dst = attnT_sb[:, h * T + slot * S:(h * T + (slot + 1) * S)]
            sch.em("vector", lambda e, d=dst: e.tensor_tensor(d, NUM[:, 0:512], bc_sb, op=ALU.mult).then_inc(sv.h))
            sch.wait("tensor", sv, sv.inc())
            if h == 0 and slot == 0:
                dump(sch, "den0", bc_sb)
    dump(sch, "attn0", attnT_sb[:, 0:2048])
    AF32.pop(); ABF.pop()
    sch.run_block(nc)
    nc.all_engine_barrier()
    if stop_after == "B4":
        return nc, dbg_handles

    sch = Sched()
    AF32.push(); ABF.push()
    x_tc2 = [AF32.take(D) for _ in range(2)]
    x2t = AF32.take(D)
    epi = AF32.take(512)
    xsq2 = AF32.take(D)
    st2 = AF32.take(16)
    h2s = AF32.take(D)
    am2 = AF32.take(64)
    ss2 = tuple(AF32.take(64) for _ in range(3))
    sc2 = AF32.take(64)
    rs2 = AF32.take(64)
    qscr2 = [AF32.take(512) for _ in range(7)]
    ga_b = AF32.take(D)
    cb1_b = AF32.take(D)
    sb_b = AF32.take(D)
    wo_t = [ABF.take(D) for _ in range(2)]
    hq = AF32.take(D)
    emit_bcast_row(sch, rows_dram[RIDX["ga"], :].rearrange("(o d) -> o d", o=1), ga_b)
    emit_bcast_row(sch, rows_dram[RIDX["cb1"], :].rearrange("(o d) -> o d", o=1), cb1_b)
    emit_bcast_row(sch, mod_flat[6144:6144 + D].rearrange("(o d) -> o d", o=1), sb_b)

    wo_reader = [0, 0]
    xc = [AF32.take(512) for _ in range(2)]
    xc_reader = [0, 0]
    ps_free = 0
    for n4 in range(4):
        for dk in range(DK):
            wb = wo_t[dk % 2]
            if wo_reader[dk % 2]:
                sch.wait("sync", sm, wo_reader[dk % 2])
            sch.em("sync", lambda e, b=wb, d=dk, n=n4: e.dma_start(
                out=b[:, 0:512], in_=wq_attn[3, d * 128:(d + 1) * 128, n * 512:(n + 1) * 512]).then_inc(sl.h, 16))
            sch.wait("tensor", sl, sl.inc(16))
            if dk == 0:
                sch.wait("tensor", sv, ps_free)
            fns = []
            for t8 in range(8):
                def mmf(e, b=wb, t=t8, d=dk):
                    lh = attnT_sb[:, d * T + (t // 4) * S + (t % 4) * 128: d * T + (t // 4) * S + (t % 4 + 1) * 128]
                    return e.matmul(PS[t][:, 0:512], lh, b[:, 0:512], start=(d == 0), stop=(d == DK - 1))
                fns.append(mmf)
            wo_reader[dk % 2] = mm_group(sch, fns)
        sch.wait("vector", sm, sm.n)
        for t8 in range(8):
            slot, tc = t8 // 4, t8 % 4
            xin = xa if slot == 0 else xb
            xb_ = xc[t8 % 2]
            if xc_reader[t8 % 2]:
                sch.wait("sync", sv, xc_reader[t8 % 2])
            sch.em("sync", lambda e, b=xb_, xi=xin, t=tc, n=n4: e.dma_start(
                out=b, in_=xi[t * 128:(t + 1) * 128, n * 512:(n + 1) * 512]).then_inc(sl.h, 16))
            sch.wait("vector", sl, sl.inc(16))
            sch.em("vector", lambda e, t=t8, n=n4: e.tensor_tensor(epi, PS[t][:, 0:512], ga_b[:, n * 512:(n + 1) * 512], op=ALU.mult))
            sch.em("vector", lambda e, b=xb_: e.tensor_tensor(epi, epi, b, op=ALU.add).then_inc(sv.h))
            xc_reader[t8 % 2] = sv.inc()
            sch.wait("gpsimd", sv, xc_reader[t8 % 2])
            sch.em("gpsimd", lambda e, t=t8, n=n4: e.dma_start(
                out=x2_dram[t * 128:(t + 1) * 128, n * 512:(n + 1) * 512], in_=epi).then_inc(sg.h, 16))
            sg.inc(16)
            sch.wait("vector", sg, sg.n)
        ps_free = sv.n
    sch.wait("sync", sg, sg.n)
    for t8 in range(8):
        sch.wait("sync", sv, sv.n)
        sch.em("sync", lambda e, t=t8: e.dma_start(out=x2t, in_=x2_dram[t * 128:(t + 1) * 128, :]).then_inc(sl.h, 16))
        sch.wait("vector", sl, sl.inc(16))

        def norm2(m, rstd):
            sch.em("vector", lambda e: e.tensor_scalar(h2s, x2t, m, rstd, op0=ALU.subtract, op1=ALU.mult))
            sch.em("vector", lambda e: e.tensor_tensor(h2s, h2s, cb1_b, op=ALU.mult))
            sch.em("vector", lambda e: e.tensor_tensor(h2s, h2s, sb_b, op=ALU.add))
            a3 = h2s.rearrange("p (a b) -> p a b", b=32)
            sch.em("vector", lambda e, a=a3: e.reduce_max(am2, a, axis=AXX, apply_absolute_value=True).then_inc(sv.h))
            sch.wait("vector", sv, sv.inc())
        emit_ln(sch, x2t, xsq2, st2, norm2)
        emit_scales(sch, am2, ss2, sc2, rs2)
        sch.wait("vector", ss, ss.n)
        for n4 in range(4):
            emit_qelem(sch, h2s[:, n4 * 512:(n4 + 1) * 512], hq[:, n4 * 512:(n4 + 1) * 512],
                       rs2[:, n4 * 16:(n4 + 1) * 16], sc2[:, n4 * 16:(n4 + 1) * 16], 512, qscr2, 3, 7.5)
        sch.wait("tensor", sv, sv.n)
        for dc in range(DK):
            pb = PS[4 + dc % 2]
            if dc >= 2:
                sch.wait("tensor", sv, sv.n)
            w = mm_group(sch, [lambda e, p=pb, d=dc: e.matmul(
                p[:, 0:128], hq[:, d * 128:(d + 1) * 128], idf_sb, start=True, stop=True, is_transpose=True)])
            sch.wait("vector", sm, w)
            sch.em("vector", lambda e, p=pb, d=dc, t=t8: e.tensor_copy(
                hqT_sb[:, d * T + t * 128:d * T + (t + 1) * 128], p[:, 0:128]).then_inc(sv.h))
            sv.inc()
        sch.wait("tensor", sv, sv.n)
        if t8 == 0:
            dump(sch, "x2", x2t)
            dump(sch, "hq0", hq)
    AF32.pop(); ABF.pop()
    sch.run_block(nc)
    nc.all_engine_barrier()
    if stop_after == "B6":
        return nc, dbg_handles

    w_ready = scc.n
    sch = Sched()
    amx_all = AF32.take(2048)
    AF32.push(); ABF.push()
    sg_scr = AF32.take(512)
    int_scr = AF32.take(512)
    wgu = attnT_sb[:, 0:2 * DK * 512]
    if not local:
        sch.wait("sync", scc, w_ready)
    for ic in range(16):
        r, hf = ic // 2, ic % 2
        if ic > 0:
            sch.wait("sync", sm, sm.n)
            sch.wait("scalar", sm, sm.n)
        idx = 0
        for d2 in range(DK):
            for gu in range(2):
                srcw = wg_all if gu == 0 else wu_all
                if local:
                    ap_ = srcw[d2 * 128:(d2 + 1) * 128, ic * 512:(ic + 1) * 512]
                else:
                    ap_ = srcw[r, d2 * 128:(d2 + 1) * 128, hf * 512:(hf + 1) * 512]
                eng, sem = ("sync", sl) if idx % 2 == 0 else ("scalar", sl2)
                sch.em(eng, lambda e, a=ap_, d=d2, g=gu, se=sem: e.dma_start(
                    out=wgu[:, (g * DK + d) * 512:(g * DK + d + 1) * 512], in_=a).then_inc(se.h, 16))
                sem.inc(16)
                idx += 1
        sch.wait("tensor", sl, sl.n)
        sch.wait("tensor", sl2, sl2.n)
        for t8 in range(8):
            pg, pu = PS[t8 % 2], PS[2 + t8 % 2]
            if not (ic == 0 and t8 < 2):
                sch.wait("tensor", sv, sv.n)
            fns = []
            for d2 in range(DK):
                def mmf(e, d=d2, t=t8, a=pg, b=pu):
                    lh = hqT_sb[:, d * T + t * 128: d * T + (t + 1) * 128]
                    e.matmul(a[:, 0:512], lh, wgu[:, d * 512:(d + 1) * 512], start=(d == 0), stop=(d == DK - 1))
                    return e.matmul(b[:, 0:512], lh, wgu[:, (DK + d) * 512:(DK + d + 1) * 512], start=(d == 0), stop=(d == DK - 1))
                fns.append(mmf)
            w = mm_group(sch, fns)
            sch.wait("scalar", sm, w)
            sch.em("scalar", lambda e, a=pg: e.activation(out=sg_scr, in_=a[:, 0:512], func=AF.Silu).then_inc(ss.h))
            sch.wait("vector", ss, ss.inc())
            sch.em("vector", lambda e, b=pu: e.tensor_tensor(int_scr, sg_scr, b[:, 0:512], op=ALU.mult))
            i3 = int_scr.rearrange("p (a b) -> p a b", b=32)
            sch.em("vector", lambda e, a=i3, t=t8, i=ic: e.reduce_max(
                amx_all[:, t * 256 + i * 16: t * 256 + (i + 1) * 16], a, axis=AXX,
                apply_absolute_value=True).then_inc(sv.h))
            vw = sv.inc()
            sch.wait("gpsimd", sv, vw)
            sch.em("gpsimd", lambda e, t=t8, i=ic: e.dma_start(
                out=inter_dram[t * 128:(t + 1) * 128, i * 512:(i + 1) * 512], in_=int_scr).then_inc(sg.h, 16))
            sg.inc(16)
            sch.wait("vector", sg, sg.n)
    dump(sch, "amx", amx_all)
    AF32.pop(); ABF.pop()
    sch.run_block(nc)
    nc.all_engine_barrier()
    if stop_after == "B8":
        return nc, dbg_handles

    sch = Sched()
    AF32.push(); ABF.push()
    sc_all = AF32.take(2048)
    rs_all = AF32.take(2048)
    ssc = tuple(AF32.take(512) for _ in range(3))
    itile = [AF32.take(512) for _ in range(2)]
    qscr3 = [AF32.take(512) for _ in range(7)]
    gb_b = AF32.take(D)
    x2l = AF32.take(D)
    ot = AF32.take(D)
    iq = AF32.take(512)
    iqT = attnT_sb[:, 0:2 * II]
    wd_t = [ABF.take(D) for _ in range(2)]
    wd_reader = [0, 0]
    itile_reader = [0, 0]
    dbgffn = AF32.take(D) if "ffn0" in dbg_handles else None
    emit_bcast_row(sch, rows_dram[RIDX["gb"], :].rearrange("(o d) -> o d", o=1), gb_b)
    if "int4" in dbg_handles:
        sch.em("gpsimd", lambda e: e.dma_start(out=dbg_handles["int4"], in_=inter_dram[512:640, :]).then_inc(sg.h, 16))
        sch.wait("gpsimd", sg, sg.inc(16))
    for qc in range(4):
        emit_scales(sch, amx_all[:, qc * 512:(qc + 1) * 512], ssc,
                    sc_all[:, qc * 512:(qc + 1) * 512], rs_all[:, qc * 512:(qc + 1) * 512])
    sch.wait("vector", ss, ss.n)
    if "scall" in dbg_handles:
        dump(sch, "scall", sc_all)
        sch.wait("vector", sg, sg.n)
        sch.wait("scalar", sg, sg.n)
    for pair in range(4):
        for half in range(2):
            t8 = pair * 2 + half
            for ic in range(16):
                buf = itile[ic % 2]
                if itile_reader[ic % 2]:
                    sch.wait("sync", sv, itile_reader[ic % 2])
                sch.em("sync", lambda e, b=buf, t=t8, i=ic: e.dma_start(
                    out=b, in_=inter_dram[t * 128:(t + 1) * 128, i * 512:(i + 1) * 512]).then_inc(sl.h, 16))
                sch.wait("vector", sl, sl.inc(16))
                rsl = rs_all[:, t8 * 256 + ic * 16: t8 * 256 + (ic + 1) * 16]
                scl = sc_all[:, t8 * 256 + ic * 16: t8 * 256 + (ic + 1) * 16]
                emit_qelem(sch, buf, iq, rsl, scl, 512, qscr3, 3, 7.5)
                itile_reader[ic % 2] = sv.n
                if pair == 0 and half == 0 and ic == 0 and "iq0" in dbg_handles:
                    dump(sch, "iq0", iq)
                    sch.wait("vector", sg, sg.n)
                if pair == 2 and half == 0 and ic == 0 and "iq4" in dbg_handles:
                    dump(sch, "iq4", iq)
                    sch.wait("vector", sg, sg.n)
                sch.wait("tensor", sv, sv.n)
                for j in range(4):
                    pb = PS[6 + j % 2]
                    if j >= 2:
                        sch.wait("tensor", sv, sv.n)
                    w = mm_group(sch, [lambda e, p=pb, q=j: e.matmul(
                        p[:, 0:128], iq[:, q * 128:(q + 1) * 128], idf_sb, start=True, stop=True, is_transpose=True)])
                    sch.wait("vector", sm, w)
                    sch.em("vector", lambda e, p=pb, hh=half, i=ic, q=j: e.tensor_copy(
                        iqT[:, hh * II + (i * 4 + q) * 128: hh * II + (i * 4 + q + 1) * 128], p[:, 0:128]).then_inc(sv.h))
                    sv.inc()
            sch.wait("tensor", sv, sv.n)
        for ik in range(II // 128):
            buf = wd_t[ik % 2]
            if wd_reader[ik % 2]:
                sch.wait("sync", sm, wd_reader[ik % 2])
            if local:
                ap_ = wd_all[ik * 128:(ik + 1) * 128, :]
            else:
                ap_ = wd_all[ik // 8, (ik % 8) * 128:(ik % 8 + 1) * 128, :]
            sch.em("sync", lambda e, b=buf, a=ap_: e.dma_start(out=b, in_=a).then_inc(sl.h, 16))
            sch.wait("tensor", sl, sl.inc(16))
            fns = []
            for half in range(2):
                for n4 in range(4):
                    def mmf(e, b=buf, hh=half, n=n4, k=ik):
                        return e.matmul(PS[hh * 4 + n][:, 0:512], iqT[:, hh * II + k * 128:hh * II + (k + 1) * 128],
                                        b[:, n * 512:(n + 1) * 512], start=(k == 0), stop=(k == II // 128 - 1))
                    fns.append(mmf)
            wd_reader[ik % 2] = mm_group(sch, fns)
        sch.wait("vector", sm, sm.n)
        if pair in (0, 2) and "ffn0" in dbg_handles:
            for hh, nm in ((0, "ffn0" if pair == 0 else "ffn4"), (1, "ffn1" if pair == 0 else "ffn5")):
                for n4 in range(4):
                    sch.em("vector", lambda e, n=n4, h_=hh: e.tensor_copy(dbgffn[:, n * 512:(n + 1) * 512], PS[h_ * 4 + n][:, 0:512]).then_inc(sv.h))
                    sv.inc()
                dump(sch, nm, dbgffn)
                sch.wait("vector", sg, sg.n)
        for half in range(2):
            t8 = pair * 2 + half
            sch.wait("sync", sv, sv.n)
            sch.em("sync", lambda e, t=t8: e.dma_start(out=x2l, in_=x2_dram[t * 128:(t + 1) * 128, :]).then_inc(sl.h, 16))
            sch.wait("vector", sl, sl.inc(16))
            sch.wait("vector", sout, sout.n)
            for n4 in range(4):
                sch.em("vector", lambda e, hh=half, n=n4: e.tensor_tensor(
                    ot[:, n * 512:(n + 1) * 512], PS[hh * 4 + n][:, 0:512], gb_b[:, n * 512:(n + 1) * 512], op=ALU.mult))
                sch.em("vector", lambda e, n=n4: e.tensor_tensor(
                    ot[:, n * 512:(n + 1) * 512], ot[:, n * 512:(n + 1) * 512], x2l[:, n * 512:(n + 1) * 512], op=ALU.add))
            sch.em("vector", lambda e: e.tensor_scalar(st2[:, 8:9] if False else ot[:, 0:1], ot[:, 0:1], 1.0, None, op0=ALU.mult).then_inc(sv.h))
            vw = sv.inc()
            sch.wait("gpsimd", sv, vw)
            sch.em("gpsimd", lambda e, t=t8: e.dma_start(out=out[t * 128:(t + 1) * 128, :], in_=ot).then_inc(sout.h, 16))
            sout.inc(16)
            sch.wait("vector", sout, sout.n)
            sch.wait("tensor", sv, vw)
    sch.wait("gpsimd", sout, sout.n)
    AF32.pop(); ABF.pop()
    sch.run_block(nc)

    return nc, dbg_handles


def _host_tables(c):
    bA, bB = c, 15 - c
    pos = np.concatenate([np.arange(bA * S, (bA + 1) * S), np.arange(bB * S, (bB + 1) * S)])
    j = np.arange(64)
    theta = 10000.0 ** (-(j.astype(np.float64)) * 2.0 / 128.0)
    ang = pos[:, None].astype(np.float64) * theta[None, :]
    cosq = np.cos(ang).astype(np.float32)
    sinq = np.sin(ang).astype(np.float32)

    def slab(g):
        return 2 * g if g < 8 else 2 * (15 - g) + 1
    ua = [slab(g) for g in range(bA, -1, -1)] + [16] * (NU_A - (bA + 1))
    ub = [slab(g) for g in range(bB, -1, -1)] + [16] * (NU_B - (bB + 1))
    sl_ = np.array(ua + ub + [16] * (64 - NU_A - NU_B), np.int64)
    utab = (sl_ * SLABE).astype(np.int32).reshape(1, 64)
    return cosq, sinq, utab


def _prep_inputs(inputs, mode="ag"):
    x = np.ascontiguousarray(np.asarray(inputs["L0_attn_x"]).reshape(8192, D))
    wqT = np.ascontiguousarray(np.asarray(inputs["L0_attn_Wq"]).T)
    wkT = np.ascontiguousarray(np.asarray(inputs["L0_attn_Wk"]).T)
    wvT = np.ascontiguousarray(np.asarray(inputs["L0_attn_Wv"]).T)
    woT = np.ascontiguousarray(np.asarray(inputs["L0_attn_Wo"]).T)
    wg = np.asarray(inputs["L0_ffn_w_gate"])
    wu = np.asarray(inputs["L0_ffn_w_up"])
    wd = np.asarray(inputs["L0_ffn_w_down"])
    cv = np.asarray(inputs["L0_c"]).reshape(D)
    cvT = np.ascontiguousarray(cv.reshape(DK, 128).T)
    wmod = np.asarray(inputs["L0_wmod"])
    in_maps = []
    for c in range(8):
        bA, bB = c, 15 - c
        cosq, sinq, utab = _host_tables(c)
        m = {
            "xa": np.ascontiguousarray(x[bA * S:(bA + 1) * S]),
            "xb": np.ascontiguousarray(x[bB * S:(bB + 1) * S]),
            "wqT": wqT, "wkT": wkT, "wvT": wvT, "woT": woT,
            "cvT": cvT, "cosq": cosq, "sinq": sinq, "utab": utab,
        }
        if mode == "local":
            m["wmod"] = wmod
            m["wg_sl"] = wg
            m["wu_sl"] = wu
            m["wd_sl"] = wd
        else:
            m["wmod"] = np.ascontiguousarray(wmod[:, c * 1536:(c + 1) * 1536])
            m["wg_sl"] = np.ascontiguousarray(wg[c * 1024:(c + 1) * 1024])
            m["wu_sl"] = np.ascontiguousarray(wu[c * 1024:(c + 1) * 1024])
            m["wd_sl"] = np.ascontiguousarray(wd[:, c * 1024:(c + 1) * 1024])
        in_maps.append(m)
    return in_maps


_MODE = "ag"
_ROUND = "magic"


def kernel(**inputs):
    in_maps = _prep_inputs(inputs, _MODE)
    nc, _ = build_program(_MODE, _ROUND, debug=())
    res = run_bass_kernel_spmd(nc, in_maps, core_ids=list(range(8)))
    outf = np.zeros((8192, D), np.float32)
    for c in range(8):
        o = np.asarray(res.results[c]["out"])
        outf[c * S:(c + 1) * S] = o[0:S]
        outf[(15 - c) * S:(16 - c) * S] = o[S:2 * S]
    return outf.reshape(1, 8192, D)
